# version 10 (final)
# baseline (speedup 1.0000x reference)
import sys
import math
import numpy as np
from contextlib import ExitStack

if "/opt/trn_rl_repo" not in sys.path:
    sys.path.insert(0, "/opt/trn_rl_repo")

import ml_dtypes
import concourse.bass as bass
import concourse.mybir as mybir
from concourse.bass_utils import run_bass_kernel_spmd

F32 = mybir.dt.float32
BF16 = mybir.dt.bfloat16
AF = mybir.ActivationFunctionType
ALU = mybir.AluOpType
AX = mybir.AxisListType.X

D = 1024
S = 4096
NB = 8
NCH = 8
CH = 512
NH = 16
HD = 64
DL = 256
LAM = 0.01
E2 = 2048
ZS = 128
F_IN = 4224
LN_EPS = 1e-5
S2 = S // 2
NKT = 32
GK = 3
NG = (NKT + GK - 1) // GK

ENGS = ("tensor", "vector", "scalar", "gpsimd", "sync")


class Sem:
    __slots__ = ("h", "n")

    def __init__(self, h):
        self.h = h
        self.n = 0

    def inc(self, k=1):
        self.n += k
        return self.n


class MK:
    def __init__(self, nc, ctx):
        self.nc = nc
        self.ctx = ctx
        self.q = {e: [] for e in ENGS}
        self._sems = {}
        self._lastwait = {}
        self.ch = {e: self.sem("ch_" + e) for e in ("tensor", "vector", "scalar", "gpsimd")}

    def sem(self, name):
        if name not in self._sems:
            self._sems[name] = Sem(self.ctx.enter_context(self.nc.semaphore(name)))
        return self._sems[name]

    def em(self, eng, fn):
        self.q[eng].append(fn)

    def wait(self, eng, sem, v):
        if v <= 0:
            return
        key = (eng, id(sem))
        if self._lastwait.get(key, -1) >= v:
            return
        self._lastwait[key] = v
        h = sem.h
        self.em(eng, lambda e: e.wait_ge(h, v))

    def mm(self, out, lhsT, rhs, start=True, stop=True, inc=False, tp=None):
        tch = self.ch["tensor"]
        cnt = tch.inc() if inc else None
        h = tch.h
        def f(e, out=out, lhsT=lhsT, rhs=rhs, start=start, stop=stop, tp=tp, cnt=cnt):
            i = e.matmul(out, lhsT, rhs, start=start, stop=stop, tile_position=tp)
            if cnt is not None:
                i.then_inc(h, 1)
        self.em("tensor", f)
        return cnt

    def trans(self, out, in_, ident, inc=True):
        tch = self.ch["tensor"]
        cnt = tch.inc() if inc else None
        h = tch.h
        def f(e, out=out, in_=in_, ident=ident, cnt=cnt):
            i = e.transpose(out, in_, ident)
            if cnt is not None:
                i.then_inc(h, 1)
        self.em("tensor", f)
        return cnt

    def vop(self, eng, name, *args, **kw):
        ch = self.ch[eng]
        prev = ch.n
        cur = ch.inc()
        h = ch.h
        def f(e, name=name, args=args, kw=kw, prev=prev):
            if prev > 0:
                e.wait_ge(h, prev)
            getattr(e, name)(*args, **kw).then_inc(h, 1)
        self.em(eng, f)
        return cur

    def dma(self, eng, dst, src, sem, slow_ok=False):
        tot = sem.inc(16)
        h = sem.h
        def f(e, dst=dst, src=src):
            e.dma_start(dst, src,
                        allow_slow_non_contiguous=slow_ok).then_inc(h, 16)
        self.em(eng, f)
        return tot

    def fence(self, name, engs=ENGS, dma_sems=()):
        counts = [(self.ch[e2], self.ch[e2].n)
                  for e2 in ("tensor", "vector", "scalar", "gpsimd")]
        counts += [(s2, s2.n) for s2 in dma_sems]
        for e in engs:
            for s2, v in counts:
                self.wait(e, s2, v)

    def run(self):
        with self.nc.Block() as block:
            for eng in ENGS:
                ops = self.q[eng]
                getattr(block, eng)(lambda e, ops=ops: [f(e) for f in ops])


def make_consts(nc):
    c = {}
    j = np.arange(CH, dtype=np.float64)
    c["esc_q1"] = np.tile((np.exp(-LAM * j) / 32.0)[None, :], (128, 1)).astype(np.float32)
    c["esc_q2"] = np.tile((np.exp(+LAM * j) / 32.0)[None, :], (128, 1)).astype(np.float32)
    c["esc_k1"] = np.tile(np.exp(+LAM * j)[None, :], (128, 1)).astype(np.float32)
    c["esc_k2"] = np.tile(np.exp(-LAM * j)[None, :], (128, 1)).astype(np.float32)
    kk = np.arange(128, dtype=np.float64)[:, None]
    qq = np.arange(512, dtype=np.float64)[None, :]
    def c1(b, w):
        kg = kk + b
        return np.where(qq[:, :w] < kg, np.exp(-2 * LAM * (kg - qq[:, :w])), 1.0).astype(np.float32)
    def c2(b, w0):
        kg = kk + b
        qs = qq[:, w0:]
        return np.where(qs > kg, np.exp(-2 * LAM * (qs - kg)), 1.0).astype(np.float32)
    c["cor0"] = c1(0, 128)
    c["cor1"] = c1(128, 256)
    c["cor2"] = c2(256, 256)
    c["cor3"] = c2(384, 384)
    c["ones_col"] = np.ones((128, 1), dtype=np.float32)
    cv = np.zeros((128, 4), dtype=np.float32)
    cv[:, 0] = LN_EPS
    cv[:, 1] = 1.0
    c["cvals"] = cv
    c["ones_row"] = np.ones((1, 128), dtype=np.float32)
    c["ident"] = np.eye(128, dtype=ml_dtypes.bfloat16)
    return {k: nc.inline_tensor(np.ascontiguousarray(v), name="c_" + k)
            for k, v in c.items()}


INPUT_SPECS = [
    ("x_t", (D, S)),
    ("W_q", (D, D)), ("W_k", (D, D)), ("W_v", (D, D)), ("W_o", (D, D)),
    ("W_down", (D, DL)), ("W_up_k", (DL, D)), ("W_up_v", (DL, D)),
    ("b_ln_g", (D, 1)), ("b_ln_b", (D, 1)),
    ("W_in", (D, F_IN)), ("b_in", (1, F_IN)),
    ("W_out", (E2, D)), ("b_out", (D, 1)),
    ("gamma_q", (ZS, 1)), ("beta_q", (ZS, 1)),
    ("gamma_k", (ZS, 1)), ("beta_k", (ZS, 1)),
    ("cn_w1", (D, D)), ("cn_b1", (D, 1)),
    ("cn_w2", (D, D // 2)), ("cn_b2", (D // 2, 1)),
    ("cn_w3", (D // 2, D // 4)), ("cn_b3", (D // 4, 1)),
    ("cn_w4", (D // 4, 1)), ("cn_b4", (1, 1)),
    ("gc_w1", (D, D // 2)), ("gc_b1", (D // 2, 1)),
    ("gc_w2", (D // 2, D // 4)), ("gc_b2", (D // 4, 1)),
    ("gc_w3", (D // 4, 1)), ("gc_b3", (1, 1)),
    ("a_range", (1, 1)),
]


class Builder:
    def __init__(self, taps=(), upto="p8", nch=NCH, p0sub=99, np2=NH // 2):
        self.taps = set(taps)
        self.upto = upto
        self.nch = nch
        self.p0sub = p0sub
        self.np2 = np2
        nc = bass.Bass("TRN2", target_bir_lowering=False)
        self.nc = nc
        self.ctx = ExitStack()
        nc.ctx.enter_context(self.ctx)
        self.T = {}
        for name, shape in INPUT_SPECS:
            self.T[name] = nc.dram_tensor(name, list(shape), F32, kind="ExternalInput")
        self.y = nc.dram_tensor("y_t", [D, S], F32, kind="ExternalOutput")
        self.consts = make_consts(nc)
        self.mk = MK(nc, self.ctx)
        self.st = {}
        self.phase_stacks = []
        Builder._name_n = {}

    def scratch(self, name, shape, dt=BF16):
        kind = "ExternalOutput" if name in self.taps else "Internal"
        self.T[name] = self.nc.dram_tensor(name, list(shape), dt, kind=kind)
        return self.T[name]

    _name_n = {}

    def sb(self, name, shape, dt, stack=None):
        n = Builder._name_n.get(name, 0)
        Builder._name_n[name] = n + 1
        alloc_name = name if n == 0 else f"{name}__{n}"
        t = (stack or self.ctx).enter_context(
            self.nc.sbuf_tensor(alloc_name, list(shape), dt))
        self.st[name] = t
        return t

    def psum(self, name, shape, dt=F32, stack=None):
        n = Builder._name_n.get(name, 0)
        Builder._name_n[name] = n + 1
        alloc_name = name if n == 0 else f"{name}__{n}"
        t = (stack or self.ctx).enter_context(
            self.nc.psum_tensor(alloc_name, list(shape), dt))
        self.st[name] = t
        return t

    def build(self):
        mk, st = self.mk, self.st
        self.scratch("qt1", (D, S))
        self.scratch("qt2", (D, S))
        self.scratch("kt1", (D, S))
        self.scratch("kt2", (D, S))
        self.scratch("vat", (S, NH * 65))
        self.scratch("Ut", (D, S))
        self.scratch("xres", (D, S), F32)
        self.scratch("ut2", (E2, S))
        self.scratch("v2t", (S, E2))

        for nm, sh in (("cor0", (128, 128)), ("cor1", (128, 256)),
                       ("cor2", (128, 256)), ("cor3", (128, 128)),
                       ("ones_col", (128, 1)), ("ones_row", (1, 128)),
                       ("cvals", (128, 4))):
            self.sb(nm, sh, F32)
        self.sb("identb", (128, 128), BF16)
        self.sb("onesb", (128, 1), BF16)
        self.sb("negc", (128, 1), F32)
        self.sb("qabs", (128, 1), F32)
        self.sb("kabs", (128, 1), F32)
        self.sb("rows", (1, 8 * CH), F32)
        self.sb("colsc", (128, 8), F32)
        self.sb("gcs", (128, 8), F32)
        self.sA = ExitStack()
        for nm in ("esc_q1", "esc_q2", "esc_k1", "esc_k2"):
            self.sb(nm, (128, CH), F32, self.sA)
        self.sb("stage", (128, 8448), F32, self.sA)

        self.p0()
        if self.upto == "p0":
            self.mk.run()
            return self.nc
        self.p1()
        if self.upto == "p1":
            self.mk.run()
            return self.nc
        self.p2()
        if self.upto == "p2":
            self.mk.run()
            return self.nc
        self.p3()
        if self.upto == "p3":
            self.mk.run()
            return self.nc
        lst = (self.mk.sem("p3ls0"), self.mk.sem("p3ls1"))
        self.gfnet()
        if self.upto == "gf1":
            self.mk.run()
            return self.nc
        self.ffn_half(self.st["ores"])
        if self.upto == "ffn1a":
            self.mk.run()
            return self.nc
        self.residual_half(self.st["ores"], 0, "xres")
        if self.upto == "res0":
            self.mk.run()
            return self.nc
        self.load_lnt(self.st["ores"], lst)
        self.ffn_half(self.st["ores"])
        self.residual_half(self.st["ores"], 1, "xres")
        self.sF.close()
        if self.upto == "res1":
            self.mk.run()
            return self.nc
        self.p5()
        if self.upto in ("p5", "p5a"):
            self.mk.run()
            return self.nc
        self.p6()
        if self.upto == "p6":
            self.mk.run()
            return self.nc
        self.p7pre()
        lst2 = (self.mk.sem("p7lnst"), self.mk.sem("p7lnst2"))
        self.gfnet()
        self.ffn_half(self.st["ores2"])
        self.residual_half(self.st["ores2"], 0, "y")
        self.load_lnt(self.st["ores2"], lst2)
        self.ffn_half(self.st["ores2"])
        self.residual_half(self.st["ores2"], 1, "y")
        self.sF2.close()
        self.sG.close()
        self.p8()
        self.mk.run()
        return self.nc

    _lc_n = 0

    def load_cast(self, dst, src, rows, cols, ld=None, queue="sync",
                  stage="stage"):
        mk, st = self.mk, self.st
        Builder._lc_n += 1
        s = mk.sem(f"lc{Builder._lc_n}")
        nblk = max(1, rows // 128)
        mk.wait(queue, mk.ch["vector"], mk.ch["vector"].n)
        if rows >= 128:
            tot = mk.dma(queue, st[stage][:, 0:nblk * cols].rearrange(
                "p (b c) -> p b c", b=nblk),
                src[:, :].rearrange("(b p) c -> p b c", p=128), s,
                slow_ok=(cols <= 4))
        else:
            tot = mk.dma(queue, st[stage][0:rows, 0:cols], src[:, :], s)
        mk.wait("vector", s, tot)
        if rows >= 128:
            return mk.vop("vector", "tensor_copy", dst[:, 0:nblk * cols],
                          st[stage][:, 0:nblk * cols])
        return mk.vop("vector", "tensor_copy", dst[0:rows, 0:cols],
                      st[stage][0:rows, 0:cols])

    def p0(self):
        mk, st, T, C = self.mk, self.st, self.T, self.consts
        self.sb("Wq", (128, NB * D), BF16, self.sA)
        self.sb("Wld", (128, NB * DL), BF16, self.sA)
        self.sb("Wuk", (128, 2 * D), BF16, self.sA)
        self.sb("Wuv", (128, 2 * D), BF16, self.sA)
        ld = mk.sem("p0ld")

        for nm in ("esc_q1", "esc_q2", "esc_k1", "esc_k2",
                   "cor0", "cor1", "cor2", "cor3", "ones_col", "ones_row",
                   "cvals"):
            mk.dma("sync", st[nm][:, :], C[nm][:, :], ld)
        mk.dma("sync", st["identb"][:, :], C["ident"][:, :], ld)
        cld = ld.n
        mk.wait("vector", ld, cld)
        mk.vop("vector", "tensor_copy", st["onesb"][:, :], st["ones_col"][:, :])
        mk.vop("vector", "memset", st["qabs"][:, :], 0.0)
        mk.vop("vector", "memset", st["kabs"][:, :], 0.0)

        if self.p0sub < 2:
            return
        self.load_cast(st["Wq"], T["W_q"], D, D)
        self.load_cast(st["Wuk"], T["W_up_k"], DL, D)
        self.load_cast(st["Wuv"], T["W_up_v"], DL, D)
        if self.p0sub < 3:
            return

        with ExitStack() as p0s:
            self.sb("stage2", (128, 8192), F32, p0s)
            self.sb("Wkv", (128, NB * D), BF16, p0s)
            self.sb("WkvT", (128, NB * D), BF16, p0s)
            self.sb("Wdn", (128, NB * DL), BF16, p0s)
            ptrs = [self.psum(f"ptr{i}", (128, 512), F32, p0s) for i in range(4)]
            pw0 = self.psum("pw0", (128, 512), F32, p0s)
            pw1 = self.psum("pw1", (128, 512), F32, p0s)

            swk = mk.sem("p0wk")
            swv = mk.sem("p0wv")
            t1 = mk.dma("sync", st["stage2"][:, :].rearrange("p (b c) -> p b c", b=NB),
                        T["W_k"][:, :].rearrange("(b p) c -> p b c", p=128), swk)
            vc = self.load_cast(st["Wdn"], T["W_down"], D, DL, queue="scalar")
            mk.wait("sync", mk.ch["vector"], vc)
            t2 = mk.dma("sync", st["stage"][:, 0:NB * D].rearrange(
                "p (b c) -> p b c", b=NB),
                T["W_v"][:, :].rearrange("(b p) c -> p b c", p=128), swv)
            mk.wait("vector", swk, t1)
            mk.wait("vector", swv, t2)
            if self.p0sub < 4:
                return
            vkv = mk.vop("vector", "tensor_add", st["Wkv"][:, :],
                         st["stage"][:, 0:NB * D], st["stage2"][:, :])

            mk.wait("tensor", mk.ch["vector"], vkv)
            mk.wait("tensor", ld, ld.n)
            evs = []
            for j in range(NB * NB):
                di, dm = divmod(j, NB)
                pslice = ptrs[j % 4][:, :].bitcast(BF16)[:, 0:128]
                if j >= 4:
                    mk.wait("tensor", mk.ch["scalar"], evs[j - 4])
                tcnt = mk.trans(pslice,
                                st["Wkv"][:, di * D + dm * 128: di * D + (dm + 1) * 128],
                                st["identb"][:, :])
                mk.wait("scalar", mk.ch["tensor"], tcnt)
                evs.append(mk.vop("scalar", "copy",
                                  st["WkvT"][:, dm * D + di * 128: dm * D + (di + 1) * 128],
                                  pslice))
            if self.p0sub < 5:
                return
            mk.wait("tensor", mk.ch["scalar"], evs[-1])
            wevs = []
            for j in range(NB * 2):
                di, lh = divmod(j, 2)
                p = (pw0, pw1)[j % 2]
                if j >= 2:
                    mk.wait("tensor", mk.ch["scalar"], wevs[j - 2])
                for dm in range(NB):
                    tcnt = mk.mm(p[:, 0:128],
                                 st["WkvT"][:, dm * D + di * 128: dm * D + (di + 1) * 128],
                                 st["Wdn"][:, dm * DL + lh * 128: dm * DL + (lh + 1) * 128],
                                 start=(dm == 0), stop=(dm == NB - 1),
                                 inc=(dm == NB - 1))
                mk.wait("scalar", mk.ch["tensor"], tcnt)
                wevs.append(mk.vop("scalar", "copy",
                                   st["Wld"][:, di * DL + lh * 128: di * DL + (lh + 1) * 128],
                                   p[:, 0:128]))
            if self.p0sub < 6:
                return
            mk.fence("p0")

    def p1(self):
        mk, st, T = self.mk, self.st, self.T
        s = self.sA
        xc = [self.sb("xc0", (128, NB * CH), F32, s),
              self.sb("xc1", (128, NB * CH), F32, s)]
        self.sb("xsq", (128, NB * CH), BF16, s)
        self.sb("xn", (128, NB * CH), BF16, s)
        self.sb("latc", (128, 2 * CH), BF16, s)
        self.sb("qo1", (128, NB * CH), BF16, s)
        self.sb("qo2", (128, NB * CH), BF16, s)
        self.sb("ko1", (128, NB * CH), BF16, s)
        self.sb("ko2", (128, NB * CH), BF16, s)
        self.sb("vo", (128, 4 * NH * 65), BF16, s)
        pa = self.psum("pa", (128, 512), F32, s)
        pb = self.psum("pb", (128, 512), F32, s)
        pc = self.psum("pc", (128, 512), F32, s)
        pd = self.psum("pd", (128, 512), F32, s)
        pe_ = self.psum("pe", (128, 512), F32, s)
        pf = self.psum("pf", (128, 512), F32, s)
        pg = self.psum("pg", (128, 512), F32, s)
        ph = self.psum("ph", (128, 512), F32, s)

        xld = [mk.sem("p1x0"), mk.sem("p1x1")]
        ost = mk.sem("p1st")
        MU, VAR, RSTD, MU2 = 0, 1, 2, 3

        def row(i):
            return st["rows"][0:1, i * CH:(i + 1) * CH]

        mk.dma("sync", xc[0][:, :].rearrange("p (b c) -> p b c", b=NB),
               T["x_t"][:, 0:CH].rearrange("(b p) c -> p b c", p=128), xld[0])

        xfree = [0, 0]
        prev_ev = {}
        last_store = 0
        for c in range(self.nch):
            buf = xc[c % 2]
            sc_q1 = float(math.exp(-LAM * CH * c))
            sc_q2 = float(math.exp(+LAM * CH * c))
            sc_k1 = float(math.exp(+LAM * CH * c))
            sc_k2 = float(math.exp(-LAM * CH * c))
            if c + 1 < self.nch:
                mk.wait("sync", mk.ch["vector"], xfree[(c + 1) % 2])
                mk.dma("sync", xc[(c + 1) % 2][:, :].rearrange("p (b c) -> p b c", b=NB),
                       T["x_t"][:, (c + 1) * CH:(c + 2) * CH].rearrange(
                           "(b p) c -> p b c", p=128), xld[(c + 1) % 2])

            mk.wait("vector", xld[c % 2], 16 * (c // 2 + 1))
            mk.wait("scalar", xld[c % 2], 16 * (c // 2 + 1))
            mk.wait("scalar", mk.ch["tensor"], prev_ev.get("xsq_rd", 0))
            s_sq = mk.vop("scalar", "activation", st["xsq"][:, :], buf[:, :],
                          AF.Square)
            mk.wait("tensor", mk.ch["vector"], prev_ev.get("rows_done", 0))
            mk.wait("tensor", xld[c % 2], 16 * (c // 2 + 1))
            for b in range(NB):
                mm_s = mk.mm(pa[0:1, :], st["ones_col"][:, :],
                             buf[:, b * CH:(b + 1) * CH], start=(b == 0),
                             stop=(b == NB - 1), inc=(b == NB - 1))
            mk.wait("tensor", mk.ch["scalar"], s_sq)
            for b in range(NB):
                mm_q = mk.mm(pb[0:1, :], st["onesb"][:, :],
                             st["xsq"][:, b * CH:(b + 1) * CH], start=(b == 0),
                             stop=(b == NB - 1), inc=(b == NB - 1))
            prev_ev["xsq_rd"] = mm_q
            mk.wait("vector", mk.ch["tensor"], mm_s)
            v_mu = mk.vop("vector", "tensor_scalar_mul", row(MU), pa[0:1, :], 1.0 / D)
            mk.wait("vector", mk.ch["tensor"], mm_q)
            mk.vop("vector", "tensor_scalar_mul", row(VAR), pb[0:1, :], 1.0 / D)
            mk.vop("vector", "tensor_mul", row(MU2), row(MU), row(MU))
            v_var = mk.vop("vector", "tensor_sub", row(VAR), row(VAR), row(MU2))
            prev_ev["rows_done"] = v_var
            mk.wait("scalar", mk.ch["vector"], v_var)
            mk.vop("scalar", "activation", row(RSTD), row(VAR), AF.Ln,
                   bias=st["cvals"][0:1, 0:1])
            s_rstd = mk.vop("scalar", "activation", row(RSTD), row(RSTD), AF.Exp,
                            scale=-0.5)
            mk.wait("tensor", mk.ch["vector"], prev_ev.get("apply_done", 0))
            mk.wait("tensor", mk.ch["vector"], v_mu)
            mm_mu = mk.mm(pc[:, :], st["ones_row"][:, :], row(MU), inc=True)
            mk.wait("tensor", mk.ch["scalar"], s_rstd)
            mm_rs = mk.mm(pd[:, :], st["ones_row"][:, :], row(RSTD), inc=True)
            mk.wait("vector", mk.ch["tensor"], mm_rs)
            for b in range(NB):
                mk.vop("vector", "tensor_sub", buf[:, b * CH:(b + 1) * CH],
                       buf[:, b * CH:(b + 1) * CH], pc[:, :])
            for b in range(NB):
                v_xn = mk.vop("vector", "tensor_mul", st["xn"][:, b * CH:(b + 1) * CH],
                              buf[:, b * CH:(b + 1) * CH], pd[:, :])
            xfree[c % 2] = v_xn
            prev_ev["apply_done"] = v_xn

            mk.wait("vector", ost, last_store)
            mk.wait("tensor", mk.ch["vector"], v_xn)

            pool = (pe_, pf, pg, ph)

            def nb_():
                i = prev_ev.get("pool_i", 0)
                prev_ev["pool_i"] = i + 1
                bank = pool[i % 4]
                mk.wait("tensor", mk.ch["vector"],
                        prev_ev.get(f"bk{i % 4}", 0))
                return bank, i % 4

            for m in range(NB):
                qb_, bki = nb_()
                for b in range(NB):
                    mmq = mk.mm(qb_[:, :],
                                st["Wq"][:, b * D + m * 128: b * D + (m + 1) * 128],
                                st["xn"][:, b * CH:(b + 1) * CH],
                                start=(b == 0), stop=(b == NB - 1), inc=(b == NB - 1))
                mk.wait("vector", mk.ch["tensor"], mmq)
                mk.vop("vector", "reduce_max", st["colsc"][:, 1:2], qb_[:, :], AX,
                       apply_absolute_value=True)
                mk.vop("vector", "tensor_max", st["qabs"][:, :], st["qabs"][:, :],
                       st["colsc"][:, 1:2])
                mk.vop("vector", "scalar_tensor_tensor",
                       st["qo1"][:, m * CH:(m + 1) * CH], qb_[:, :], sc_q1,
                       st["esc_q1"][:, :], ALU.mult, ALU.mult)
                prev_ev[f"bk{bki}"] = mk.vop(
                    "vector", "scalar_tensor_tensor",
                    st["qo2"][:, m * CH:(m + 1) * CH], qb_[:, :],
                    sc_q2, st["esc_q2"][:, :], ALU.mult, ALU.mult)
            prev_ev["qev"] = prev_ev[f"bk{bki}"]
            mk.wait("gpsimd", mk.ch["vector"], prev_ev["qev"])
            mk.dma("gpsimd", T["qt1"][:, :].rearrange("(m p) s -> p m s", p=128)[
                :, :, c * CH:(c + 1) * CH],
                st["qo1"][:, :].rearrange("p (m s) -> p m s", m=NB), ost)
            mk.dma("gpsimd", T["qt2"][:, :].rearrange("(m p) s -> p m s", p=128)[
                :, :, c * CH:(c + 1) * CH],
                st["qo2"][:, :].rearrange("p (m s) -> p m s", m=NB), ost)

            for lh in range(2):
                lb_, bki = nb_()
                for b in range(NB):
                    mml = mk.mm(lb_[:, :],
                                st["Wld"][:, b * DL + lh * 128: b * DL + (lh + 1) * 128],
                                st["xn"][:, b * CH:(b + 1) * CH],
                                start=(b == 0), stop=(b == NB - 1), inc=(b == NB - 1))
                mk.wait("vector", mk.ch["tensor"], mml)
                prev_ev[f"bk{bki}"] = mk.vop(
                    "vector", "tensor_copy",
                    st["latc"][:, lh * CH:(lh + 1) * CH], lb_[:, :])
            prev_ev["lev"] = prev_ev[f"bk{bki}"]

            mk.wait("tensor", mk.ch["vector"], prev_ev["lev"])
            for m in range(NB):
                kb_, bki = nb_()
                for lh in range(2):
                    mmk = mk.mm(kb_[:, :],
                                st["Wuk"][:, lh * D + m * 128: lh * D + (m + 1) * 128],
                                st["latc"][:, lh * CH:(lh + 1) * CH],
                                start=(lh == 0), stop=(lh == 1), inc=(lh == 1))
                mk.wait("vector", mk.ch["tensor"], mmk)
                mk.vop("vector", "reduce_max", st["colsc"][:, 2:3], kb_[:, :], AX,
                       apply_absolute_value=True)
                mk.vop("vector", "tensor_max", st["kabs"][:, :], st["kabs"][:, :],
                       st["colsc"][:, 2:3])
                mk.vop("vector", "scalar_tensor_tensor",
                       st["ko1"][:, m * CH:(m + 1) * CH], kb_[:, :], sc_k1,
                       st["esc_k1"][:, :], ALU.mult, ALU.mult)
                prev_ev[f"bk{bki}"] = mk.vop(
                    "vector", "scalar_tensor_tensor",
                    st["ko2"][:, m * CH:(m + 1) * CH], kb_[:, :],
                    sc_k2, st["esc_k2"][:, :], ALU.mult, ALU.mult)
            prev_ev["kev"] = prev_ev[f"bk{bki}"]
            mk.wait("gpsimd", mk.ch["vector"], prev_ev["kev"])
            mk.dma("gpsimd", T["kt1"][:, :].rearrange("(m p) s -> p m s", p=128)[
                :, :, c * CH:(c + 1) * CH],
                st["ko1"][:, :].rearrange("p (m s) -> p m s", m=NB), ost)
            mk.dma("gpsimd", T["kt2"][:, :].rearrange("(m p) s -> p m s", p=128)[
                :, :, c * CH:(c + 1) * CH],
                st["ko2"][:, :].rearrange("p (m s) -> p m s", m=NB), ost)

            v_ms = mk.vop(
                "vector", "memset",
                st["vo"][:, :].rearrange("p (k c) -> p k c", c=65)[:, :, 64:65],
                1.0)
            for sT in range(4):
                for nh2 in range(2):
                    vb_, bki = nb_()
                    for lh in range(2):
                        mmv = mk.mm(vb_[:, :],
                                    st["latc"][:, lh * CH + sT * 128: lh * CH + (sT + 1) * 128],
                                    st["Wuv"][:, lh * D + nh2 * 512: lh * D + (nh2 + 1) * 512],
                                    start=(lh == 0), stop=(lh == 1), inc=(lh == 1))
                    mk.wait("vector", mk.ch["tensor"], mmv)
                    base = sT * NH * 65 + nh2 * 8 * 65
                    cnt = mk.vop(
                        "vector", "tensor_copy",
                        st["vo"][:, base: base + 8 * 65].rearrange(
                            "p (h c) -> p h c", c=65)[:, :, 0:64],
                        vb_[:, :].rearrange("p (h c) -> p h c", c=64))
                    prev_ev[f"bk{bki}"] = cnt
            prev_ev["vev"] = cnt
            mk.wait("gpsimd", mk.ch["vector"], prev_ev["vev"])
            mk.dma("gpsimd",
                   T["vat"][c * CH:(c + 1) * CH, :].rearrange("(t p) c -> p t c", p=128),
                   st["vo"][:, :].rearrange("p (t c) -> p t c", t=4), ost)
            last_store = ost.n

        self.sb("absb", (128, 2), BF16, s)
        v_ab = mk.vop("vector", "tensor_copy", st["absb"][:, 0:1], st["qabs"][:, :])
        v_ab = mk.vop("vector", "tensor_copy", st["absb"][:, 1:2], st["kabs"][:, :])
        mk.wait("tensor", mk.ch["vector"], v_ab)
        pabf = pa[:, :].bitcast(BF16)
        t_q = mk.trans(pabf[0:1, 0:128], st["absb"][:, 0:1], st["identb"][:, :])
        t_k = mk.trans(pabf[0:1, 128:256], st["absb"][:, 1:2], st["identb"][:, :])
        mk.wait("vector", mk.ch["tensor"], t_k)
        mk.vop("vector", "reduce_max", st["colsc"][0:1, 4:5], pabf[0:1, 0:128], AX)
        mk.vop("vector", "reduce_max", st["colsc"][0:1, 5:6], pabf[0:1, 128:256], AX)
        mk.vop("vector", "tensor_mul", st["colsc"][0:1, 6:7],
               st["colsc"][0:1, 4:5], st["colsc"][0:1, 5:6])
        v_c = mk.vop("vector", "tensor_scalar_mul", st["colsc"][0:1, 6:7],
                     st["colsc"][0:1, 6:7], -2.03)
        mk.wait("tensor", mk.ch["vector"], v_c)
        mm_c = mk.mm(pb[:, 0:1], st["ones_row"][:, :], st["colsc"][0:1, 6:7], inc=True)
        mk.wait("vector", mk.ch["tensor"], mm_c)
        v_nc = mk.vop("vector", "tensor_copy", st["negc"][:, :], pb[:, 0:1])
        if "tap_c" in self.taps:
            tc = self.nc.dram_tensor("tap_c", [128, 1], F32, kind="ExternalOutput")
            mk.wait("gpsimd", mk.ch["vector"], v_nc)
            mk.dma("gpsimd", tc[:, :], st["negc"][:, :], ost)
        mk.fence("p1end")
        self.sA.close()

    def p2(self):
        mk, st, T = self.mk, self.st, self.T
        self.p2s = ExitStack()
        s = self.p2s
        ktb1 = [self.sb(f"ktb1_{i}", (128, S), BF16, s) for i in range(2)]
        ktb2 = [self.sb(f"ktb2_{i}", (128, S), BF16, s) for i in range(2)]
        qtb1 = [self.sb(f"qtb1_{i}", (128, S), BF16, s) for i in range(2)]
        qtb2 = [self.sb(f"qtb2_{i}", (128, S), BF16, s) for i in range(2)]
        vab = [self.sb(f"vab_{i}", (128, NKT * 65), BF16, s) for i in range(2)]
        attn = [self.sb(f"attn_{i}", (128, NKT * 512), BF16, s) for i in range(2)]
        ustage = [self.sb(f"ust_{i}", (128, 512), BF16, s) for i in range(2)]
        ovraw = [self.sb(f"ovraw_{i}", (65, 512), F32, s) for i in range(2)]
        rrec = [self.sb(f"rrec_{i}", (1, 512), F32, s) for i in range(2)]
        rbc = self.sb("rbc", (64, 512), F32, s)
        sc = [self.psum("sc0", (128, GK * 512), F32, s),
              self.psum("sc1", (128, GK * 512), F32, s)]
        pv = self.psum("pv", (65, 512), F32, s)
        pbc = self.psum("pbc", (64, 512), F32, s)

        kvld = [mk.sem("p2kv0"), mk.sem("p2kv1")]
        ust = [mk.sem("p2ust0"), mk.sem("p2ust1")]
        ost = mk.sem("p1st")

        def load_head(hh):
            sem = kvld[hh % 2]
            rs = slice(hh * 64, (hh + 1) * 64)
            for buf, src_t in ((ktb1, "kt1"), (ktb2, "kt2"),
                               (qtb1, "qt1"), (qtb2, "qt2")):
                mk.dma("sync", buf[hh % 2][0:64, :], T[src_t][rs, :], sem)
                mk.dma("sync", buf[hh % 2][64:128, :], T[src_t][rs, :], sem)
            mk.dma("sync", vab[hh % 2][:, :].rearrange("p (t c) -> p t c", t=NKT),
                   T["vat"][:, hh * 65:(hh + 1) * 65].rearrange(
                       "(t p) c -> p t c", p=128), sem)
            return sem.n

        mk.wait("sync", ost, ost.n)
        kvtot = [0, 0]
        kvtot[0] = load_head(0)
        if self.np2 * 2 > 1:
            kvtot[1] = load_head(1)

        gidx = 0
        exp_cnt = {}
        pv_done = 0
        raw_done = 0
        ust_free = [0, 0]
        it = 0
        pair_tch_done = 0
        pend = None

        def emit_norm(p):
            pit, phh, pqc, pg_rec = p
            mk.wait("tensor", mk.ch["vector"], max(pg_rec, rbc_free[0]))
            mm_bc = mk.mm(pbc[:, :], st["ones_row"][0:1, 0:64],
                          rrec[pit % 2][:, :], inc=True)
            mk.wait("vector", mk.ch["tensor"], mm_bc)
            mk.wait("vector", ust[pit % 2], ust_free[pit % 2])
            rbc_free[0] = mk.vop("vector", "tensor_copy", rbc[:, :], pbc[:, :])
            nd = mk.vop("vector", "tensor_mul",
                        ustage[pit % 2][0:64, :],
                        ovraw[pit % 2][0:64, :], rbc[:, :])
            mk.wait("gpsimd", mk.ch["vector"], nd)
            mk.dma("gpsimd",
                   T["Ut"][phh * 64:(phh + 1) * 64, pqc * 512:(pqc + 1) * 512],
                   ustage[pit % 2][0:64, :], ust[pit % 2])
            ust_free[pit % 2] = ust[pit % 2].n
            nd_of[pit] = nd
            return nd

        its = [(hh * NCH + qc, hh, qc)
               for hh in range(self.np2 * 2) for qc in range(NCH)]
        PERHEAD = NCH
        pv_tch = {}
        grec_of = {}
        nd_of = {}
        rbc_free = [0]
        emitted_prefetch = set()

        def emit_scores(i, pv_prev=None, pv_tiles=None, pv_fin=None):
            nonlocal gidx
            pvpos = 6
            _, hh, qc = its[i]
            qcols = slice(qc * 512, (qc + 1) * 512)
            kb1, kb2 = ktb1[hh % 2], ktb2[hh % 2]
            qb1, qb2 = qtb1[hh % 2], qtb2[hh % 2]
            ab = attn[i % 2]
            for g in range(NG):
                buf = sc[gidx % 2]
                n = min(GK, NKT - g * GK)
                if gidx >= 2:
                    mk.wait("tensor", mk.ch["scalar"], exp_cnt[gidx - 2])
                for j in range(n):
                    t = g * GK + j
                    if t <= 4 * qc + 1:
                        kb, qb = kb1, qb1
                    else:
                        kb, qb = kb2, qb2
                    mk.mm(buf[0:64, j * 512:(j + 1) * 512],
                          kb[0:64, t * 128:t * 128 + 64],
                          qb[0:64, qcols],
                          start=True, stop=True, tp=(0, 0))
                    mm = mk.mm(buf[64:128, j * 512:(j + 1) * 512],
                               kb[64:128, t * 128 + 64:(t + 1) * 128],
                               qb[64:128, qcols],
                               start=True, stop=True, tp=(64, 64),
                               inc=(j == n - 1))
                has_corr = False
                for j in range(n):
                    t = g * GK + j
                    dj = t - 4 * qc
                    if 0 <= dj <= 3:
                        has_corr = True
                        mk.wait("vector", mk.ch["tensor"], mm)
                        pslab = buf[:, j * 512:(j + 1) * 512]
                        if dj == 0:
                            cl = mk.vop("vector", "tensor_mul",
                                        pslab[:, 0:128], pslab[:, 0:128],
                                        st["cor0"][:, :])
                        elif dj == 1:
                            cl = mk.vop("vector", "tensor_mul",
                                        pslab[:, 0:256], pslab[:, 0:256],
                                        st["cor1"][:, :])
                        elif dj == 2:
                            cl = mk.vop("vector", "tensor_mul",
                                        pslab[:, 256:512], pslab[:, 256:512],
                                        st["cor2"][:, :])
                        else:
                            cl = mk.vop("vector", "tensor_mul",
                                        pslab[:, 384:512], pslab[:, 384:512],
                                        st["cor3"][:, :])
                if g == 0 and i >= 2:
                    mk.wait("scalar", mk.ch["tensor"], pv_tch[i - 2])
                if has_corr:
                    mk.wait("scalar", mk.ch["vector"], cl)
                else:
                    mk.wait("scalar", mk.ch["tensor"], mm)
                exp_cnt[gidx] = mk.vop(
                    "scalar", "activation",
                    ab[:, g * GK * 512: g * GK * 512 + n * 512],
                    buf[:, 0:n * 512], AF.Exp, bias=st["negc"][:, 0:1])
                gidx += 1
                if pv_prev is not None:
                    ngpv = NG - 2
                    upto = min(NKT, 6 + ((g + 1) * (NKT - 6) + ngpv - 1) // ngpv)
                    if upto > pvpos:
                        pv_tiles(pv_prev, pvpos, upto)
                        pvpos = upto
                    if pvpos == NKT and pv_fin is not None:
                        pv_fin(pv_prev)
                        pv_fin = None
            if pv_prev is not None and pv_fin is not None:
                if pvpos < NKT:
                    pv_tiles(pv_prev, pvpos, NKT)
                pv_fin(pv_prev)

        def emit_pv_tiles(i, t0, t1):
            _, hh, qc = its[i]
            vb = vab[hh % 2]
            ab = attn[i % 2]
            if t0 == 0:
                mk.wait("tensor", mk.ch["vector"], raw_done)
            for t in range(t0, t1):
                g = t // GK
                mk.wait("tensor", mk.ch["scalar"], exp_cnt[i * NG + g])
                mmpv = mk.mm(pv[:, :],
                             vb[:, t * 65:(t + 1) * 65],
                             ab[:, t * 512:(t + 1) * 512],
                             start=(t == 0), stop=(t == NKT - 1),
                             inc=(t == NKT - 1))
            if t1 == NKT:
                pv_tch[i] = mmpv

        def emit_pv_finish(i):
            nonlocal raw_done
            mk.wait("vector", mk.ch["tensor"], pv_tch[i])
            raw_done = mk.vop("vector", "tensor_copy",
                              ovraw[i % 2][:, :], pv[:, :])
            grec_of[i] = mk.vop("vector", "reciprocal", rrec[i % 2][:, :],
                                ovraw[i % 2][64:65, :])

        for i in range(len(its) + 3):
            has_prev = 0 <= i - 1 < len(its)
            if i < len(its):
                _, hh, qc = its[i]
                if qc == 0:
                    mk.wait("tensor", kvld[hh % 2], kvtot[hh % 2])
                if has_prev:
                    emit_pv_tiles(i - 1, 0, 6)
                if 0 <= i - 3 < len(its):
                    j = i - 3
                    emit_norm((j, its[j][1], its[j][2], grec_of[j]))
                emit_scores(i, pv_prev=(i - 1) if has_prev else None,
                            pv_tiles=emit_pv_tiles, pv_fin=emit_pv_finish)
            else:
                if 0 <= i - 3 < len(its):
                    j = i - 3
                    emit_norm((j, its[j][1], its[j][2], grec_of[j]))
                if has_prev:
                    emit_pv_tiles(i - 1, 0, NKT)
                    emit_pv_finish(i - 1)
            if has_prev:
                j = i - 1
                if (j + 1) % PERHEAD == 0:
                    p = j // PERHEAD
                    np_ = p + 2
                    if np_ < self.np2 * 2:
                        mk.wait("sync", mk.ch["tensor"], pv_tch[j])
                        kvtot[np_ % 2] = load_head(np_)
        mk.fence("p2end", dma_sems=(ust[0], ust[1]))
        self.p2s.close()

    def ln_chunk(self, xsrc, xsq, ps_s, ps_q, ps_mu, ps_rs, guards=None,
                 ro=0, src_sem=None, src_vch=None):
        mk, st = self.mk, self.st
        guards = guards or {}
        MU, VAR, RSTD = 0, 1, 2

        def row(i):
            return st["rows"][0:1, (i + ro) * CH:(i + ro + 1) * CH]

        vch_need = mk.ch["vector"].n if src_vch is None else src_vch
        mk.wait("scalar", mk.ch["vector"], vch_need)
        mk.wait("scalar", mk.ch["tensor"], guards.get("xb_rd", 0))
        if src_sem is not None:
            for sem_, cnt_ in src_sem:
                mk.wait("scalar", sem_, cnt_)
                mk.wait("tensor", sem_, cnt_)
        H = NB // 2
        s_sq = mk.vop("scalar", "activation", xsq[:, :], xsrc[:, 0:H * CH],
                      AF.Square)
        mk.wait("tensor", mk.ch["vector"], guards.get("rows_done", 0))
        mk.wait("tensor", mk.ch["vector"], vch_need)
        for b in range(NB):
            mm_s = mk.mm(ps_s[0:1, :], st["ones_col"][:, :],
                         xsrc[:, b * CH:(b + 1) * CH], start=(b == 0),
                         stop=(b == NB - 1), inc=(b == NB - 1))
        mk.wait("tensor", mk.ch["scalar"], s_sq)
        for b in range(H):
            mm_q = mk.mm(ps_q[0:1, :], st["onesb"][:, :],
                         xsq[:, b * CH:(b + 1) * CH], start=(b == 0),
                         stop=False, inc=True)
        mk.wait("scalar", mk.ch["tensor"], mm_q)
        s_sq = mk.vop("scalar", "activation", xsq[:, :], xsrc[:, H * CH:NB * CH],
                      AF.Square)
        mk.wait("tensor", mk.ch["scalar"], s_sq)
        for b in range(H):
            mm_q = mk.mm(ps_q[0:1, :], st["onesb"][:, :],
                         xsq[:, b * CH:(b + 1) * CH], start=False,
                         stop=(b == H - 1), inc=(b == H - 1))
        mk.wait("vector", mk.ch["tensor"], mm_s)
        v_mu = mk.vop("vector", "tensor_scalar_mul", row(MU), ps_s[0:1, :], 1.0 / D)
        mk.wait("vector", mk.ch["tensor"], mm_q)
        mk.vop("vector", "tensor_scalar_mul", row(VAR), ps_q[0:1, :], 1.0 / D)
        mk.vop("vector", "tensor_mul", row(RSTD), row(MU), row(MU))
        v_var = mk.vop("vector", "tensor_sub", row(VAR), row(VAR), row(RSTD))
        guards["rows_done"] = v_var
        mk.wait("scalar", mk.ch["vector"], v_var)
        mk.vop("scalar", "activation", row(RSTD), row(VAR), AF.Ln,
               bias=st["cvals"][0:1, 0:1])
        s_rstd = mk.vop("scalar", "activation", row(RSTD), row(RSTD), AF.Exp,
                        scale=-0.5)
        mk.wait("tensor", mk.ch["vector"], guards.get("apply_done", 0))
        mk.wait("tensor", mk.ch["vector"], v_mu)
        mm_mu = mk.mm(ps_mu[:, :], st["ones_row"][:, :], row(MU), inc=True)
        mk.wait("tensor", mk.ch["scalar"], s_rstd)
        mm_rs = mk.mm(ps_rs[:, :], st["ones_row"][:, :], row(RSTD), inc=True)
        guards["xb_rd"] = mm_q
        return mm_mu, mm_rs, guards

    def load_ffn_weights(self, sF):
        mk, st, T = self.mk, self.st, self.T
        self.sb("Wc1", (128, NB * D), BF16, sF)
        self.sb("Wc2", (128, NB * 512), BF16, sF)
        self.sb("Wc3", (128, 4 * 256), BF16, sF)
        self.sb("Wc4", (128, 2 * 1), BF16, sF)
        self.sb("Wg1", (128, NB * 512), BF16, sF)
        self.sb("Wg2", (128, 4 * 256), BF16, sF)
        self.sb("Wg3", (128, 2 * 1), BF16, sF)
        self.sb("cnb1", (128, 8), F32, sF)
        self.sb("cnb2", (128, 4), F32, sF)
        self.sb("cnb3", (128, 2), F32, sF)
        self.sb("gcb1", (128, 4), F32, sF)
        self.sb("gcb2", (128, 2), F32, sF)
        self.sb("arow", (1, 8), F32, sF)
        self.sb("s4full", (1, S2), F32, sF)
        self.sb("crow", (1, S2), F32, sF)
        Builder._lc_n += 1
        ld = mk.sem(f"ffw{Builder._lc_n}")
        with ExitStack() as s:
            self.sb("ffstage", (128, 8448), F32, s)
            self.load_cast(st["Wc1"], T["cn_w1"], D, D, queue="sync", stage="ffstage")
            self.load_cast(st["Wc2"], T["cn_w2"], D, 512, queue="sync", stage="ffstage")
            self.load_cast(st["Wc3"], T["cn_w3"], 512, 256, queue="sync", stage="ffstage")
            self.load_cast(st["Wc4"], T["cn_w4"], 256, 1, queue="sync", stage="ffstage")
            self.load_cast(st["Wg1"], T["gc_w1"], D, 512, queue="sync", stage="ffstage")
            self.load_cast(st["Wg2"], T["gc_w2"], 512, 256, queue="sync", stage="ffstage")
            vlast = self.load_cast(st["Wg3"], T["gc_w3"], 256, 1, queue="sync",
                                   stage="ffstage")
            for dst, srt, nb in (("cnb1", "cn_b1", 8), ("cnb2", "cn_b2", 4),
                                 ("cnb3", "cn_b3", 2), ("gcb1", "gc_b1", 4),
                                 ("gcb2", "gc_b2", 2)):
                mk.dma("sync", st[dst][:, :].rearrange("p (m o) -> p m o", o=1),
                       T[srt][:, :].rearrange("(m p) o -> p m o", p=128), ld,
                       slow_ok=True)
            mk.dma("sync", st["arow"][0:1, 0:1], T["a_range"][:, :], ld)
            mk.dma("sync", st["arow"][0:1, 1:2], T["cn_b4"][:, :], ld)
            mk.dma("sync", st["arow"][0:1, 2:3], T["gc_b3"][:, :], ld)
            mk.wait("sync", mk.ch["vector"], vlast)
            mk.wait("vector", ld, ld.n)
        return ld.n

    def p3(self):
        mk, st, T = self.mk, self.st, self.T
        self.scratch("rowst", (2, S), F32)
        self.scratch("xn2t", (D, S))
        self.scratch("lnt", (D, S2), F32)

        self.sD = ExitStack()
        self.sb("Wo3", (128, NB * D), BF16, self.sD)
        ores = self.sb("ores", (128, NB * S2), F32, self.sD)
        with ExitStack() as s0:
            self.sb("p3stage", (128, 8448), F32, s0)
            vw = self.load_cast(st["Wo3"], T["W_o"], D, D, queue="sync",
                                stage="p3stage")
            mk.wait("sync", mk.ch["vector"], vw)
        self.sF = ExitStack()
        self.load_ffn_weights(self.sF)
        mk.vop("vector", "memset", st["gcs"][:, :], 0.0)

        with ExitStack() as s:
            uc = [self.sb(f"p3u{i}", (128, NB * CH), BF16, s) for i in range(2)]
            xc = [self.sb(f"p3x{i}", (128, NB * CH), F32, s) for i in range(2)]
            xsq = self.sb("p3xsq", (128, NB * CH // 2), BF16, s)
            pwo = [self.psum(f"p3wo{i}", (128, 512), F32, s) for i in range(2)]
            pa = self.psum("p3pa", (128, 512), F32, s)
            pb = self.psum("p3pb", (128, 512), F32, s)
            pc = self.psum("p3pc", (128, 512), F32, s)
            pd = self.psum("p3pd", (128, 512), F32, s)
            uld = [mk.sem("p3u0"), mk.sem("p3u1")]
            xld = [mk.sem("p3xl0"), mk.sem("p3xl1")]
            ost = [mk.sem("p3st0"), mk.sem("p3st1")]
            lst = [mk.sem("p3ls0"), mk.sem("p3ls1")]
            ust = [mk.sem("p2ust0"), mk.sem("p2ust1")]

            def loadu(c):
                mk.dma("sync", uc[c % 2][:, :].rearrange("p (m s) -> p m s", m=NB),
                       T["Ut"][:, c * CH:(c + 1) * CH].rearrange(
                           "(m p) s -> p m s", p=128), uld[c % 2])

            def loadx(c):
                mk.dma("sync", xc[c % 2][:, :].rearrange("p (b s) -> p b s", b=NB),
                       T["x_t"][:, c * CH:(c + 1) * CH].rearrange(
                           "(b p) s -> p b s", p=128), xld[c % 2])

            mk.wait("sync", ust[0], ust[0].n)
            mk.wait("sync", ust[1], ust[1].n)
            loadu(0)
            loadx(0)
            guards = {}
            xfree = [0, 0]
            ost_tot = [0, 0]
            lst_tot = [0, 0]
            for c in range(NCH):
                cols = slice(c * CH, (c + 1) * CH)
                if c + 1 < NCH:
                    mk.wait("sync", mk.ch["tensor"], guards.get("uc_rd", 0))
                    loadu(c + 1)
                    np_ = (c + 1) % 2
                    mk.wait("sync", mk.ch["vector"], xfree[np_])
                    mk.wait("sync", mk.ch["gpsimd"], guards.get("g_ap", 0))
                    mk.wait("sync", ost[np_], ost_tot[np_])
                    mk.wait("sync", lst[np_], lst_tot[np_])
                    loadx(c + 1)
                xcb = xc[c % 2]
                mk.wait("tensor", uld[c % 2], 16 * (c // 2 + 1))
                mk.wait("vector", xld[c % 2], 16 * (c // 2 + 1))
                ub = uc[c % 2]
                for db in range(NB):
                    p = pwo[db % 2]
                    mk.wait("tensor", mk.ch["vector"], guards.get(f"wo{db % 2}", 0))
                    for kb in range(NB):
                        mmw = mk.mm(p[:, :],
                                    st["Wo3"][:, kb * D + db * 128: kb * D + (db + 1) * 128],
                                    ub[:, kb * CH:(kb + 1) * CH],
                                    start=(kb == 0), stop=(kb == NB - 1),
                                    inc=(kb == NB - 1))
                    mk.wait("vector", mk.ch["tensor"], mmw)
                    v_add = mk.vop("vector", "tensor_add",
                                   xcb[:, db * CH:(db + 1) * CH],
                                   xcb[:, db * CH:(db + 1) * CH], p[:, :])
                    guards[f"wo{db % 2}"] = v_add
                guards["uc_rd"] = mmw
                mk.wait("gpsimd", mk.ch["vector"], v_add)
                mk.dma("gpsimd", T["xres"][:, cols].rearrange("(b p) s -> p b s", p=128),
                       xcb[:, :].rearrange("p (b s) -> p b s", b=NB), ost[c % 2])
                ost_tot[c % 2] = ost[c % 2].n
                mk.wait("scalar", mk.ch["gpsimd"], guards.get("g_ap", 0))
                mm_mu, mm_rs, guards = self.ln_chunk(
                    xcb, xsq, pa, pb, pc, pd, guards,
                    src_sem=[(xld[c % 2], 16 * (c // 2 + 1))], src_vch=v_add)
                dst_is_sbuf = c < NCH // 2
                if not dst_is_sbuf:
                    mk.wait("vector", ost[c % 2], ost_tot[c % 2])
                    mk.wait("gpsimd", ost[c % 2], ost_tot[c % 2])
                xqf = xsq[:, :].bitcast(F32)
                mk.wait("vector", mk.ch["tensor"], mm_rs)
                v_pc = mk.vop("vector", "tensor_copy", xqf[:, 0:512], pc[:, :])
                v_pd = mk.vop("vector", "tensor_copy", xqf[:, 512:1024], pd[:, :])

                def dstb(b):
                    return (ores[:, b * S2 + c * CH: b * S2 + (c + 1) * CH]
                            if dst_is_sbuf else xcb[:, b * CH:(b + 1) * CH])

                mk.wait("gpsimd", mk.ch["vector"], v_pd)
                for b in range(4, NB):
                    mk.vop("gpsimd", "tensor_sub", dstb(b),
                           xcb[:, b * CH:(b + 1) * CH], xqf[:, 0:512])
                for b in range(4, NB):
                    g_ap = mk.vop("gpsimd", "tensor_mul", dstb(b), dstb(b),
                                  xqf[:, 512:1024])
                guards["g_ap"] = g_ap
                for b in range(4):
                    mk.vop("vector", "tensor_sub", dstb(b),
                           xcb[:, b * CH:(b + 1) * CH], pc[:, :])
                for b in range(4):
                    v_ap = mk.vop("vector", "tensor_mul", dstb(b), dstb(b),
                                  pd[:, :])
                guards["apply_done"] = v_ap
                for b in range(4):
                    mk.vop("vector", "reduce_sum", st["colsc"][:, 3:4], dstb(b), AX)
                    v_gc = mk.vop("vector", "tensor_add", st["gcs"][:, b:b + 1],
                                  st["gcs"][:, b:b + 1], st["colsc"][:, 3:4])
                mk.wait("vector", mk.ch["gpsimd"], g_ap)
                for b in range(4, NB):
                    mk.vop("vector", "reduce_sum", st["colsc"][:, 3:4], dstb(b), AX)
                    v_gc = mk.vop("vector", "tensor_add", st["gcs"][:, b:b + 1],
                                  st["gcs"][:, b:b + 1], st["colsc"][:, 3:4])
                xfree[c % 2] = max(v_ap, v_gc)
                if not dst_is_sbuf:
                    mk.wait("gpsimd", mk.ch["vector"], v_gc)
                    mk.dma("gpsimd", T["lnt"][:, (c - NCH // 2) * CH:
                                              (c - NCH // 2 + 1) * CH].rearrange(
                        "(b p) s -> p b s", p=128),
                        xcb[:, :].rearrange("p (b s) -> p b s", b=NB), lst[c % 2])
                    lst_tot[c % 2] = lst[c % 2].n
            mk.wait("gpsimd", ost[0], ost[0].n)
            mk.wait("gpsimd", ost[1], ost[1].n)
            mk.wait("gpsimd", lst[0], lst[0].n)
            mk.wait("gpsimd", lst[1], lst[1].n)
            mk.fence("p3end", dma_sems=(ost[0], ost[1], lst[0], lst[1]))

    def gfnet(self):
        mk, st = self.mk, self.st
        v_gc = mk.vop("vector", "tensor_scalar_mul", st["gcs"][:, :],
                      st["gcs"][:, :], 1.0 / S)
        with ExitStack() as s:
            pg = self.psum("gf_p", (128, 512), F32, s)
            h1 = self.sb("gf_h1", (128, 4), BF16, s)
            h2 = self.sb("gf_h2", (128, 2), BF16, s)
            gb = self.sb("gf_gb", (128, 8), BF16, s)
            v_cast = mk.vop("vector", "tensor_copy", gb[:, :], st["gcs"][:, :])
            mk.wait("tensor", mk.ch["vector"], v_cast)
            sc_prev = mk.ch["scalar"].n
            for m in range(4):
                mk.wait("tensor", mk.ch["scalar"], sc_prev)
                for b in range(NB):
                    mm1 = mk.mm(pg[:, m:m + 1],
                                st["Wg1"][:, b * 512 + m * 128: b * 512 + (m + 1) * 128],
                                gb[:, b:b + 1], start=(b == 0), stop=(b == NB - 1),
                                inc=(b == NB - 1))
                mk.wait("scalar", mk.ch["tensor"], mm1)
                sc_prev = mk.vop("scalar", "activation", h1[:, m:m + 1],
                                 pg[:, m:m + 1], AF.Gelu, bias=st["gcb1"][:, m:m + 1])
            mk.wait("tensor", mk.ch["scalar"], sc_prev)
            for m in range(2):
                for b in range(4):
                    mm2 = mk.mm(pg[:, 8 + m:9 + m],
                                st["Wg2"][:, b * 256 + m * 128: b * 256 + (m + 1) * 128],
                                h1[:, b:b + 1], start=(b == 0), stop=(b == 3),
                                inc=(b == 3))
                mk.wait("scalar", mk.ch["tensor"], mm2)
                sc_prev = mk.vop("scalar", "activation", h2[:, m:m + 1],
                                 pg[:, 8 + m:9 + m], AF.Gelu,
                                 bias=st["gcb2"][:, m:m + 1])
            mk.wait("tensor", mk.ch["scalar"], sc_prev)
            for b in range(2):
                mm3 = mk.mm(pg[0:1, 16:17], st["Wg3"][:, b:b + 1], h2[:, b:b + 1],
                            start=(b == 0), stop=(b == 1), inc=(b == 1))
            mk.wait("vector", mk.ch["tensor"], mm3)
            v_b4h = mk.vop("vector", "tensor_scalar_mul", st["arow"][0:1, 5:6],
                           st["arow"][0:1, 2:3], 0.5)
            mk.wait("scalar", mk.ch["vector"], v_b4h)
            s_t = mk.vop("scalar", "activation", st["arow"][0:1, 6:7],
                         pg[0:1, 16:17], AF.Tanh, scale=0.5,
                         bias=st["arow"][0:1, 5:6])
            mk.wait("vector", mk.ch["scalar"], s_t)
            mk.vop("vector", "tensor_scalar_mul", st["arow"][0:1, 3:4],
                   st["arow"][0:1, 0:1], 0.3)
            mk.vop("vector", "tensor_scalar_mul", st["arow"][0:1, 4:5],
                   st["arow"][0:1, 0:1], 0.7)
            mk.vop("vector", "tensor_mul", st["arow"][0:1, 4:5],
                   st["arow"][0:1, 4:5], st["arow"][0:1, 6:7])
            mk.vop("vector", "tensor_scalar_add", st["arow"][0:1, 4:5],
                   st["arow"][0:1, 4:5], 0.1)
            v_k0 = mk.vop("vector", "tensor_scalar_mul", st["arow"][0:1, 5:6],
                          st["arow"][0:1, 1:2], 0.5)
        return v_k0

    _ffn_n = 0

    def ffn_half(self, ores):
        mk, st = self.mk, self.st
        Builder._ffn_n += 1
        mk.fence(f"ffnh{Builder._ffn_n}")
        NC2 = NCH // 2
        with ExitStack() as s:
            ob = [self.sb(f"f_ob{i}", (128, NB * CH), BF16, s) for i in range(2)]
            h1b = self.sb("f_h1", (128, NB * CH), BF16, s)
            h2b = self.sb("f_h2", (128, 4 * CH), BF16, s)
            h3b = self.sb("f_h3", (128, 2 * CH), BF16, s)
            tbf = self.sb("f_t", (128, CH), BF16, s)
            mbf = self.sb("f_m", (128, CH), BF16, s)
            tb2 = self.sb("f_t2", (128, CH), BF16, s)
            mb2 = self.sb("f_m2", (128, CH), BF16, s)
            cpos = [self.sb(f"f_c{i}", (128, CH), F32, s) for i in range(2)]
            ph1 = [self.psum(f"f_p1{i}", (128, 512), F32, s) for i in range(2)]
            ph2 = [self.psum(f"f_p2{i}", (128, 512), F32, s) for i in range(2)]
            ps4 = self.psum("f_p4", (128, 512), F32, s)
            pcb = [self.psum(f"f_pc{i}", (128, 512), F32, s) for i in range(2)]
            guards = {}

            def emit_bcast(c):
                cols = slice(c * CH, (c + 1) * CH)
                mk.wait("tensor", mk.ch["vector"],
                        max(guards.get(f"tw_v{c % 2}", 0), guards.get("crow_v", 0)))
                mk.wait("tensor", mk.ch["gpsimd"], guards.get(f"tw_g{c % 2}", 0))
                mm_c = mk.mm(pcb[c % 2][:, :], st["ones_row"][:, :],
                             st["crow"][0:1, cols], inc=True)
                mk.wait("scalar", mk.ch["tensor"], mm_c)
                s_cb = mk.vop("scalar", "copy", cpos[c % 2][:, :], pcb[c % 2][:, :])
                guards[f"cb{c % 2}"] = s_cb

            def emit_twist(c):
                def oc(b):
                    return ores[:, b * S2 + c * CH: b * S2 + (c + 1) * CH]
                cp = cpos[c % 2][:, :]
                mk.wait("vector", mk.ch["scalar"], guards[f"cb{c % 2}"])
                mk.wait("gpsimd", mk.ch["scalar"], guards[f"cb{c % 2}"])
                for eng, tb, mb, pairs in (
                        ("vector", tbf, mbf, ((0, 4, "add", "subtract"),
                                              (1, 5, "add", "subtract"))),
                        ("gpsimd", tb2, mb2, ((2, 6, "subtract", "add"),
                                              (3, 7, "subtract", "add")))):
                    for (da, db2, sa, sb2) in pairs:
                        mk.vop(eng, "tensor_copy", tb[:, :], oc(da))
                        mk.vop(eng, "tensor_mul", mb[:, :], cp, oc(db2))
                        mk.vop(eng, "tensor_tensor", oc(da), oc(da),
                               mb[:, :], getattr(ALU, sa))
                        mk.vop(eng, "tensor_mul", mb[:, :], cp, tb[:, :])
                        v_tw = mk.vop(eng, "tensor_tensor", oc(db2), oc(db2),
                                      mb[:, :], getattr(ALU, sb2))
                    guards[f"tw_{eng[0]}{c % 2}"] = v_tw
                guards[f"tw_done{c}"] = (guards[f"tw_v{c % 2}"],
                                         guards[f"tw_g{c % 2}"])

            def emit_cast(c, twisted):
                if twisted:
                    vtw, gtw = guards[f"tw_done{c}"]
                    mk.wait("scalar", mk.ch["vector"], vtw)
                    mk.wait("scalar", mk.ch["gpsimd"], gtw)
                mk.wait("scalar", mk.ch["tensor"], guards.get(f"obrd{c % 2}", 0))
                for b in range(NB):
                    s_ob = mk.vop("scalar", "copy", ob[c % 2][:, b * CH:(b + 1) * CH],
                                  ores[:, b * S2 + c * CH: b * S2 + (c + 1) * CH])
                guards[f"cast{c % 2}"] = s_ob

            def emit_passA(c, cyc, twisted, nxt_cast=None):
                cols = slice(c * CH, (c + 1) * CH)
                obc = ob[c % 2]
                mk.wait("tensor", mk.ch["scalar"], guards[f"cast{c % 2}"])
                for m in range(NB):
                    p = ph1[m % 2]
                    mk.wait("tensor", mk.ch["scalar"], guards.get(f"h1_{m % 2}", 0))
                    for b in range(NB):
                        mm1 = mk.mm(p[:, :],
                                    st["Wc1"][:, b * D + m * 128: b * D + (m + 1) * 128],
                                    obc[:, b * CH:(b + 1) * CH],
                                    start=(b == 0), stop=(b == NB - 1),
                                    inc=(b == NB - 1))
                    mk.wait("scalar", mk.ch["tensor"], mm1)
                    guards[f"h1_{m % 2}"] = mk.vop(
                        "scalar", "activation", h1b[:, m * CH:(m + 1) * CH],
                        p[:, :], AF.Gelu, bias=st["cnb1"][:, m:m + 1])
                guards[f"obrd{c % 2}"] = mm1
                mk.wait("tensor", mk.ch["scalar"], guards[f"h1_{(NB - 1) % 2}"])
                for m in range(4):
                    p = ph2[m % 2]
                    mk.wait("tensor", mk.ch["scalar"], guards.get(f"h2_{m % 2}", 0))
                    for b in range(NB):
                        mm2 = mk.mm(p[:, :],
                                    st["Wc2"][:, b * 512 + m * 128: b * 512 + (m + 1) * 128],
                                    h1b[:, b * CH:(b + 1) * CH],
                                    start=(b == 0), stop=(b == NB - 1),
                                    inc=(b == NB - 1))
                    mk.wait("scalar", mk.ch["tensor"], mm2)
                    guards[f"h2_{m % 2}"] = mk.vop(
                        "scalar", "activation", h2b[:, m * CH:(m + 1) * CH],
                        p[:, :], AF.Gelu, bias=st["cnb2"][:, m:m + 1])
                if nxt_cast is not None:
                    emit_cast(*nxt_cast)
                mk.wait("tensor", mk.ch["scalar"], guards[f"h2_{3 % 2}"])
                for m in range(2):
                    p = ph1[m % 2]
                    mk.wait("tensor", mk.ch["scalar"], guards.get(f"h1_{m % 2}", 0))
                    for b in range(4):
                        mm3 = mk.mm(p[:, :],
                                    st["Wc3"][:, b * 256 + m * 128: b * 256 + (m + 1) * 128],
                                    h2b[:, b * CH:(b + 1) * CH],
                                    start=(b == 0), stop=(b == 3), inc=(b == 3))
                    mk.wait("scalar", mk.ch["tensor"], mm3)
                    guards[f"h1_{m % 2}"] = mk.vop(
                        "scalar", "activation", h3b[:, m * CH:(m + 1) * CH],
                        p[:, :], AF.Gelu, bias=st["cnb3"][:, m:m + 1])
                mk.wait("tensor", mk.ch["scalar"], guards[f"h1_{1 % 2}"])
                mk.wait("tensor", mk.ch["scalar"], guards.get("s4", 0))
                for b in range(2):
                    mm4 = mk.mm(ps4[0:1, :], st["Wc4"][:, b:b + 1],
                                h3b[:, b * CH:(b + 1) * CH],
                                start=(b == 0), stop=(b == 1), inc=(b == 1))
                mk.wait("scalar", mk.ch["tensor"], mm4)
                guards["s4"] = mk.vop("scalar", "copy",
                                      st["s4full"][0:1, cols], ps4[0:1, :])

            emit_cast(0, False)
            for cyc in range(3):
                for c in range(NC2):
                    if c + 1 < NC2:
                        nxt_cast = (c + 1, cyc > 0)
                    elif cyc + 1 < 3:
                        nxt_cast = (0, True)
                    else:
                        nxt_cast = None
                    emit_passA(c, cyc, twisted=(cyc > 0), nxt_cast=nxt_cast)
                    cols = slice(c * CH, (c + 1) * CH)
                    s_t4 = mk.vop("scalar", "activation",
                                  st["s4full"][0:1, cols],
                                  st["s4full"][0:1, cols], AF.Tanh, scale=0.5,
                                  bias=st["arow"][0:1, 5:6])
                    mk.wait("vector", mk.ch["scalar"], s_t4)
                    guards["crow_v"] = mk.vop(
                        "vector", "tensor_scalar", st["crow"][0:1, cols],
                        st["s4full"][0:1, cols], st["arow"][0:1, 3:4],
                        st["arow"][0:1, 4:5], ALU.mult, ALU.add)
                    emit_bcast(c)
                    emit_twist(c)

    def residual_half(self, ores, half, dst_name, wait_sems=()):
        mk, st, T = self.mk, self.st, self.T
        NC2 = NCH // 2
        dst = self.y if dst_name == "y" else T[dst_name]
        mk.fence(f"rh{half}{dst_name}")
        with ExitStack() as s:
            xc = [self.sb(f"rh{half}{dst_name}{i}", (128, NB * CH), F32, s)
                  for i in range(2)]
            xld = [mk.sem("rhxa"), mk.sem("rhxb")]
            sst = [mk.sem("rhsa"), mk.sem("rhsb")]
            ss2 = [mk.sem("rh2a"), mk.sem("rh2b")]

            ld_tot = [0, 0]

            def loadx(c):
                src3 = T["xres"][:, (half * NC2 + c) * CH:
                                 (half * NC2 + c + 1) * CH].rearrange(
                    "(b p) s -> p b s", p=128)
                dst3 = xc[c % 2][:, :].rearrange("p (b s) -> p b s", b=NB)
                mk.dma("sync", dst3[:, 0:NB // 2, :], src3[:, 0:NB // 2, :],
                       xld[c % 2])
                mk.dma("scalar", dst3[:, NB // 2:NB, :], src3[:, NB // 2:NB, :],
                       xld[c % 2])
                ld_tot[c % 2] = xld[c % 2].n

            for sem in wait_sems:
                mk.wait("sync", sem, sem.n)
                mk.wait("scalar", sem, sem.n)
            loadx(0)
            freed = [0, 0]
            for c in range(NC2):
                gcol = (half * NC2 + c) * CH
                if c + 1 < NC2:
                    mk.wait("sync", mk.ch["vector"], freed[(c + 1) % 2])
                    mk.wait("scalar", mk.ch["vector"], freed[(c + 1) % 2])
                    mk.wait("sync", sst[(c + 1) % 2], sst[(c + 1) % 2].n)
                    mk.wait("scalar", sst[(c + 1) % 2], sst[(c + 1) % 2].n)
                    mk.wait("sync", ss2[(c + 1) % 2], ss2[(c + 1) % 2].n)
                    mk.wait("scalar", ss2[(c + 1) % 2], ss2[(c + 1) % 2].n)
                    loadx(c + 1)
                mk.wait("vector", xld[c % 2], ld_tot[c % 2])
                for b in range(NB):
                    v_y = mk.vop("vector", "tensor_add",
                                 xc[c % 2][:, b * CH:(b + 1) * CH],
                                 xc[c % 2][:, b * CH:(b + 1) * CH],
                                 ores[:, b * S2 + c * CH: b * S2 + (c + 1) * CH])
                freed[c % 2] = v_y
                mk.wait("gpsimd", mk.ch["vector"], v_y)
                d3 = dst[:, gcol:gcol + CH].rearrange("(b p) s -> p b s", p=128)
                s3 = xc[c % 2][:, :].rearrange("p (b s) -> p b s", b=NB)
                if dst_name == "y":
                    mk.dma("gpsimd", d3, s3, sst[c % 2])
                else:
                    mk.dma("gpsimd", d3[0:64, :, :], s3[0:64, :, :], sst[c % 2])
                    mk.wait("scalar", mk.ch["vector"], v_y)
                    mk.dma("scalar", d3[64:128, :, :], s3[64:128, :, :],
                           ss2[c % 2])
            for e in ENGS:
                mk.wait(e, sst[0], sst[0].n)
                mk.wait(e, sst[1], sst[1].n)
                mk.wait(e, ss2[0], ss2[0].n)
                mk.wait(e, ss2[1], ss2[1].n)

    def load_lnt(self, ores, lnt_sems):
        mk, st, T = self.mk, self.st, self.T
        for lnt_sem in lnt_sems:
            mk.wait("sync", lnt_sem, lnt_sem.n)
        mk.wait("sync", mk.ch["vector"], mk.ch["vector"].n)
        mk.wait("sync", mk.ch["tensor"], mk.ch["tensor"].n)
        mk.wait("sync", mk.ch["scalar"], mk.ch["scalar"].n)
        mk.wait("sync", mk.ch["gpsimd"], mk.ch["gpsimd"].n)
        Builder._lc_n += 1
        sem = mk.sem(f"lnt{Builder._lc_n}")
        for c in range(NCH // 2):
            mk.dma("sync", ores[:, :].rearrange("p (b s) -> p b s", b=NB)[
                :, :, c * CH:(c + 1) * CH],
                T["lnt"][:, c * CH:(c + 1) * CH].rearrange(
                    "(b p) s -> p b s", p=128), sem)
        for e in ("tensor", "vector", "scalar", "gpsimd"):
            mk.wait(e, sem, sem.n)
        return sem

    def p5(self):
        mk, st, T = self.mk, self.st, self.T
        mk.fence("p5pre")
        self.sD.close()

        self.sE = ExitStack()
        q2b = self.sb("q2b", (128, S), BF16, self.sE)
        k2b = self.sb("k2b", (128, S), BF16, self.sE)
        gcols = self.sb("gcols", (128, 8 * 2 + 4), F32, self.sE)
        ld5 = mk.sem("p5ld")
        mk.dma("sync", st["gcols"][:, 0:8].rearrange("p (m o) -> p m o", o=1),
               T["b_ln_g"][:, :].rearrange("(m p) o -> p m o", p=128), ld5,
               slow_ok=True)
        mk.dma("sync", st["gcols"][:, 8:16].rearrange("p (m o) -> p m o", o=1),
               T["b_ln_b"][:, :].rearrange("(m p) o -> p m o", p=128), ld5,
               slow_ok=True)
        mk.dma("sync", st["gcols"][:, 16:17], T["gamma_q"][:, :], ld5)
        mk.dma("sync", st["gcols"][:, 17:18], T["beta_q"][:, :], ld5)
        mk.dma("sync", st["gcols"][:, 18:19], T["gamma_k"][:, :], ld5)
        mk.dma("sync", st["gcols"][:, 19:20], T["beta_k"][:, :], ld5)
        gcold = ld5.n

        with ExitStack() as s:
            zt = self.sb("p5z", (128, S), F32, s)
            stg = self.sb("p5stage", (128, 4224), F32, s)
            wsl = self.sb("p5w", (128, NB * 2048), BF16, s)
            wz = self.sb("p5wz", (128, NB * 128), BF16, s)
            ubc = self.sb("p5ubc", (128, 17), F32, s)
            xc2 = self.sb("p5x2", (128, NB * CH), F32, s)
            xnp = self.sb("p5xnp", (128, NB * CH), F32, s)
            xsq = self.sb("p5xsq", (128, NB * CH // 2), BF16, s)
            xn2 = self.sb("p5xn2", (128, NB * CH), BF16, s)
            ubuf = self.sb("p5ubuf", (128, 16 * CH), BF16, s)
            pa = self.psum("p5pa", (128, 512), F32, s)
            pb = self.psum("p5pb", (128, 512), F32, s)
            pc = self.psum("p5pc", (128, 512), F32, s)
            pd = self.psum("p5pd", (128, 512), F32, s)
            pu = [self.psum(f"p5pu{i}", (128, 512), F32, s) for i in range(2)]

            def wload(c0, c1, dstbuf, dst_off, dst_w):
                n = c1 - c0
                mk.wait("sync", mk.ch["vector"], mk.ch["vector"].n)
                sem = mk.sem(f"p5wl{c0}")
                t = mk.dma("sync", stg[:, 0:NB * n].rearrange("p (b s) -> p b s", b=NB),
                           T["W_in"][:, c0:c1].rearrange("(b p) s -> p b s", p=128),
                           sem)
                mk.wait("vector", sem, t)
                dv = dstbuf[:, :].rearrange("p (b s) -> p b s", b=NB, s=dst_w)
                return mk.vop("vector", "tensor_copy",
                              dv[:, :, dst_off:dst_off + n],
                              stg[:, 0:NB * n].rearrange("p (b s) -> p b s", b=NB))

            for j in range(4):
                wload(j * 512, (j + 1) * 512, wsl, j * 512, 2048)
            wload(4096, 4224, wz, 0, 128)
            ld5b = mk.sem("p5ldb")
            mk.dma("sync", ubc[:, 0:16].rearrange("p (m o) -> p m o", o=1),
                   T["b_in"][:, 0:2048].rearrange("o (m p) -> p m o", p=128), ld5b,
                   slow_ok=True)
            mk.dma("sync", ubc[:, 16:17],
                   T["b_in"][:, 4096:4224].rearrange("o (m p) -> p m o", p=128)[
                       :, 0, :], ld5b, slow_ok=True)
            biasld = ld5b.n

            xld = [mk.sem("p5x2a"), mk.sem("p5x2b")]
            rst = mk.sem("p5rowst")
            u_st = mk.sem("p5ust")
            xnst = mk.sem("p5xnst")
            guards = {}

            def loadx2(c):
                mk.dma("scalar", xc2[:, :].rearrange("p (b s) -> p b s", b=NB),
                       T["xres"][:, c * CH:(c + 1) * CH].rearrange(
                           "(b p) s -> p b s", p=128), xld[c % 2])

            loadx2(0)
            mk.wait("vector", ld5, gcold)
            for c in range(NCH):
                cols = slice(c * CH, (c + 1) * CH)
                mk.wait("vector", xld[c % 2], 16 * (c // 2 + 1))
                mk.wait("vector", rst, rst.n)
                mm_mu, mm_rs, guards = self.ln_chunk(
                    xc2, xsq, pa, pb, pc, pd, guards,
                    src_sem=[(xld[c % 2], 16 * (c // 2 + 1))],
                    src_vch=guards.get("apply_done", 0))
                mk.wait("gpsimd", mk.ch["tensor"], mm_rs)
                mk.dma("gpsimd", T["rowst"][0:1, cols], st["rows"][0:1, 0:CH], rst)
                mk.dma("gpsimd", T["rowst"][1:2, cols],
                       st["rows"][0:1, 2 * CH:3 * CH], rst)
                mk.wait("vector", mk.ch["tensor"], mm_rs)
                for b in range(NB):
                    mk.vop("vector", "tensor_sub", xnp[:, b * CH:(b + 1) * CH],
                           xc2[:, b * CH:(b + 1) * CH], pc[:, :])
                for b in range(NB):
                    v_xnp = mk.vop("vector", "tensor_mul", xnp[:, b * CH:(b + 1) * CH],
                                   xnp[:, b * CH:(b + 1) * CH], pd[:, :])
                guards["apply_done"] = v_xnp
                if c + 1 < NCH:
                    mk.wait("scalar", mk.ch["vector"], v_xnp)
                    loadx2(c + 1)
                mm_mu2, mm_rs2, g2 = self.ln_chunk(xnp, xsq, pa, pb, pc, pd,
                                                   {"rows_done": guards.get("rows_done", 0),
                                                    "apply_done": guards.get("apply2", 0),
                                                    "xb_rd": guards.get("xb_rd", 0)},
                                                   ro=4, src_vch=v_xnp)
                guards["rows_done"] = g2["rows_done"]
                mk.wait("vector", mk.ch["tensor"], mm_rs2)
                for b in range(NB):
                    mk.vop("vector", "tensor_sub", xnp[:, b * CH:(b + 1) * CH],
                           xnp[:, b * CH:(b + 1) * CH], pc[:, :])
                for b in range(NB):
                    mk.vop("vector", "tensor_mul", xnp[:, b * CH:(b + 1) * CH],
                           xnp[:, b * CH:(b + 1) * CH], pd[:, :])
                mk.wait("vector", xnst, xnst.n)
                for b in range(NB):
                    v_xn2 = mk.vop("vector", "tensor_scalar", xn2[:, b * CH:(b + 1) * CH],
                                   xnp[:, b * CH:(b + 1) * CH],
                                   st["gcols"][:, b:b + 1], st["gcols"][:, 8 + b:9 + b],
                                   ALU.mult, ALU.add)
                guards["apply2"] = v_xn2
                mk.wait("gpsimd", mk.ch["vector"], v_xn2)
                mk.dma("gpsimd", T["xn2t"][:, cols].rearrange(
                    "(b p) s -> p b s", p=128),
                    xn2[:, :].rearrange("p (b s) -> p b s", b=NB), xnst)
                mk.wait("tensor", mk.ch["vector"], v_xn2)
                mk.wait("tensor", ld5b, biasld)
                mk.wait("scalar", u_st, u_st.n)
                for m in range(16):
                    p = pu[m % 2]
                    mk.wait("tensor", mk.ch["scalar"], guards.get(f"u{m % 2}", 0))
                    for b in range(NB):
                        mmu = mk.mm(p[:, :],
                                    wsl[:, b * 2048 + m * 128: b * 2048 + (m + 1) * 128],
                                    xn2[:, b * CH:(b + 1) * CH],
                                    start=(b == 0), stop=(b == NB - 1),
                                    inc=(b == NB - 1))
                    mk.wait("scalar", mk.ch["tensor"], mmu)
                    guards[f"u{m % 2}"] = mk.vop(
                        "scalar", "activation", ubuf[:, m * CH:(m + 1) * CH],
                        p[:, :], AF.Silu, bias=ubc[:, m:m + 1])
                mk.wait("gpsimd", mk.ch["scalar"], guards[f"u{15 % 2}"])
                mk.dma("gpsimd", T["ut2"][:, cols].rearrange("(m p) s -> p m s", p=128),
                       ubuf[:, :].rearrange("p (m s) -> p m s", m=16), u_st)
                mk.wait("tensor", mk.ch["vector"], g2["rows_done"])
                for b in range(NB):
                    mmz = mk.mm(pa[:, :],
                                wz[:, b * 128:(b + 1) * 128],
                                xn2[:, b * CH:(b + 1) * CH],
                                start=(b == 0), stop=(b == NB - 1), inc=(b == NB - 1))
                mk.wait("scalar", mk.ch["tensor"], mmz)
                s_z = mk.vop("scalar", "activation", zt[:, cols], pa[:, :],
                             AF.Silu, bias=ubc[:, 16:17])
                mk.wait("tensor", mk.ch["scalar"], s_z)
                guards["rows_done"] = max(guards.get("rows_done", 0),
                                          g2["rows_done"])
            mk.wait("vector", ld5, gcold)
            mk.wait("vector", mk.ch["scalar"], s_z)
            mk.vop("vector", "tensor_scalar", q2b[:, :], zt[:, :],
                   st["gcols"][:, 16:17], st["gcols"][:, 17:18], ALU.mult, ALU.add)
            v_qk = mk.vop("vector", "tensor_scalar", k2b[:, :], zt[:, :],
                          st["gcols"][:, 18:19], st["gcols"][:, 19:20],
                          ALU.mult, ALU.add)
            mk.fence("p5a", dma_sems=(u_st, xnst, rst))
        if self.upto == "p5a":
            return

        with ExitStack() as s:
            stg = self.sb("p5bstage", (128, 4224), F32, s)
            wsl = self.sb("p5bw", (128, NB * 2048), BF16, s)
            xnb = [self.sb(f"p5bx{i}", (128, NB * CH), BF16, s) for i in range(2)]
            bbc = self.sb("p5bbc", (128, E2), F32, s)
            v2s = self.sb("p5bv2", (128, 4 * E2), BF16, s)
            vtmp = self.sb("p5bt", (128, 512), F32, s)
            pv2 = [self.psum(f"p5bp{i}", (128, 512), F32, s) for i in range(2)]
            pbb = self.psum("p5bpb", (128, 512), F32, s)
            binr = self.sb("p5binr", (1, E2), F32, s)

            def wloadb(c0, c1, dst_off):
                n = c1 - c0
                mk.wait("sync", mk.ch["vector"], mk.ch["vector"].n)
                sem = mk.sem(f"p5bw{c0}")
                t = mk.dma("sync", stg[:, 0:NB * n].rearrange("p (b s) -> p b s", b=NB),
                           T["W_in"][:, c0:c1].rearrange("(b p) s -> p b s", p=128),
                           sem)
                mk.wait("vector", sem, t)
                dv = wsl[:, :].rearrange("p (b s) -> p b s", b=NB)
                return mk.vop("vector", "tensor_copy",
                              dv[:, :, dst_off:dst_off + n],
                              stg[:, 0:NB * n].rearrange("p (b s) -> p b s", b=NB))

            for j in range(4):
                wloadb(2048 + j * 512, 2048 + (j + 1) * 512, j * 512)
            sb5 = mk.sem("p5bbi")
            t = mk.dma("sync", binr[:, :], T["b_in"][:, 2048:4096], sb5)
            mk.wait("tensor", sb5, t)
            for j in range(4):
                mk.wait("tensor", mk.ch["scalar"], mk.ch["scalar"].n)
                mm_b = mk.mm(pbb[:, :], st["ones_row"][:, :],
                             binr[0:1, j * 512:(j + 1) * 512], inc=True)
                mk.wait("scalar", mk.ch["tensor"], mm_b)
                mk.vop("scalar", "copy", bbc[:, j * 512:(j + 1) * 512], pbb[:, :])
            xldb = [mk.sem("p5bxa"), mk.sem("p5bxb")]
            v2st = mk.sem("p5bv2st")
            xnrd = mk.sem("p5xnst")

            def loadxn(c):
                mk.dma("scalar", xnb[c % 2][:, :].rearrange("p (b s) -> p b s", b=NB),
                       T["xn2t"][:, c * CH:(c + 1) * CH].rearrange(
                           "(b p) s -> p b s", p=128), xldb[c % 2])

            mk.wait("scalar", xnrd, xnrd.n)
            loadxn(0)
            guards = {}
            freed = [0, 0]
            for c in range(NCH):
                if c + 1 < NCH:
                    mk.wait("scalar", mk.ch["tensor"], freed[(c + 1) % 2])
                    loadxn(c + 1)
                mk.wait("tensor", xldb[c % 2], 16 * (c // 2 + 1))
                mk.wait("scalar", v2st, v2st.n)
                for sT in range(4):
                    for ec in range(4):
                        p = pv2[ec % 2]
                        mk.wait("tensor", mk.ch["scalar"], guards.get(f"v{ec % 2}", 0))
                        for b in range(NB):
                            mmv = mk.mm(p[:, :],
                                        xnb[c % 2][:, b * CH + sT * 128: b * CH + (sT + 1) * 128],
                                        wsl[:, b * 2048 + ec * 512: b * 2048 + (ec + 1) * 512],
                                        start=(b == 0), stop=(b == NB - 1),
                                        inc=(b == NB - 1))
                        mk.wait("vector", mk.ch["tensor"], mmv)
                        mk.wait("vector", mk.ch["scalar"],
                                guards.get(f"v{(ec + 1) % 2}", 0))
                        v_a = mk.vop("vector", "tensor_add", vtmp[:, :], p[:, :],
                                     bbc[:, ec * 512:(ec + 1) * 512])
                        mk.wait("scalar", mk.ch["vector"], v_a)
                        guards[f"v{ec % 2}"] = mk.vop(
                            "scalar", "activation",
                            v2s[:, sT * E2 + ec * 512: sT * E2 + (ec + 1) * 512],
                            vtmp[:, :], AF.Silu)
                freed[c % 2] = mmv
                mk.wait("gpsimd", mk.ch["scalar"], mk.ch["scalar"].n)
                mk.dma("gpsimd", T["v2t"][c * CH:(c + 1) * CH, :].rearrange(
                    "(t p) e -> p t e", p=128),
                    v2s[:, :].rearrange("p (t e) -> p t e", t=4), v2st)
            mk.fence("p5b", dma_sems=(v2st,))

    def p6(self):
        mk, st, T = self.mk, self.st, self.T
        with ExitStack() as s:
            Wout = self.sb("p6Wout", (128, 16 * D), BF16, s)
            with ExitStack() as s6l:
                self.sb("p6stage", (128, 8448), F32, s6l)
                self.load_cast(Wout[:, 0:8 * D], T["W_out"][0:1024, :], 1024, D,
                               queue="sync", stage="p6stage")
                vwo = self.load_cast(Wout[:, 8 * D:16 * D], T["W_out"][1024:2048, :],
                                     1024, D, queue="sync", stage="p6stage")
                mk.wait("sync", mk.ch["vector"], vwo)
                mk.wait("scalar", mk.ch["vector"], vwo)
            attn2 = self.sb("p6attn", (128, NKT * 512), BF16, s)
            a2f = self.sb("p6a2f", (128, 512), F32, s)
            v2b = [self.sb(f"p6v2{i}", (128, NKT * 128), BF16, s) for i in range(2)]
            ubuf = [self.sb("p6u0", (128, 16 * CH), BF16, s)] * 2
            uo = self.sb("p6uo", (128, 16 * CH), BF16, s)
            xc2 = [self.sb(f"p6x{i}", (128, NB * CH), F32, s) for i in range(2)]
            xnp = self.sb("p6xnp", (128, NB * CH), F32, s)
            rws = self.sb("p6rws", (1, 2 * CH), F32, s)
            sc = [self.psum("p6s0", (128, GK * 512), F32, s),
                  self.psum("p6s1", (128, GK * 512), F32, s)]
            ppv = self.psum("p6pv", (128, 512), F32, s)
            pwo = self.psum("p6wo", (128, 512), F32, s)

            ld6 = mk.sem("p6ld")
            boc8 = self.sb("p6bo8", (128, 8), F32, s)
            mk.dma("sync", boc8[:, :].rearrange("p (m o) -> p m o", o=1),
                   T["b_out"][:, :].rearrange("(m p) o -> p m o", p=128), ld6,
                   slow_ok=True)
            bold = ld6.n

            q2b, k2b = st["q2b"], st["k2b"]
            uld = [mk.sem("p6ua"), mk.sem("p6ub")]
            vld = [mk.sem("p6va"), mk.sem("p6vb")]
            xld = [mk.sem("p6xa"), mk.sem("p6xb")]
            rld = mk.sem("p6r")
            ost = [mk.sem("p6sta"), mk.sem("p6stb")]
            ust5 = mk.sem("p5ust")
            v2st5 = mk.sem("p5bv2st")
            rst5 = mk.sem("p5rowst")
            x2st5 = mk.sem("p5st")

            def loadu(c):
                mk.dma("sync", ubuf[c % 2][:, :].rearrange("p (m s) -> p m s", m=16),
                       T["ut2"][:, c * CH:(c + 1) * CH].rearrange(
                           "(m p) s -> p m s", p=128), uld[0])

            def loadx(c):
                mk.dma("scalar", xc2[c % 2][:, :].rearrange("p (b s) -> p b s", b=NB),
                       T["xres"][:, c * CH:(c + 1) * CH].rearrange(
                           "(b p) s -> p b s", p=128), xld[0])

            def loadv(eb, par):
                mk.dma("sync", v2b[par][:, :].rearrange("p (t e) -> p t e", t=NKT),
                       T["v2t"][:, eb * 128:(eb + 1) * 128].rearrange(
                           "(t p) e -> p t e", p=128), vld[par])

            mk.wait("sync", ust5, ust5.n)
            mk.wait("sync", v2st5, v2st5.n)
            mk.wait("scalar", x2st5, x2st5.n)
            mk.wait("scalar", rst5, rst5.n)
            loadu(0)
            loadx(0)
            loadv(0, 0)
            vtot = [16, 0]
            guards = {}
            gidx = 0
            exp_cnt = {}
            rel_cnt = {}
            ufree = [0, 0]
            xfree = [0, 0]
            ost_par = [0, 0]
            def emit_sgroup(cc, g):
                nonlocal gidx
                ccols = slice(cc * CH, (cc + 1) * CH)
                buf = sc[gidx % 2]
                n = min(GK, NKT - g * GK)
                if g == 0:
                    mk.wait("tensor", mk.ch["vector"], guards.get("bcre", 0))
                if gidx >= 2:
                    mk.wait("tensor", mk.ch["scalar"], rel_cnt[gidx - 2])
                for j in range(n):
                    t = g * GK + j
                    mm = mk.mm(buf[:, j * 512:(j + 1) * 512],
                               k2b[:, t * 128:(t + 1) * 128],
                               q2b[:, ccols], start=True, stop=True,
                               inc=(j == n - 1))
                if g == 0:
                    mk.wait("scalar", mk.ch["tensor"], guards.get("pv_done", 0))
                mk.wait("scalar", mk.ch["tensor"], mm)
                aslab = attn2[:, g * GK * 512: g * GK * 512 + n * 512]
                s_r = mk.vop("scalar", "activation", aslab, buf[:, 0:n * 512],
                             AF.Relu, scale=1.0 / S)
                rel_cnt[gidx] = s_r
                mk.wait("vector", mk.ch["scalar"], s_r)
                exp_cnt[gidx] = mk.vop("vector", "tensor_mul", aslab, aslab,
                                       aslab)
                gidx += 1

            NCH6 = getattr(self, "nch6", NCH)
            for c in range(NCH6):
                cols = slice(c * CH, (c + 1) * CH)
                if c > 0:
                    mk.wait("sync", mk.ch["vector"], max(ufree))
                    mk.wait("sync", mk.ch["tensor"], guards.get("pv_done", 0))
                    loadu(c)
                    mk.wait("scalar", mk.ch["vector"], xfree[c % 2])
                    mk.wait("scalar", ost[c % 2], ost_par[c % 2])
                    loadx(c)
                if c == 0:
                    for g in range(NG):
                        emit_sgroup(0, g)
                for eb in range(16):
                    par = eb % 2
                    mk.wait("tensor", vld[par], vtot[par])
                    nxt = (eb + 1 if eb + 1 < 16
                           else (0 if c + 1 < getattr(self, "nch6", NCH) else None))
                    if nxt is not None:
                        opar = (par + 1) % 2
                        mk.wait("sync", mk.ch["tensor"],
                                guards.get(f"pv_mm_{opar}", 0))
                        loadv(nxt, opar)
                        vtot[opar] = vld[opar].n
                    p_pv = (ppv, pwo)[par]
                    if eb < 2:
                        mk.wait("tensor", mk.ch["scalar"],
                                max(guards.get("woev0", 0),
                                    guards.get("woev1", 0)))
                    mk.wait("tensor", mk.ch["vector"], guards.get(f"uo_ev{par}", 0))
                    for t in range(NKT):
                        g = t // GK
                        mk.wait("tensor", mk.ch["vector"], exp_cnt[c * NG + g])
                        mmpv = mk.mm(p_pv[:, :],
                                     v2b[par][:, t * 128:(t + 1) * 128],
                                     attn2[:, t * 512:(t + 1) * 512],
                                     start=(t == 0), stop=(t == NKT - 1),
                                     inc=(t == NKT - 1))
                    guards[f"pv_mm_{par}"] = mmpv
                    guards["pv_done"] = mmpv
                    mk.wait("vector", mk.ch["tensor"], mmpv)
                    mk.wait("vector", uld[0], 16 * (c + 1))
                    guards[f"uo_ev{par}"] = mk.vop(
                        "vector", "tensor_mul", uo[:, eb * CH:(eb + 1) * CH],
                        p_pv[:, :], ubuf[c % 2][:, eb * CH:(eb + 1) * CH])
                guards["uo_ev"] = max(guards["uo_ev0"], guards["uo_ev1"])
                ufree[c % 2] = guards["uo_ev"]
                mk.wait("scalar", mk.ch["tensor"], guards.get("rows_mm", 0))
                mk.dma("scalar", rws[0:1, 0:CH], T["rowst"][0:1, cols], rld)
                mk.dma("scalar", rws[0:1, CH:2 * CH], T["rowst"][1:2, cols], rld)
                r_tot = rld.n
                mk.wait("tensor", rld, r_tot)
                mk.wait("tensor", mk.ch["vector"], guards.get("bcre", 0))
                mm_mu = mk.mm(sc[0][:, 0:512], st["ones_row"][:, :], rws[0:1, 0:CH],
                              inc=True)
                mm_rs = mk.mm(sc[0][:, 512:1024], st["ones_row"][:, :],
                              rws[0:1, CH:2 * CH], inc=True)
                guards["rows_mm"] = mm_rs
                mk.wait("vector", xld[0], 16 * (c + 1))
                mk.wait("vector", mk.ch["tensor"], mm_rs)
                for b in range(NB):
                    mk.vop("vector", "tensor_sub", xnp[:, b * CH:(b + 1) * CH],
                           xc2[c % 2][:, b * CH:(b + 1) * CH], sc[0][:, 0:512])
                for b in range(NB):
                    v_xnp = mk.vop("vector", "tensor_mul",
                                   xnp[:, b * CH:(b + 1) * CH],
                                   xnp[:, b * CH:(b + 1) * CH], sc[0][:, 512:1024])
                guards["bcre"] = v_xnp
                mk.wait("tensor", mk.ch["vector"], guards["uo_ev"])
                mk.wait("tensor", ld6, bold)
                gpos = 0
                nxt = c + 1 if c + 1 < NCH6 else None
                for db in range(NB):
                    p_wo = (pwo, ppv)[db % 2]
                    mk.wait("tensor", mk.ch["scalar"], guards.get(f"woev{db % 2}", 0))
                    for eb in range(16):
                        mmw = mk.mm(p_wo[:, :],
                                    Wout[:, eb * D + db * 128: eb * D + (db + 1) * 128],
                                    uo[:, eb * CH:(eb + 1) * CH],
                                    start=(eb == 0), stop=(eb == 15), inc=(eb == 15))
                    if nxt is not None:
                        upto = ((db + 1) * NG + NB - 1) // NB
                        while gpos < upto:
                            emit_sgroup(nxt, gpos)
                            gpos += 1
                    mk.wait("scalar", mk.ch["tensor"], mmw)
                    mk.wait("scalar", mk.ch["vector"], guards.get("a2f_rd", 0))
                    s_bo = mk.vop("scalar", "activation", a2f[:, 0:512], p_wo[:, :],
                                  AF.Identity, bias=boc8[:, db:db + 1])
                    guards[f"woev{db % 2}"] = s_bo
                    mk.wait("vector", mk.ch["scalar"], s_bo)
                    guards["a2f_rd"] = mk.vop("vector", "tensor_add",
                                              xnp[:, db * CH:(db + 1) * CH],
                                              xnp[:, db * CH:(db + 1) * CH],
                                              a2f[:, 0:512])
                    v_x3 = mk.vop("vector", "tensor_add",
                                  xc2[c % 2][:, db * CH:(db + 1) * CH],
                                  xc2[c % 2][:, db * CH:(db + 1) * CH],
                                  xnp[:, db * CH:(db + 1) * CH])
                xfree[c % 2] = v_x3
                mk.wait("gpsimd", mk.ch["vector"], v_x3)
                mk.dma("gpsimd", T["xres"][:, cols].rearrange("(b p) s -> p b s", p=128),
                       xc2[c % 2][:, :].rearrange("p (b s) -> p b s", b=NB),
                       ost[c % 2])
                ost_par[c % 2] = ost[c % 2].n
            mk.fence("p6end", dma_sems=(ost[0], ost[1]))
        self.sE.close()

    def p7pre(self):
        mk, st, T = self.mk, self.st, self.T
        self.sG = ExitStack()
        ores = self.sb("ores2", (128, NB * S2), F32, self.sG)
        self.sF2 = ExitStack()
        self.load_ffn_weights(self.sF2)
        mk.vop("vector", "memset", st["gcs"][:, :], 0.0)
        with ExitStack() as s:
            xc = [self.sb(f"p7x{i}", (128, NB * CH), F32, s) for i in range(2)]
            xsq = self.sb("p7xsq", (128, NB * CH // 2), BF16, s)
            pa = self.psum("p7pa", (128, 512), F32, s)
            pb = self.psum("p7pb", (128, 512), F32, s)
            pc = self.psum("p7pc", (128, 512), F32, s)
            pd = self.psum("p7pd", (128, 512), F32, s)
            xld = [mk.sem("p7xa"), mk.sem("p7xb")]
            lst = [mk.sem("p7lnst"), mk.sem("p7lnst2")]
            lst_tot = [0, 0]

            def loadx(c):
                mk.dma("scalar", xc[c % 2][:, :].rearrange("p (b s) -> p b s", b=NB),
                       T["xres"][:, c * CH:(c + 1) * CH].rearrange(
                           "(b p) s -> p b s", p=128), xld[c % 2])

            mk.wait("scalar", mk.ch["vector"], mk.ch["vector"].n)
            loadx(0)
            guards = {}
            freed = [0, 0]
            g_ap = 0
            for c in range(NCH):
                if c + 1 < NCH:
                    mk.wait("scalar", mk.ch["vector"], freed[(c + 1) % 2])
                    mk.wait("scalar", mk.ch["gpsimd"], guards.get("g_ap", 0))
                    mk.wait("scalar", lst[(c + 1) % 2], lst_tot[(c + 1) % 2])
                    loadx(c + 1)
                mk.wait("vector", xld[c % 2], 16 * (c // 2 + 1))
                mk.wait("scalar", mk.ch["gpsimd"], guards.get("g_ap", 0))
                mm_mu, mm_rs, guards = self.ln_chunk(
                    xc[c % 2], xsq, pa, pb, pc, pd, guards,
                    src_sem=[(xld[c % 2], 16 * (c // 2 + 1))], src_vch=0)
                dst_is_sbuf = c < NCH // 2
                xqf = xsq[:, :].bitcast(F32)
                mk.wait("vector", mk.ch["tensor"], mm_rs)
                v_pc = mk.vop("vector", "tensor_copy", xqf[:, 0:512], pc[:, :])
                v_pd = mk.vop("vector", "tensor_copy", xqf[:, 512:1024], pd[:, :])

                def dstb(b):
                    return (ores[:, b * S2 + c * CH: b * S2 + (c + 1) * CH]
                            if dst_is_sbuf else xc[c % 2][:, b * CH:(b + 1) * CH])

                mk.wait("gpsimd", mk.ch["vector"], v_pd)
                for b in range(4, NB):
                    mk.vop("gpsimd", "tensor_sub", dstb(b),
                           xc[c % 2][:, b * CH:(b + 1) * CH], xqf[:, 0:512])
                for b in range(4, NB):
                    g_ap = mk.vop("gpsimd", "tensor_mul", dstb(b), dstb(b),
                                  xqf[:, 512:1024])
                guards["g_ap"] = g_ap
                for b in range(4):
                    mk.vop("vector", "tensor_sub", dstb(b),
                           xc[c % 2][:, b * CH:(b + 1) * CH], pc[:, :])
                for b in range(4):
                    v_ap = mk.vop("vector", "tensor_mul", dstb(b), dstb(b),
                                  pd[:, :])
                guards["apply_done"] = v_ap
                for b in range(4):
                    mk.vop("vector", "reduce_sum", st["colsc"][:, 3:4], dstb(b), AX)
                    v_gc = mk.vop("vector", "tensor_add", st["gcs"][:, b:b + 1],
                                  st["gcs"][:, b:b + 1], st["colsc"][:, 3:4])
                mk.wait("vector", mk.ch["gpsimd"], g_ap)
                for b in range(4, NB):
                    mk.vop("vector", "reduce_sum", st["colsc"][:, 3:4], dstb(b), AX)
                    v_gc = mk.vop("vector", "tensor_add", st["gcs"][:, b:b + 1],
                                  st["gcs"][:, b:b + 1], st["colsc"][:, 3:4])
                freed[c % 2] = max(v_ap, v_gc)
                if not dst_is_sbuf:
                    mk.wait("gpsimd", mk.ch["vector"], v_gc)
                    mk.dma("gpsimd", T["lnt"][:, (c - NCH // 2) * CH:
                                              (c - NCH // 2 + 1) * CH].rearrange(
                        "(b p) s -> p b s", p=128),
                        xc[c % 2][:, :].rearrange("p (b s) -> p b s", b=NB),
                        lst[c % 2])
                    lst_tot[c % 2] = lst[c % 2].n
                    freed[c % 2] = max(v_ap, v_gc)
            mk.fence("p7pre", dma_sems=(lst[0], lst[1]))

    def p8(self):
        pass


def _host_map(inputs, core):
    np32 = np.float32
    x = np.ascontiguousarray(np.asarray(inputs["L0_attn_a_x"][core]).T)
    m = {"x_t": x.astype(np32, copy=False)}
    def a(k):
        return np.ascontiguousarray(np.asarray(inputs[k], dtype=np32))
    m["W_q"] = a("L0_attn_a_W_q")
    m["W_k"] = a("L0_attn_a_W_k")
    m["W_v"] = a("L0_attn_a_W_v")
    m["W_o"] = a("L0_attn_a_W_o")
    m["W_down"] = a("L0_attn_a_W_down_kv")
    m["W_up_k"] = a("L0_attn_a_W_up_k")
    m["W_up_v"] = a("L0_attn_a_W_up_v")
    m["b_ln_g"] = a("L0_attn_b_ln_g").reshape(-1, 1)
    m["b_ln_b"] = a("L0_attn_b_ln_b").reshape(-1, 1)
    m["W_in"] = a("L0_attn_b_W_in")
    m["b_in"] = a("L0_attn_b_b_in").reshape(1, -1)
    m["W_out"] = a("L0_attn_b_W_out")
    m["b_out"] = a("L0_attn_b_b_out").reshape(-1, 1)
    m["gamma_q"] = a("L0_attn_b_gamma_q").reshape(-1, 1)
    m["beta_q"] = a("L0_attn_b_beta_q").reshape(-1, 1)
    m["gamma_k"] = a("L0_attn_b_gamma_k").reshape(-1, 1)
    m["beta_k"] = a("L0_attn_b_beta_k").reshape(-1, 1)
    for i in range(1, 5):
        m[f"cn_w{i}"] = a(f"L0_ffn_cn_w{i}")
        m[f"cn_b{i}"] = a(f"L0_ffn_cn_b{i}").reshape(-1, 1)
    for i in range(1, 4):
        m[f"gc_w{i}"] = a(f"L0_ffn_gc_w{i}")
        m[f"gc_b{i}"] = a(f"L0_ffn_gc_b{i}").reshape(-1, 1)
    m["a_range"] = a("L0_ffn_adaptive_range").reshape(1, 1)
    return m


def kernel(**inputs):
    n_cores = 8
    in_maps = [_host_map(inputs, c) for c in range(n_cores)]
    b = Builder()
    nc = b.build()
    res = run_bass_kernel_spmd(nc, in_maps, core_ids=list(range(n_cores)))
    out = np.stack([np.ascontiguousarray(r["y_t"].T) for r in res.results], axis=0)
    return out.astype(np.float32, copy=False)


# version 11
# speedup vs baseline: 1.0070x; 1.0070x over previous
import sys
import math
import numpy as np
from contextlib import ExitStack

if "/opt/trn_rl_repo" not in sys.path:
    sys.path.insert(0, "/opt/trn_rl_repo")

import ml_dtypes
import concourse.bass as bass
import concourse.mybir as mybir
from concourse.bass_utils import run_bass_kernel_spmd

F32 = mybir.dt.float32
BF16 = mybir.dt.bfloat16
AF = mybir.ActivationFunctionType
ALU = mybir.AluOpType
AX = mybir.AxisListType.X

D = 1024
S = 4096
NB = 8
NCH = 8
CH = 512
NH = 16
HD = 64
DL = 256
LAM = 0.01
E2 = 2048
ZS = 128
F_IN = 4224
LN_EPS = 1e-5
S2 = S // 2
NKT = 32
GK = 3
NG = (NKT + GK - 1) // GK

ENGS = ("tensor", "vector", "scalar", "gpsimd", "sync")


class Sem:
    __slots__ = ("h", "n")

    def __init__(self, h):
        self.h = h
        self.n = 0

    def inc(self, k=1):
        self.n += k
        return self.n


class MK:
    def __init__(self, nc, ctx):
        self.nc = nc
        self.ctx = ctx
        self.q = {e: [] for e in ENGS}
        self._sems = {}
        self._lastwait = {}
        self.ch = {e: self.sem("ch_" + e) for e in ("tensor", "vector", "scalar", "gpsimd")}

    def sem(self, name):
        if name not in self._sems:
            self._sems[name] = Sem(self.ctx.enter_context(self.nc.semaphore(name)))
        return self._sems[name]

    def em(self, eng, fn):
        self.q[eng].append(fn)

    def wait(self, eng, sem, v):
        if v <= 0:
            return
        key = (eng, id(sem))
        if self._lastwait.get(key, -1) >= v:
            return
        self._lastwait[key] = v
        h = sem.h
        self.em(eng, lambda e: e.wait_ge(h, v))

    def mm(self, out, lhsT, rhs, start=True, stop=True, inc=False, tp=None):
        tch = self.ch["tensor"]
        cnt = tch.inc() if inc else None
        h = tch.h
        def f(e, out=out, lhsT=lhsT, rhs=rhs, start=start, stop=stop, tp=tp, cnt=cnt):
            i = e.matmul(out, lhsT, rhs, start=start, stop=stop, tile_position=tp)
            if cnt is not None:
                i.then_inc(h, 1)
        self.em("tensor", f)
        return cnt

    def trans(self, out, in_, ident, inc=True):
        tch = self.ch["tensor"]
        cnt = tch.inc() if inc else None
        h = tch.h
        def f(e, out=out, in_=in_, ident=ident, cnt=cnt):
            i = e.transpose(out, in_, ident)
            if cnt is not None:
                i.then_inc(h, 1)
        self.em("tensor", f)
        return cnt

    def vop(self, eng, name, *args, **kw):
        ch = self.ch[eng]
        prev = ch.n
        cur = ch.inc()
        h = ch.h
        def f(e, name=name, args=args, kw=kw, prev=prev):
            if prev > 0:
                e.wait_ge(h, prev)
            getattr(e, name)(*args, **kw).then_inc(h, 1)
        self.em(eng, f)
        return cur

    def dma(self, eng, dst, src, sem, slow_ok=False):
        tot = sem.inc(16)
        h = sem.h
        def f(e, dst=dst, src=src):
            e.dma_start(dst, src,
                        allow_slow_non_contiguous=slow_ok).then_inc(h, 16)
        self.em(eng, f)
        return tot

    def fence(self, name, engs=ENGS, dma_sems=()):
        counts = [(self.ch[e2], self.ch[e2].n)
                  for e2 in ("tensor", "vector", "scalar", "gpsimd")]
        counts += [(s2, s2.n) for s2 in dma_sems]
        for e in engs:
            for s2, v in counts:
                self.wait(e, s2, v)

    def run(self):
        with self.nc.Block() as block:
            for eng in ENGS:
                ops = self.q[eng]
                getattr(block, eng)(lambda e, ops=ops: [f(e) for f in ops])


def make_consts(nc):
    c = {}
    j = np.arange(CH, dtype=np.float64)
    c["esc_q1"] = np.tile((np.exp(-LAM * j) / 32.0)[None, :], (128, 1)).astype(np.float32)
    c["esc_q2"] = np.tile((np.exp(+LAM * j) / 32.0)[None, :], (128, 1)).astype(np.float32)
    c["esc_k1"] = np.tile(np.exp(+LAM * j)[None, :], (128, 1)).astype(np.float32)
    c["esc_k2"] = np.tile(np.exp(-LAM * j)[None, :], (128, 1)).astype(np.float32)
    kk = np.arange(128, dtype=np.float64)[:, None]
    qq = np.arange(512, dtype=np.float64)[None, :]
    def c1(b, w):
        kg = kk + b
        return np.where(qq[:, :w] < kg, np.exp(-2 * LAM * (kg - qq[:, :w])), 1.0).astype(np.float32)
    def c2(b, w0):
        kg = kk + b
        qs = qq[:, w0:]
        return np.where(qs > kg, np.exp(-2 * LAM * (qs - kg)), 1.0).astype(np.float32)
    c["cor0"] = c1(0, 128)
    c["cor1"] = c1(128, 256)
    c["cor2"] = c2(256, 256)
    c["cor3"] = c2(384, 384)
    c["ones_col"] = np.ones((128, 1), dtype=np.float32)
    cv = np.zeros((128, 4), dtype=np.float32)
    cv[:, 0] = LN_EPS
    cv[:, 1] = 1.0
    c["cvals"] = cv
    c["ones_row"] = np.ones((1, 128), dtype=np.float32)
    c["ident"] = np.eye(128, dtype=ml_dtypes.bfloat16)
    return {k: nc.inline_tensor(np.ascontiguousarray(v), name="c_" + k)
            for k, v in c.items()}


INPUT_SPECS = [
    ("x_t", (D, S)),
    ("W_q", (D, D)), ("W_k", (D, D)), ("W_v", (D, D)), ("W_o", (D, D)),
    ("W_down", (D, DL)), ("W_up_k", (DL, D)), ("W_up_v", (DL, D)),
    ("b_ln_g", (D, 1)), ("b_ln_b", (D, 1)),
    ("W_in", (D, F_IN)), ("b_in", (1, F_IN)),
    ("W_out", (E2, D)), ("b_out", (D, 1)),
    ("gamma_q", (ZS, 1)), ("beta_q", (ZS, 1)),
    ("gamma_k", (ZS, 1)), ("beta_k", (ZS, 1)),
    ("cn_w1", (D, D)), ("cn_b1", (D, 1)),
    ("cn_w2", (D, D // 2)), ("cn_b2", (D // 2, 1)),
    ("cn_w3", (D // 2, D // 4)), ("cn_b3", (D // 4, 1)),
    ("cn_w4", (D // 4, 1)), ("cn_b4", (1, 1)),
    ("gc_w1", (D, D // 2)), ("gc_b1", (D // 2, 1)),
    ("gc_w2", (D // 2, D // 4)), ("gc_b2", (D // 4, 1)),
    ("gc_w3", (D // 4, 1)), ("gc_b3", (1, 1)),
    ("a_range", (1, 1)),
]


class Builder:
    def __init__(self, taps=(), upto="p8", nch=NCH, p0sub=99, np2=NH // 2):
        self.taps = set(taps)
        self.upto = upto
        self.nch = nch
        self.p0sub = p0sub
        self.np2 = np2
        nc = bass.Bass("TRN2", target_bir_lowering=False)
        self.nc = nc
        self.ctx = ExitStack()
        nc.ctx.enter_context(self.ctx)
        self.T = {}
        for name, shape in INPUT_SPECS:
            self.T[name] = nc.dram_tensor(name, list(shape), F32, kind="ExternalInput")
        self.y = nc.dram_tensor("y_t", [D, S], F32, kind="ExternalOutput")
        self.consts = make_consts(nc)
        self.mk = MK(nc, self.ctx)
        self.st = {}
        self.phase_stacks = []
        Builder._name_n = {}

    def scratch(self, name, shape, dt=BF16):
        kind = "ExternalOutput" if name in self.taps else "Internal"
        self.T[name] = self.nc.dram_tensor(name, list(shape), dt, kind=kind)
        return self.T[name]

    _name_n = {}

    def sb(self, name, shape, dt, stack=None):
        n = Builder._name_n.get(name, 0)
        Builder._name_n[name] = n + 1
        alloc_name = name if n == 0 else f"{name}__{n}"
        t = (stack or self.ctx).enter_context(
            self.nc.sbuf_tensor(alloc_name, list(shape), dt))
        self.st[name] = t
        return t

    def psum(self, name, shape, dt=F32, stack=None):
        n = Builder._name_n.get(name, 0)
        Builder._name_n[name] = n + 1
        alloc_name = name if n == 0 else f"{name}__{n}"
        t = (stack or self.ctx).enter_context(
            self.nc.psum_tensor(alloc_name, list(shape), dt))
        self.st[name] = t
        return t

    def build(self):
        mk, st = self.mk, self.st
        self.scratch("qt1", (D, S))
        self.scratch("qt2", (D, S))
        self.scratch("kt1", (D, S))
        self.scratch("kt2", (D, S))
        self.scratch("vat", (S, NH * 65))
        self.scratch("Ut", (D, S))
        self.scratch("xres", (D, S), F32)
        self.scratch("ut2", (E2, S))
        self.scratch("v2t", (S, E2))

        for nm, sh in (("cor0", (128, 128)), ("cor1", (128, 256)),
                       ("cor2", (128, 256)), ("cor3", (128, 128)),
                       ("ones_col", (128, 1)), ("ones_row", (1, 128)),
                       ("cvals", (128, 4))):
            self.sb(nm, sh, F32)
        self.sb("identb", (128, 128), BF16)
        self.sb("onesb", (128, 1), BF16)
        self.sb("negc", (128, 1), F32)
        self.sb("qabs", (128, 1), F32)
        self.sb("kabs", (128, 1), F32)
        self.sb("rows", (1, 8 * CH), F32)
        self.sb("colsc", (128, 8), F32)
        self.sb("gcs", (128, 8), F32)
        self.sA = ExitStack()
        for nm in ("esc_q1", "esc_q2", "esc_k1", "esc_k2"):
            self.sb(nm, (128, CH), F32, self.sA)
        self.sb("stage", (128, 8448), F32, self.sA)

        self.p0()
        if self.upto == "p0":
            self.mk.run()
            return self.nc
        self.p1()
        if self.upto == "p1":
            self.mk.run()
            return self.nc
        self.p2()
        if self.upto == "p2":
            self.mk.run()
            return self.nc
        self.p3()
        if self.upto == "p3":
            self.mk.run()
            return self.nc
        lst = (self.mk.sem("p3ls0"), self.mk.sem("p3ls1"))
        self.gfnet()
        if self.upto == "gf1":
            self.mk.run()
            return self.nc
        self.ffn_half(self.st["ores"])
        if self.upto == "ffn1a":
            self.mk.run()
            return self.nc
        self.residual_half(self.st["ores"], 0, "xres")
        if self.upto == "res0":
            self.mk.run()
            return self.nc
        self.load_lnt(self.st["ores"], lst)
        self.ffn_half(self.st["ores"])
        self.residual_half(self.st["ores"], 1, "xres")
        self.sF.close()
        if self.upto == "res1":
            self.mk.run()
            return self.nc
        self.p5()
        if self.upto in ("p5", "p5a"):
            self.mk.run()
            return self.nc
        self.p6()
        if self.upto == "p6":
            self.mk.run()
            return self.nc
        self.p7pre()
        lst2 = (self.mk.sem("p7lnst"), self.mk.sem("p7lnst2"))
        self.gfnet()
        self.ffn_half(self.st["ores2"])
        self.residual_half(self.st["ores2"], 0, "y")
        self.load_lnt(self.st["ores2"], lst2)
        self.ffn_half(self.st["ores2"])
        self.residual_half(self.st["ores2"], 1, "y")
        self.sF2.close()
        self.sG.close()
        self.p8()
        self.mk.run()
        return self.nc

    _lc_n = 0

    def load_cast(self, dst, src, rows, cols, ld=None, queue="sync",
                  stage="stage"):
        mk, st = self.mk, self.st
        Builder._lc_n += 1
        s = mk.sem(f"lc{Builder._lc_n}")
        nblk = max(1, rows // 128)
        mk.wait(queue, mk.ch["vector"], mk.ch["vector"].n)
        if rows >= 128:
            tot = mk.dma(queue, st[stage][:, 0:nblk * cols].rearrange(
                "p (b c) -> p b c", b=nblk),
                src[:, :].rearrange("(b p) c -> p b c", p=128), s,
                slow_ok=(cols <= 4))
        else:
            tot = mk.dma(queue, st[stage][0:rows, 0:cols], src[:, :], s)
        mk.wait("vector", s, tot)
        if rows >= 128:
            return mk.vop("vector", "tensor_copy", dst[:, 0:nblk * cols],
                          st[stage][:, 0:nblk * cols])
        return mk.vop("vector", "tensor_copy", dst[0:rows, 0:cols],
                      st[stage][0:rows, 0:cols])

    def p0(self):
        mk, st, T, C = self.mk, self.st, self.T, self.consts
        self.sb("Wq", (128, NB * D), BF16, self.sA)
        self.sb("Wld", (128, NB * DL), BF16, self.sA)
        self.sb("Wuk", (128, 2 * D), BF16, self.sA)
        self.sb("Wuv", (128, 2 * D), BF16, self.sA)
        ld = mk.sem("p0ld")

        for nm in ("esc_q1", "esc_q2", "esc_k1", "esc_k2",
                   "cor0", "cor1", "cor2", "cor3", "ones_col", "ones_row",
                   "cvals"):
            mk.dma("sync", st[nm][:, :], C[nm][:, :], ld)
        mk.dma("sync", st["identb"][:, :], C["ident"][:, :], ld)
        cld = ld.n
        mk.wait("vector", ld, cld)
        mk.vop("vector", "tensor_copy", st["onesb"][:, :], st["ones_col"][:, :])
        mk.vop("vector", "memset", st["qabs"][:, :], 0.0)
        mk.vop("vector", "memset", st["kabs"][:, :], 0.0)

        if self.p0sub < 2:
            return
        self.load_cast(st["Wq"], T["W_q"], D, D)
        self.load_cast(st["Wuk"], T["W_up_k"], DL, D)
        self.load_cast(st["Wuv"], T["W_up_v"], DL, D)
        if self.p0sub < 3:
            return

        with ExitStack() as p0s:
            self.sb("stage2", (128, 8192), F32, p0s)
            self.sb("Wkv", (128, NB * D), BF16, p0s)
            self.sb("WkvT", (128, NB * D), BF16, p0s)
            self.sb("Wdn", (128, NB * DL), BF16, p0s)
            ptrs = [self.psum(f"ptr{i}", (128, 512), F32, p0s) for i in range(4)]
            pw0 = self.psum("pw0", (128, 512), F32, p0s)
            pw1 = self.psum("pw1", (128, 512), F32, p0s)

            swk = mk.sem("p0wk")
            swv = mk.sem("p0wv")
            t1 = mk.dma("sync", st["stage2"][:, :].rearrange("p (b c) -> p b c", b=NB),
                        T["W_k"][:, :].rearrange("(b p) c -> p b c", p=128), swk)
            vc = self.load_cast(st["Wdn"], T["W_down"], D, DL, queue="scalar")
            mk.wait("sync", mk.ch["vector"], vc)
            t2 = mk.dma("sync", st["stage"][:, 0:NB * D].rearrange(
                "p (b c) -> p b c", b=NB),
                T["W_v"][:, :].rearrange("(b p) c -> p b c", p=128), swv)
            mk.wait("vector", swk, t1)
            mk.wait("vector", swv, t2)
            if self.p0sub < 4:
                return
            vkv = mk.vop("vector", "tensor_add", st["Wkv"][:, :],
                         st["stage"][:, 0:NB * D], st["stage2"][:, :])

            mk.wait("tensor", mk.ch["vector"], vkv)
            mk.wait("tensor", ld, ld.n)
            evs = []
            for j in range(NB * NB):
                di, dm = divmod(j, NB)
                pslice = ptrs[j % 4][:, :].bitcast(BF16)[:, 0:128]
                if j >= 4:
                    mk.wait("tensor", mk.ch["scalar"], evs[j - 4])
                tcnt = mk.trans(pslice,
                                st["Wkv"][:, di * D + dm * 128: di * D + (dm + 1) * 128],
                                st["identb"][:, :])
                mk.wait("scalar", mk.ch["tensor"], tcnt)
                evs.append(mk.vop("scalar", "copy",
                                  st["WkvT"][:, dm * D + di * 128: dm * D + (di + 1) * 128],
                                  pslice))
            if self.p0sub < 5:
                return
            mk.wait("tensor", mk.ch["scalar"], evs[-1])
            wevs = []
            for j in range(NB * 2):
                di, lh = divmod(j, 2)
                p = (pw0, pw1)[j % 2]
                if j >= 2:
                    mk.wait("tensor", mk.ch["scalar"], wevs[j - 2])
                for dm in range(NB):
                    tcnt = mk.mm(p[:, 0:128],
                                 st["WkvT"][:, dm * D + di * 128: dm * D + (di + 1) * 128],
                                 st["Wdn"][:, dm * DL + lh * 128: dm * DL + (lh + 1) * 128],
                                 start=(dm == 0), stop=(dm == NB - 1),
                                 inc=(dm == NB - 1))
                mk.wait("scalar", mk.ch["tensor"], tcnt)
                wevs.append(mk.vop("scalar", "copy",
                                   st["Wld"][:, di * DL + lh * 128: di * DL + (lh + 1) * 128],
                                   p[:, 0:128]))
            if self.p0sub < 6:
                return
            mk.fence("p0")

    def p1(self):
        mk, st, T = self.mk, self.st, self.T
        s = self.sA
        xc = [self.sb("xc0", (128, NB * CH), F32, s),
              self.sb("xc1", (128, NB * CH), F32, s)]
        self.sb("xsq", (128, NB * CH), BF16, s)
        self.sb("xn", (128, NB * CH), BF16, s)
        self.sb("latc", (128, 2 * CH), BF16, s)
        self.sb("qo1", (128, NB * CH), BF16, s)
        self.sb("qo2", (128, NB * CH), BF16, s)
        self.sb("ko1", (128, NB * CH), BF16, s)
        self.sb("ko2", (128, NB * CH), BF16, s)
        self.sb("vo", (128, 4 * NH * 65), BF16, s)
        pa = self.psum("pa", (128, 512), F32, s)
        pb = self.psum("pb", (128, 512), F32, s)
        pc = self.psum("pc", (128, 512), F32, s)
        pd = self.psum("pd", (128, 512), F32, s)
        pe_ = self.psum("pe", (128, 512), F32, s)
        pf = self.psum("pf", (128, 512), F32, s)
        pg = self.psum("pg", (128, 512), F32, s)
        ph = self.psum("ph", (128, 512), F32, s)

        xld = [mk.sem("p1x0"), mk.sem("p1x1")]
        ost = mk.sem("p1st")
        MU, VAR, RSTD, MU2 = 0, 1, 2, 3

        def row(i):
            return st["rows"][0:1, i * CH:(i + 1) * CH]

        mk.dma("sync", xc[0][:, :].rearrange("p (b c) -> p b c", b=NB),
               T["x_t"][:, 0:CH].rearrange("(b p) c -> p b c", p=128), xld[0])

        xfree = [0, 0]
        prev_ev = {}
        last_store = 0
        for c in range(self.nch):
            buf = xc[c % 2]
            sc_q1 = float(math.exp(-LAM * CH * c))
            sc_q2 = float(math.exp(+LAM * CH * c))
            sc_k1 = float(math.exp(+LAM * CH * c))
            sc_k2 = float(math.exp(-LAM * CH * c))
            if c + 1 < self.nch:
                mk.wait("sync", mk.ch["vector"], xfree[(c + 1) % 2])
                mk.dma("sync", xc[(c + 1) % 2][:, :].rearrange("p (b c) -> p b c", b=NB),
                       T["x_t"][:, (c + 1) * CH:(c + 2) * CH].rearrange(
                           "(b p) c -> p b c", p=128), xld[(c + 1) % 2])

            mk.wait("vector", xld[c % 2], 16 * (c // 2 + 1))
            mk.wait("scalar", xld[c % 2], 16 * (c // 2 + 1))
            mk.wait("scalar", mk.ch["tensor"], prev_ev.get("xsq_rd", 0))
            s_sq = mk.vop("scalar", "activation", st["xsq"][:, :], buf[:, :],
                          AF.Square)
            mk.wait("tensor", mk.ch["vector"], prev_ev.get("rows_done", 0))
            mk.wait("tensor", xld[c % 2], 16 * (c // 2 + 1))
            for b in range(NB):
                mm_s = mk.mm(pa[0:1, :], st["ones_col"][:, :],
                             buf[:, b * CH:(b + 1) * CH], start=(b == 0),
                             stop=(b == NB - 1), inc=(b == NB - 1))
            mk.wait("tensor", mk.ch["scalar"], s_sq)
            for b in range(NB):
                mm_q = mk.mm(pb[0:1, :], st["onesb"][:, :],
                             st["xsq"][:, b * CH:(b + 1) * CH], start=(b == 0),
                             stop=(b == NB - 1), inc=(b == NB - 1))
            prev_ev["xsq_rd"] = mm_q
            mk.wait("vector", mk.ch["tensor"], mm_s)
            v_mu = mk.vop("vector", "tensor_scalar_mul", row(MU), pa[0:1, :], 1.0 / D)
            mk.wait("vector", mk.ch["tensor"], mm_q)
            mk.vop("vector", "tensor_scalar_mul", row(VAR), pb[0:1, :], 1.0 / D)
            mk.vop("vector", "tensor_mul", row(MU2), row(MU), row(MU))
            v_var = mk.vop("vector", "tensor_sub", row(VAR), row(VAR), row(MU2))
            prev_ev["rows_done"] = v_var
            mk.wait("scalar", mk.ch["vector"], v_var)
            mk.vop("scalar", "activation", row(RSTD), row(VAR), AF.Ln,
                   bias=st["cvals"][0:1, 0:1])
            s_rstd = mk.vop("scalar", "activation", row(RSTD), row(RSTD), AF.Exp,
                            scale=-0.5)
            mk.wait("tensor", mk.ch["vector"], prev_ev.get("apply_done", 0))
            mk.wait("tensor", mk.ch["vector"], v_mu)
            mm_mu = mk.mm(pc[:, :], st["ones_row"][:, :], row(MU), inc=True)
            mk.wait("tensor", mk.ch["scalar"], s_rstd)
            mm_rs = mk.mm(pd[:, :], st["ones_row"][:, :], row(RSTD), inc=True)
            mk.wait("vector", mk.ch["tensor"], mm_rs)
            for b in range(NB):
                mk.vop("vector", "tensor_sub", buf[:, b * CH:(b + 1) * CH],
                       buf[:, b * CH:(b + 1) * CH], pc[:, :])
            for b in range(NB):
                v_xn = mk.vop("vector", "tensor_mul", st["xn"][:, b * CH:(b + 1) * CH],
                              buf[:, b * CH:(b + 1) * CH], pd[:, :])
            xfree[c % 2] = v_xn
            prev_ev["apply_done"] = v_xn

            mk.wait("vector", ost, last_store)
            mk.wait("tensor", mk.ch["vector"], v_xn)

            pool = (pe_, pf, pg, ph)

            def nb_():
                i = prev_ev.get("pool_i", 0)
                prev_ev["pool_i"] = i + 1
                bank = pool[i % 4]
                mk.wait("tensor", mk.ch["vector"],
                        prev_ev.get(f"bk{i % 4}", 0))
                return bank, i % 4

            for m in range(NB):
                qb_, bki = nb_()
                for b in range(NB):
                    mmq = mk.mm(qb_[:, :],
                                st["Wq"][:, b * D + m * 128: b * D + (m + 1) * 128],
                                st["xn"][:, b * CH:(b + 1) * CH],
                                start=(b == 0), stop=(b == NB - 1), inc=(b == NB - 1))
                mk.wait("vector", mk.ch["tensor"], mmq)
                mk.vop("vector", "reduce_max", st["colsc"][:, 1:2], qb_[:, :], AX,
                       apply_absolute_value=True)
                mk.vop("vector", "tensor_max", st["qabs"][:, :], st["qabs"][:, :],
                       st["colsc"][:, 1:2])
                mk.vop("vector", "scalar_tensor_tensor",
                       st["qo1"][:, m * CH:(m + 1) * CH], qb_[:, :], sc_q1,
                       st["esc_q1"][:, :], ALU.mult, ALU.mult)
                prev_ev[f"bk{bki}"] = mk.vop(
                    "vector", "scalar_tensor_tensor",
                    st["qo2"][:, m * CH:(m + 1) * CH], qb_[:, :],
                    sc_q2, st["esc_q2"][:, :], ALU.mult, ALU.mult)
            prev_ev["qev"] = prev_ev[f"bk{bki}"]
            mk.wait("gpsimd", mk.ch["vector"], prev_ev["qev"])
            mk.dma("gpsimd", T["qt1"][:, :].rearrange("(m p) s -> p m s", p=128)[
                :, :, c * CH:(c + 1) * CH],
                st["qo1"][:, :].rearrange("p (m s) -> p m s", m=NB), ost)
            mk.dma("gpsimd", T["qt2"][:, :].rearrange("(m p) s -> p m s", p=128)[
                :, :, c * CH:(c + 1) * CH],
                st["qo2"][:, :].rearrange("p (m s) -> p m s", m=NB), ost)

            for lh in range(2):
                lb_, bki = nb_()
                for b in range(NB):
                    mml = mk.mm(lb_[:, :],
                                st["Wld"][:, b * DL + lh * 128: b * DL + (lh + 1) * 128],
                                st["xn"][:, b * CH:(b + 1) * CH],
                                start=(b == 0), stop=(b == NB - 1), inc=(b == NB - 1))
                mk.wait("vector", mk.ch["tensor"], mml)
                prev_ev[f"bk{bki}"] = mk.vop(
                    "vector", "tensor_copy",
                    st["latc"][:, lh * CH:(lh + 1) * CH], lb_[:, :])
            prev_ev["lev"] = prev_ev[f"bk{bki}"]

            mk.wait("tensor", mk.ch["vector"], prev_ev["lev"])
            for m in range(NB):
                kb_, bki = nb_()
                for lh in range(2):
                    mmk = mk.mm(kb_[:, :],
                                st["Wuk"][:, lh * D + m * 128: lh * D + (m + 1) * 128],
                                st["latc"][:, lh * CH:(lh + 1) * CH],
                                start=(lh == 0), stop=(lh == 1), inc=(lh == 1))
                mk.wait("vector", mk.ch["tensor"], mmk)
                mk.vop("vector", "reduce_max", st["colsc"][:, 2:3], kb_[:, :], AX,
                       apply_absolute_value=True)
                mk.vop("vector", "tensor_max", st["kabs"][:, :], st["kabs"][:, :],
                       st["colsc"][:, 2:3])
                mk.vop("vector", "scalar_tensor_tensor",
                       st["ko1"][:, m * CH:(m + 1) * CH], kb_[:, :], sc_k1,
                       st["esc_k1"][:, :], ALU.mult, ALU.mult)
                prev_ev[f"bk{bki}"] = mk.vop(
                    "vector", "scalar_tensor_tensor",
                    st["ko2"][:, m * CH:(m + 1) * CH], kb_[:, :],
                    sc_k2, st["esc_k2"][:, :], ALU.mult, ALU.mult)
            prev_ev["kev"] = prev_ev[f"bk{bki}"]
            mk.wait("gpsimd", mk.ch["vector"], prev_ev["kev"])
            mk.dma("gpsimd", T["kt1"][:, :].rearrange("(m p) s -> p m s", p=128)[
                :, :, c * CH:(c + 1) * CH],
                st["ko1"][:, :].rearrange("p (m s) -> p m s", m=NB), ost)
            mk.dma("gpsimd", T["kt2"][:, :].rearrange("(m p) s -> p m s", p=128)[
                :, :, c * CH:(c + 1) * CH],
                st["ko2"][:, :].rearrange("p (m s) -> p m s", m=NB), ost)

            v_ms = mk.vop(
                "vector", "memset",
                st["vo"][:, :].rearrange("p (k c) -> p k c", c=65)[:, :, 64:65],
                1.0)
            for sT in range(4):
                for nh2 in range(2):
                    vb_, bki = nb_()
                    for lh in range(2):
                        mmv = mk.mm(vb_[:, :],
                                    st["latc"][:, lh * CH + sT * 128: lh * CH + (sT + 1) * 128],
                                    st["Wuv"][:, lh * D + nh2 * 512: lh * D + (nh2 + 1) * 512],
                                    start=(lh == 0), stop=(lh == 1), inc=(lh == 1))
                    mk.wait("vector", mk.ch["tensor"], mmv)
                    base = sT * NH * 65 + nh2 * 8 * 65
                    cnt = mk.vop(
                        "vector", "tensor_copy",
                        st["vo"][:, base: base + 8 * 65].rearrange(
                            "p (h c) -> p h c", c=65)[:, :, 0:64],
                        vb_[:, :].rearrange("p (h c) -> p h c", c=64))
                    prev_ev[f"bk{bki}"] = cnt
            prev_ev["vev"] = cnt
            mk.wait("gpsimd", mk.ch["vector"], prev_ev["vev"])
            mk.dma("gpsimd",
                   T["vat"][c * CH:(c + 1) * CH, :].rearrange("(t p) c -> p t c", p=128),
                   st["vo"][:, :].rearrange("p (t c) -> p t c", t=4), ost)
            last_store = ost.n

        self.sb("absb", (128, 2), BF16, s)
        v_ab = mk.vop("vector", "tensor_copy", st["absb"][:, 0:1], st["qabs"][:, :])
        v_ab = mk.vop("vector", "tensor_copy", st["absb"][:, 1:2], st["kabs"][:, :])
        mk.wait("tensor", mk.ch["vector"], v_ab)
        pabf = pa[:, :].bitcast(BF16)
        t_q = mk.trans(pabf[0:1, 0:128], st["absb"][:, 0:1], st["identb"][:, :])
        t_k = mk.trans(pabf[0:1, 128:256], st["absb"][:, 1:2], st["identb"][:, :])
        mk.wait("vector", mk.ch["tensor"], t_k)
        mk.vop("vector", "reduce_max", st["colsc"][0:1, 4:5], pabf[0:1, 0:128], AX)
        mk.vop("vector", "reduce_max", st["colsc"][0:1, 5:6], pabf[0:1, 128:256], AX)
        mk.vop("vector", "tensor_mul", st["colsc"][0:1, 6:7],
               st["colsc"][0:1, 4:5], st["colsc"][0:1, 5:6])
        v_c = mk.vop("vector", "tensor_scalar_mul", st["colsc"][0:1, 6:7],
                     st["colsc"][0:1, 6:7], -2.03)
        mk.wait("tensor", mk.ch["vector"], v_c)
        mm_c = mk.mm(pb[:, 0:1], st["ones_row"][:, :], st["colsc"][0:1, 6:7], inc=True)
        mk.wait("vector", mk.ch["tensor"], mm_c)
        v_nc = mk.vop("vector", "tensor_copy", st["negc"][:, :], pb[:, 0:1])
        if "tap_c" in self.taps:
            tc = self.nc.dram_tensor("tap_c", [128, 1], F32, kind="ExternalOutput")
            mk.wait("gpsimd", mk.ch["vector"], v_nc)
            mk.dma("gpsimd", tc[:, :], st["negc"][:, :], ost)
        mk.fence("p1end")
        self.sA.close()

    def p2(self):
        mk, st, T = self.mk, self.st, self.T
        self.p2s = ExitStack()
        s = self.p2s
        ktb1 = [self.sb(f"ktb1_{i}", (128, S), BF16, s) for i in range(2)]
        ktb2 = [self.sb(f"ktb2_{i}", (128, S), BF16, s) for i in range(2)]
        qtb1 = [self.sb(f"qtb1_{i}", (128, S), BF16, s) for i in range(2)]
        qtb2 = [self.sb(f"qtb2_{i}", (128, S), BF16, s) for i in range(2)]
        vab = [self.sb(f"vab_{i}", (128, NKT * 65), BF16, s) for i in range(2)]
        attn = [self.sb(f"attn_{i}", (128, NKT * 512), BF16, s) for i in range(2)]
        ustage = [self.sb(f"ust_{i}", (128, 512), BF16, s) for i in range(2)]
        ovraw = [self.sb(f"ovraw_{i}", (65, 512), F32, s) for i in range(2)]
        rrec = [self.sb(f"rrec_{i}", (1, 512), F32, s) for i in range(2)]
        rbc = self.sb("rbc", (64, 512), F32, s)
        sc = [self.psum("sc0", (128, GK * 512), F32, s),
              self.psum("sc1", (128, GK * 512), F32, s)]
        pv = self.psum("pv", (65, 512), F32, s)
        pbc = self.psum("pbc", (64, 512), F32, s)

        kvld = [mk.sem("p2kv0"), mk.sem("p2kv1")]
        ust = [mk.sem("p2ust0"), mk.sem("p2ust1")]
        ost = mk.sem("p1st")

        def load_head(hh):
            sem = kvld[hh % 2]
            rs = slice(hh * 64, (hh + 1) * 64)
            for buf, src_t in ((ktb1, "kt1"), (ktb2, "kt2"),
                               (qtb1, "qt1"), (qtb2, "qt2")):
                mk.dma("sync", buf[hh % 2][0:64, :], T[src_t][rs, :], sem)
                mk.dma("sync", buf[hh % 2][64:128, :], T[src_t][rs, :], sem)
            mk.dma("sync", vab[hh % 2][:, :].rearrange("p (t c) -> p t c", t=NKT),
                   T["vat"][:, hh * 65:(hh + 1) * 65].rearrange(
                       "(t p) c -> p t c", p=128), sem)
            return sem.n

        mk.wait("sync", ost, ost.n)
        kvtot = [0, 0]
        kvtot[0] = load_head(0)
        if self.np2 * 2 > 1:
            kvtot[1] = load_head(1)

        gidx = 0
        exp_cnt = {}
        pv_done = 0
        raw_done = 0
        ust_free = [0, 0]
        it = 0
        pair_tch_done = 0
        pend = None

        def emit_norm(p):
            pit, phh, pqc, pg_rec = p
            mk.wait("tensor", mk.ch["vector"], max(pg_rec, rbc_free[0]))
            mm_bc = mk.mm(pbc[:, :], st["ones_row"][0:1, 0:64],
                          rrec[pit % 2][:, :], inc=True)
            mk.wait("vector", mk.ch["tensor"], mm_bc)
            mk.wait("vector", ust[pit % 2], ust_free[pit % 2])
            rbc_free[0] = mk.vop("vector", "tensor_copy", rbc[:, :], pbc[:, :])
            nd = mk.vop("vector", "tensor_mul",
                        ustage[pit % 2][0:64, :],
                        ovraw[pit % 2][0:64, :], rbc[:, :])
            mk.wait("gpsimd", mk.ch["vector"], nd)
            mk.dma("gpsimd",
                   T["Ut"][phh * 64:(phh + 1) * 64, pqc * 512:(pqc + 1) * 512],
                   ustage[pit % 2][0:64, :], ust[pit % 2])
            ust_free[pit % 2] = ust[pit % 2].n
            nd_of[pit] = nd
            return nd

        its = [(hh * NCH + qc, hh, qc)
               for hh in range(self.np2 * 2) for qc in range(NCH)]
        PERHEAD = NCH
        pv_tch = {}
        grec_of = {}
        nd_of = {}
        rbc_free = [0]
        emitted_prefetch = set()

        def emit_scores(i, pv_prev=None, pv_tiles=None, pv_fin=None):
            nonlocal gidx
            pvpos = 6
            _, hh, qc = its[i]
            qcols = slice(qc * 512, (qc + 1) * 512)
            kb1, kb2 = ktb1[hh % 2], ktb2[hh % 2]
            qb1, qb2 = qtb1[hh % 2], qtb2[hh % 2]
            ab = attn[i % 2]
            for g in range(NG):
                buf = sc[gidx % 2]
                n = min(GK, NKT - g * GK)
                if gidx >= 2:
                    mk.wait("tensor", mk.ch["scalar"], exp_cnt[gidx - 2])
                for j in range(n):
                    t = g * GK + j
                    if t <= 4 * qc + 1:
                        kb, qb = kb1, qb1
                    else:
                        kb, qb = kb2, qb2
                    mk.mm(buf[0:64, j * 512:(j + 1) * 512],
                          kb[0:64, t * 128:t * 128 + 64],
                          qb[0:64, qcols],
                          start=True, stop=True, tp=(0, 0))
                    mm = mk.mm(buf[64:128, j * 512:(j + 1) * 512],
                               kb[64:128, t * 128 + 64:(t + 1) * 128],
                               qb[64:128, qcols],
                               start=True, stop=True, tp=(64, 64),
                               inc=(j == n - 1))
                has_corr = False
                for j in range(n):
                    t = g * GK + j
                    dj = t - 4 * qc
                    if 0 <= dj <= 3:
                        has_corr = True
                        mk.wait("vector", mk.ch["tensor"], mm)
                        pslab = buf[:, j * 512:(j + 1) * 512]
                        if dj == 0:
                            cl = mk.vop("vector", "tensor_mul",
                                        pslab[:, 0:128], pslab[:, 0:128],
                                        st["cor0"][:, :])
                        elif dj == 1:
                            cl = mk.vop("vector", "tensor_mul",
                                        pslab[:, 0:256], pslab[:, 0:256],
                                        st["cor1"][:, :])
                        elif dj == 2:
                            cl = mk.vop("vector", "tensor_mul",
                                        pslab[:, 256:512], pslab[:, 256:512],
                                        st["cor2"][:, :])
                        else:
                            cl = mk.vop("vector", "tensor_mul",
                                        pslab[:, 384:512], pslab[:, 384:512],
                                        st["cor3"][:, :])
                if g == 0 and i >= 2:
                    mk.wait("scalar", mk.ch["tensor"], pv_tch[i - 2])
                if has_corr:
                    mk.wait("scalar", mk.ch["vector"], cl)
                else:
                    mk.wait("scalar", mk.ch["tensor"], mm)
                exp_cnt[gidx] = mk.vop(
                    "scalar", "activation",
                    ab[:, g * GK * 512: g * GK * 512 + n * 512],
                    buf[:, 0:n * 512], AF.Exp, bias=st["negc"][:, 0:1])
                gidx += 1
                if pv_prev is not None:
                    ngpv = NG - 2
                    upto = min(NKT, 6 + ((g + 1) * (NKT - 6) + ngpv - 1) // ngpv)
                    if upto > pvpos:
                        pv_tiles(pv_prev, pvpos, upto)
                        pvpos = upto
                    if pvpos == NKT and pv_fin is not None:
                        pv_fin(pv_prev)
                        pv_fin = None
            if pv_prev is not None and pv_fin is not None:
                if pvpos < NKT:
                    pv_tiles(pv_prev, pvpos, NKT)
                pv_fin(pv_prev)

        def emit_pv_tiles(i, t0, t1):
            _, hh, qc = its[i]
            vb = vab[hh % 2]
            ab = attn[i % 2]
            if t0 == 0:
                mk.wait("tensor", mk.ch["vector"], raw_done)
            for t in range(t0, t1):
                g = t // GK
                mk.wait("tensor", mk.ch["scalar"], exp_cnt[i * NG + g])
                mmpv = mk.mm(pv[:, :],
                             vb[:, t * 65:(t + 1) * 65],
                             ab[:, t * 512:(t + 1) * 512],
                             start=(t == 0), stop=(t == NKT - 1),
                             inc=(t == NKT - 1))
            if t1 == NKT:
                pv_tch[i] = mmpv

        def emit_pv_finish(i):
            nonlocal raw_done
            mk.wait("vector", mk.ch["tensor"], pv_tch[i])
            raw_done = mk.vop("vector", "tensor_copy",
                              ovraw[i % 2][:, :], pv[:, :])
            grec_of[i] = mk.vop("vector", "reciprocal", rrec[i % 2][:, :],
                                ovraw[i % 2][64:65, :])

        for i in range(len(its) + 3):
            has_prev = 0 <= i - 1 < len(its)
            if i < len(its):
                _, hh, qc = its[i]
                if qc == 0:
                    mk.wait("tensor", kvld[hh % 2], kvtot[hh % 2])
                if has_prev:
                    emit_pv_tiles(i - 1, 0, 6)
                if 0 <= i - 3 < len(its):
                    j = i - 3
                    emit_norm((j, its[j][1], its[j][2], grec_of[j]))
                emit_scores(i, pv_prev=(i - 1) if has_prev else None,
                            pv_tiles=emit_pv_tiles, pv_fin=emit_pv_finish)
            else:
                if 0 <= i - 3 < len(its):
                    j = i - 3
                    emit_norm((j, its[j][1], its[j][2], grec_of[j]))
                if has_prev:
                    emit_pv_tiles(i - 1, 0, NKT)
                    emit_pv_finish(i - 1)
            if has_prev:
                j = i - 1
                if (j + 1) % PERHEAD == 0:
                    p = j // PERHEAD
                    np_ = p + 2
                    if np_ < self.np2 * 2:
                        mk.wait("sync", mk.ch["tensor"], pv_tch[j])
                        kvtot[np_ % 2] = load_head(np_)
        mk.fence("p2end", dma_sems=(ust[0], ust[1]))
        self.p2s.close()

    def ln_chunk(self, xsrc, xsq, ps_s, ps_q, ps_mu, ps_rs, guards=None,
                 ro=0, src_sem=None, src_vch=None):
        mk, st = self.mk, self.st
        guards = guards or {}
        MU, VAR, RSTD = 0, 1, 2

        def row(i):
            return st["rows"][0:1, (i + ro) * CH:(i + ro + 1) * CH]

        vch_need = mk.ch["vector"].n if src_vch is None else src_vch
        mk.wait("scalar", mk.ch["vector"], vch_need)
        mk.wait("scalar", mk.ch["tensor"], guards.get("xb_rd", 0))
        if src_sem is not None:
            for sem_, cnt_ in src_sem:
                mk.wait("scalar", sem_, cnt_)
                mk.wait("tensor", sem_, cnt_)
        H = NB // 2
        s_sq = mk.vop("scalar", "activation", xsq[:, :], xsrc[:, 0:H * CH],
                      AF.Square)
        mk.wait("tensor", mk.ch["vector"], guards.get("rows_done", 0))
        mk.wait("tensor", mk.ch["vector"], vch_need)
        for b in range(NB):
            mm_s = mk.mm(ps_s[0:1, :], st["ones_col"][:, :],
                         xsrc[:, b * CH:(b + 1) * CH], start=(b == 0),
                         stop=(b == NB - 1), inc=(b == NB - 1))
        mk.wait("tensor", mk.ch["scalar"], s_sq)
        for b in range(H):
            mm_q = mk.mm(ps_q[0:1, :], st["onesb"][:, :],
                         xsq[:, b * CH:(b + 1) * CH], start=(b == 0),
                         stop=False, inc=True)
        mk.wait("scalar", mk.ch["tensor"], mm_q)
        s_sq = mk.vop("scalar", "activation", xsq[:, :], xsrc[:, H * CH:NB * CH],
                      AF.Square)
        mk.wait("tensor", mk.ch["scalar"], s_sq)
        for b in range(H):
            mm_q = mk.mm(ps_q[0:1, :], st["onesb"][:, :],
                         xsq[:, b * CH:(b + 1) * CH], start=False,
                         stop=(b == H - 1), inc=(b == H - 1))
        mk.wait("vector", mk.ch["tensor"], mm_s)
        v_mu = mk.vop("vector", "tensor_scalar_mul", row(MU), ps_s[0:1, :], 1.0 / D)
        mk.wait("vector", mk.ch["tensor"], mm_q)
        mk.vop("vector", "tensor_scalar_mul", row(VAR), ps_q[0:1, :], 1.0 / D)
        mk.vop("vector", "tensor_mul", row(RSTD), row(MU), row(MU))
        v_var = mk.vop("vector", "tensor_sub", row(VAR), row(VAR), row(RSTD))
        guards["rows_done"] = v_var
        mk.wait("scalar", mk.ch["vector"], v_var)
        mk.vop("scalar", "activation", row(RSTD), row(VAR), AF.Ln,
               bias=st["cvals"][0:1, 0:1])
        s_rstd = mk.vop("scalar", "activation", row(RSTD), row(RSTD), AF.Exp,
                        scale=-0.5)
        mk.wait("tensor", mk.ch["vector"], guards.get("apply_done", 0))
        mk.wait("tensor", mk.ch["vector"], v_mu)
        mm_mu = mk.mm(ps_mu[:, :], st["ones_row"][:, :], row(MU), inc=True)
        mk.wait("tensor", mk.ch["scalar"], s_rstd)
        mm_rs = mk.mm(ps_rs[:, :], st["ones_row"][:, :], row(RSTD), inc=True)
        guards["xb_rd"] = mm_q
        return mm_mu, mm_rs, guards

    def load_ffn_weights(self, sF):
        mk, st, T = self.mk, self.st, self.T
        self.sb("Wc1", (128, NB * D), BF16, sF)
        self.sb("Wc2", (128, NB * 512), BF16, sF)
        self.sb("Wc3", (128, 4 * 256), BF16, sF)
        self.sb("Wc4", (128, 2 * 1), BF16, sF)
        self.sb("Wg1", (128, NB * 512), BF16, sF)
        self.sb("Wg2", (128, 4 * 256), BF16, sF)
        self.sb("Wg3", (128, 2 * 1), BF16, sF)
        self.sb("cnb1", (128, 8), F32, sF)
        self.sb("cnb2", (128, 4), F32, sF)
        self.sb("cnb3", (128, 2), F32, sF)
        self.sb("gcb1", (128, 4), F32, sF)
        self.sb("gcb2", (128, 2), F32, sF)
        self.sb("arow", (1, 8), F32, sF)
        self.sb("s4full", (1, S2), F32, sF)
        self.sb("crow", (1, S2), F32, sF)
        Builder._lc_n += 1
        ld = mk.sem(f"ffw{Builder._lc_n}")
        with ExitStack() as s:
            self.sb("ffstage", (128, 8448), F32, s)
            self.load_cast(st["Wc1"], T["cn_w1"], D, D, queue="sync", stage="ffstage")
            self.load_cast(st["Wc2"], T["cn_w2"], D, 512, queue="sync", stage="ffstage")
            self.load_cast(st["Wc3"], T["cn_w3"], 512, 256, queue="sync", stage="ffstage")
            self.load_cast(st["Wc4"], T["cn_w4"], 256, 1, queue="sync", stage="ffstage")
            self.load_cast(st["Wg1"], T["gc_w1"], D, 512, queue="sync", stage="ffstage")
            self.load_cast(st["Wg2"], T["gc_w2"], 512, 256, queue="sync", stage="ffstage")
            vlast = self.load_cast(st["Wg3"], T["gc_w3"], 256, 1, queue="sync",
                                   stage="ffstage")
            for dst, srt, nb in (("cnb1", "cn_b1", 8), ("cnb2", "cn_b2", 4),
                                 ("cnb3", "cn_b3", 2), ("gcb1", "gc_b1", 4),
                                 ("gcb2", "gc_b2", 2)):
                mk.dma("sync", st[dst][:, :].rearrange("p (m o) -> p m o", o=1),
                       T[srt][:, :].rearrange("(m p) o -> p m o", p=128), ld,
                       slow_ok=True)
            mk.dma("sync", st["arow"][0:1, 0:1], T["a_range"][:, :], ld)
            mk.dma("sync", st["arow"][0:1, 1:2], T["cn_b4"][:, :], ld)
            mk.dma("sync", st["arow"][0:1, 2:3], T["gc_b3"][:, :], ld)
            mk.wait("sync", mk.ch["vector"], vlast)
            mk.wait("vector", ld, ld.n)
        return ld.n

    def p3(self):
        mk, st, T = self.mk, self.st, self.T
        self.scratch("rowst", (2, S), F32)
        self.scratch("xn2t", (D, S))
        self.scratch("lnt", (D, S2), F32)

        self.sD = ExitStack()
        self.sb("Wo3", (128, NB * D), BF16, self.sD)
        ores = self.sb("ores", (128, NB * S2), F32, self.sD)
        with ExitStack() as s0:
            self.sb("p3stage", (128, 8448), F32, s0)
            vw = self.load_cast(st["Wo3"], T["W_o"], D, D, queue="sync",
                                stage="p3stage")
            mk.wait("sync", mk.ch["vector"], vw)
        self.sF = ExitStack()
        self.load_ffn_weights(self.sF)
        mk.vop("vector", "memset", st["gcs"][:, :], 0.0)

        with ExitStack() as s:
            uc = [self.sb(f"p3u{i}", (128, NB * CH), BF16, s) for i in range(2)]
            xc = [self.sb(f"p3x{i}", (128, NB * CH), F32, s) for i in range(2)]
            xsq = self.sb("p3xsq", (128, NB * CH // 2), BF16, s)
            pwo = [self.psum(f"p3wo{i}", (128, 512), F32, s) for i in range(2)]
            pa = self.psum("p3pa", (128, 512), F32, s)
            pb = self.psum("p3pb", (128, 512), F32, s)
            pc = self.psum("p3pc", (128, 512), F32, s)
            pd = self.psum("p3pd", (128, 512), F32, s)
            uld = [mk.sem("p3u0"), mk.sem("p3u1")]
            xld = [mk.sem("p3xl0"), mk.sem("p3xl1")]
            ost = [mk.sem("p3st0"), mk.sem("p3st1")]
            lst = [mk.sem("p3ls0"), mk.sem("p3ls1")]
            ust = [mk.sem("p2ust0"), mk.sem("p2ust1")]

            def loadu(c):
                mk.dma("scalar", uc[c % 2][:, :].rearrange("p (m s) -> p m s", m=NB),
                       T["Ut"][:, c * CH:(c + 1) * CH].rearrange(
                           "(m p) s -> p m s", p=128), uld[c % 2])

            def loadx(c):
                mk.dma("scalar", xc[c % 2][:, :].rearrange("p (b s) -> p b s", b=NB),
                       T["x_t"][:, c * CH:(c + 1) * CH].rearrange(
                           "(b p) s -> p b s", p=128), xld[c % 2])

            mk.wait("scalar", ust[0], ust[0].n)
            mk.wait("scalar", ust[1], ust[1].n)
            mk.wait("scalar", mk.ch["vector"], mk.ch["vector"].n)
            loadu(0)
            loadx(0)
            guards = {}
            xfree = [0, 0]
            ost_tot = [0, 0]
            lst_tot = [0, 0]
            for c in range(NCH):
                cols = slice(c * CH, (c + 1) * CH)
                if c + 1 < NCH:
                    mk.wait("scalar", mk.ch["tensor"], guards.get("uc_rd", 0))
                    loadu(c + 1)
                    np_ = (c + 1) % 2
                    mk.wait("scalar", mk.ch["vector"], xfree[np_])
                    mk.wait("scalar", mk.ch["gpsimd"], guards.get("g_ap", 0))
                    mk.wait("scalar", ost[np_], ost_tot[np_])
                    mk.wait("scalar", lst[np_], lst_tot[np_])
                    loadx(c + 1)
                xcb = xc[c % 2]
                mk.wait("tensor", uld[c % 2], 16 * (c // 2 + 1))
                mk.wait("vector", xld[c % 2], 16 * (c // 2 + 1))
                ub = uc[c % 2]
                for db in range(NB):
                    p = pwo[db % 2]
                    mk.wait("tensor", mk.ch["vector"], guards.get(f"wo{db % 2}", 0))
                    for kb in range(NB):
                        mmw = mk.mm(p[:, :],
                                    st["Wo3"][:, kb * D + db * 128: kb * D + (db + 1) * 128],
                                    ub[:, kb * CH:(kb + 1) * CH],
                                    start=(kb == 0), stop=(kb == NB - 1),
                                    inc=(kb == NB - 1))
                    mk.wait("vector", mk.ch["tensor"], mmw)
                    v_add = mk.vop("vector", "tensor_add",
                                   xcb[:, db * CH:(db + 1) * CH],
                                   xcb[:, db * CH:(db + 1) * CH], p[:, :])
                    guards[f"wo{db % 2}"] = v_add
                guards["uc_rd"] = mmw
                mk.wait("gpsimd", mk.ch["vector"], v_add)
                mk.dma("gpsimd", T["xres"][:, cols].rearrange("(b p) s -> p b s", p=128),
                       xcb[:, :].rearrange("p (b s) -> p b s", b=NB), ost[c % 2])
                ost_tot[c % 2] = ost[c % 2].n
                mk.wait("scalar", mk.ch["gpsimd"], guards.get("g_ap", 0))
                mm_mu, mm_rs, guards = self.ln_chunk(
                    xcb, xsq, pa, pb, pc, pd, guards,
                    src_sem=[(xld[c % 2], 16 * (c // 2 + 1))], src_vch=v_add)
                dst_is_sbuf = c < NCH // 2
                if not dst_is_sbuf:
                    mk.wait("vector", ost[c % 2], ost_tot[c % 2])
                    mk.wait("gpsimd", ost[c % 2], ost_tot[c % 2])
                xqf = xsq[:, :].bitcast(F32)
                mk.wait("vector", mk.ch["tensor"], mm_rs)
                v_pc = mk.vop("vector", "tensor_copy", xqf[:, 0:512], pc[:, :])
                v_pd = mk.vop("vector", "tensor_copy", xqf[:, 512:1024], pd[:, :])

                def dstb(b):
                    return (ores[:, b * S2 + c * CH: b * S2 + (c + 1) * CH]
                            if dst_is_sbuf else xcb[:, b * CH:(b + 1) * CH])

                mk.wait("gpsimd", mk.ch["vector"], v_pd)
                for b in range(4, NB):
                    mk.vop("gpsimd", "tensor_sub", dstb(b),
                           xcb[:, b * CH:(b + 1) * CH], xqf[:, 0:512])
                for b in range(4, NB):
                    g_ap = mk.vop("gpsimd", "tensor_mul", dstb(b), dstb(b),
                                  xqf[:, 512:1024])
                guards["g_ap"] = g_ap
                for b in range(4):
                    mk.vop("vector", "tensor_sub", dstb(b),
                           xcb[:, b * CH:(b + 1) * CH], pc[:, :])
                for b in range(4):
                    v_ap = mk.vop("vector", "tensor_mul", dstb(b), dstb(b),
                                  pd[:, :])
                guards["apply_done"] = v_ap
                for b in range(4):
                    mk.vop("vector", "reduce_sum", st["colsc"][:, 3:4], dstb(b), AX)
                    v_gc = mk.vop("vector", "tensor_add", st["gcs"][:, b:b + 1],
                                  st["gcs"][:, b:b + 1], st["colsc"][:, 3:4])
                mk.wait("vector", mk.ch["gpsimd"], g_ap)
                for b in range(4, NB):
                    mk.vop("vector", "reduce_sum", st["colsc"][:, 3:4], dstb(b), AX)
                    v_gc = mk.vop("vector", "tensor_add", st["gcs"][:, b:b + 1],
                                  st["gcs"][:, b:b + 1], st["colsc"][:, 3:4])
                xfree[c % 2] = max(v_ap, v_gc)
                if not dst_is_sbuf:
                    mk.wait("gpsimd", mk.ch["vector"], v_gc)
                    mk.dma("gpsimd", T["lnt"][:, (c - NCH // 2) * CH:
                                              (c - NCH // 2 + 1) * CH].rearrange(
                        "(b p) s -> p b s", p=128),
                        xcb[:, :].rearrange("p (b s) -> p b s", b=NB), lst[c % 2])
                    lst_tot[c % 2] = lst[c % 2].n
            mk.wait("gpsimd", ost[0], ost[0].n)
            mk.wait("gpsimd", ost[1], ost[1].n)
            mk.wait("gpsimd", lst[0], lst[0].n)
            mk.wait("gpsimd", lst[1], lst[1].n)
            mk.fence("p3end", dma_sems=(ost[0], ost[1], lst[0], lst[1]))

    def gfnet(self):
        mk, st = self.mk, self.st
        v_gc = mk.vop("vector", "tensor_scalar_mul", st["gcs"][:, :],
                      st["gcs"][:, :], 1.0 / S)
        with ExitStack() as s:
            pg = self.psum("gf_p", (128, 512), F32, s)
            h1 = self.sb("gf_h1", (128, 4), BF16, s)
            h2 = self.sb("gf_h2", (128, 2), BF16, s)
            gb = self.sb("gf_gb", (128, 8), BF16, s)
            v_cast = mk.vop("vector", "tensor_copy", gb[:, :], st["gcs"][:, :])
            mk.wait("tensor", mk.ch["vector"], v_cast)
            sc_prev = mk.ch["scalar"].n
            for m in range(4):
                mk.wait("tensor", mk.ch["scalar"], sc_prev)
                for b in range(NB):
                    mm1 = mk.mm(pg[:, m:m + 1],
                                st["Wg1"][:, b * 512 + m * 128: b * 512 + (m + 1) * 128],
                                gb[:, b:b + 1], start=(b == 0), stop=(b == NB - 1),
                                inc=(b == NB - 1))
                mk.wait("scalar", mk.ch["tensor"], mm1)
                sc_prev = mk.vop("scalar", "activation", h1[:, m:m + 1],
                                 pg[:, m:m + 1], AF.Gelu, bias=st["gcb1"][:, m:m + 1])
            mk.wait("tensor", mk.ch["scalar"], sc_prev)
            for m in range(2):
                for b in range(4):
                    mm2 = mk.mm(pg[:, 8 + m:9 + m],
                                st["Wg2"][:, b * 256 + m * 128: b * 256 + (m + 1) * 128],
                                h1[:, b:b + 1], start=(b == 0), stop=(b == 3),
                                inc=(b == 3))
                mk.wait("scalar", mk.ch["tensor"], mm2)
                sc_prev = mk.vop("scalar", "activation", h2[:, m:m + 1],
                                 pg[:, 8 + m:9 + m], AF.Gelu,
                                 bias=st["gcb2"][:, m:m + 1])
            mk.wait("tensor", mk.ch["scalar"], sc_prev)
            for b in range(2):
                mm3 = mk.mm(pg[0:1, 16:17], st["Wg3"][:, b:b + 1], h2[:, b:b + 1],
                            start=(b == 0), stop=(b == 1), inc=(b == 1))
            mk.wait("vector", mk.ch["tensor"], mm3)
            v_b4h = mk.vop("vector", "tensor_scalar_mul", st["arow"][0:1, 5:6],
                           st["arow"][0:1, 2:3], 0.5)
            mk.wait("scalar", mk.ch["vector"], v_b4h)
            s_t = mk.vop("scalar", "activation", st["arow"][0:1, 6:7],
                         pg[0:1, 16:17], AF.Tanh, scale=0.5,
                         bias=st["arow"][0:1, 5:6])
            mk.wait("vector", mk.ch["scalar"], s_t)
            mk.vop("vector", "tensor_scalar_mul", st["arow"][0:1, 3:4],
                   st["arow"][0:1, 0:1], 0.3)
            mk.vop("vector", "tensor_scalar_mul", st["arow"][0:1, 4:5],
                   st["arow"][0:1, 0:1], 0.7)
            mk.vop("vector", "tensor_mul", st["arow"][0:1, 4:5],
                   st["arow"][0:1, 4:5], st["arow"][0:1, 6:7])
            mk.vop("vector", "tensor_scalar_add", st["arow"][0:1, 4:5],
                   st["arow"][0:1, 4:5], 0.1)
            v_k0 = mk.vop("vector", "tensor_scalar_mul", st["arow"][0:1, 5:6],
                          st["arow"][0:1, 1:2], 0.5)
        return v_k0

    _ffn_n = 0

    def ffn_half(self, ores):
        mk, st = self.mk, self.st
        Builder._ffn_n += 1
        mk.fence(f"ffnh{Builder._ffn_n}")
        NC2 = NCH // 2
        with ExitStack() as s:
            ob = [self.sb(f"f_ob{i}", (128, NB * CH), BF16, s) for i in range(2)]
            h1b = self.sb("f_h1", (128, NB * CH), BF16, s)
            h2b = self.sb("f_h2", (128, 4 * CH), BF16, s)
            h3b = self.sb("f_h3", (128, 2 * CH), BF16, s)
            tbf = self.sb("f_t", (128, CH), BF16, s)
            mbf = self.sb("f_m", (128, CH), BF16, s)
            tb2 = self.sb("f_t2", (128, CH), BF16, s)
            mb2 = self.sb("f_m2", (128, CH), BF16, s)
            cpos = [self.sb(f"f_c{i}", (128, CH), F32, s) for i in range(2)]
            ph1 = [self.psum(f"f_p1{i}", (128, 512), F32, s) for i in range(2)]
            ph2 = [self.psum(f"f_p2{i}", (128, 512), F32, s) for i in range(2)]
            ps4 = self.psum("f_p4", (128, 512), F32, s)
            pcb = [self.psum(f"f_pc{i}", (128, 512), F32, s) for i in range(2)]
            guards = {}

            def emit_bcast(c):
                cols = slice(c * CH, (c + 1) * CH)
                mk.wait("tensor", mk.ch["vector"],
                        max(guards.get(f"tw_v{c % 2}", 0), guards.get("crow_v", 0)))
                mk.wait("tensor", mk.ch["gpsimd"], guards.get(f"tw_g{c % 2}", 0))
                mm_c = mk.mm(pcb[c % 2][:, :], st["ones_row"][:, :],
                             st["crow"][0:1, cols], inc=True)
                mk.wait("scalar", mk.ch["tensor"], mm_c)
                s_cb = mk.vop("scalar", "copy", cpos[c % 2][:, :], pcb[c % 2][:, :])
                guards[f"cb{c % 2}"] = s_cb

            def emit_twist(c):
                def oc(b):
                    return ores[:, b * S2 + c * CH: b * S2 + (c + 1) * CH]
                cp = cpos[c % 2][:, :]
                mk.wait("vector", mk.ch["scalar"], guards[f"cb{c % 2}"])
                mk.wait("gpsimd", mk.ch["scalar"], guards[f"cb{c % 2}"])
                for eng, tb, mb, pairs in (
                        ("vector", tbf, mbf, ((0, 4, "add", "subtract"),
                                              (1, 5, "add", "subtract"))),
                        ("gpsimd", tb2, mb2, ((2, 6, "subtract", "add"),
                                              (3, 7, "subtract", "add")))):
                    for (da, db2, sa, sb2) in pairs:
                        mk.vop(eng, "tensor_copy", tb[:, :], oc(da))
                        mk.vop(eng, "tensor_mul", mb[:, :], cp, oc(db2))
                        mk.vop(eng, "tensor_tensor", oc(da), oc(da),
                               mb[:, :], getattr(ALU, sa))
                        mk.vop(eng, "tensor_mul", mb[:, :], cp, tb[:, :])
                        v_tw = mk.vop(eng, "tensor_tensor", oc(db2), oc(db2),
                                      mb[:, :], getattr(ALU, sb2))
                    guards[f"tw_{eng[0]}{c % 2}"] = v_tw
                guards[f"tw_done{c}"] = (guards[f"tw_v{c % 2}"],
                                         guards[f"tw_g{c % 2}"])

            def emit_cast(c, twisted):
                if twisted:
                    vtw, gtw = guards[f"tw_done{c}"]
                    mk.wait("scalar", mk.ch["vector"], vtw)
                    mk.wait("scalar", mk.ch["gpsimd"], gtw)
                mk.wait("scalar", mk.ch["tensor"], guards.get(f"obrd{c % 2}", 0))
                for b in range(NB):
                    s_ob = mk.vop("scalar", "copy", ob[c % 2][:, b * CH:(b + 1) * CH],
                                  ores[:, b * S2 + c * CH: b * S2 + (c + 1) * CH])
                guards[f"cast{c % 2}"] = s_ob

            def emit_passA(c, cyc, twisted, nxt_cast=None):
                cols = slice(c * CH, (c + 1) * CH)
                obc = ob[c % 2]
                mk.wait("tensor", mk.ch["scalar"], guards[f"cast{c % 2}"])
                for m in range(NB):
                    p = ph1[m % 2]
                    mk.wait("tensor", mk.ch["scalar"], guards.get(f"h1_{m % 2}", 0))
                    for b in range(NB):
                        mm1 = mk.mm(p[:, :],
                                    st["Wc1"][:, b * D + m * 128: b * D + (m + 1) * 128],
                                    obc[:, b * CH:(b + 1) * CH],
                                    start=(b == 0), stop=(b == NB - 1),
                                    inc=(b == NB - 1))
                    mk.wait("scalar", mk.ch["tensor"], mm1)
                    guards[f"h1_{m % 2}"] = mk.vop(
                        "scalar", "activation", h1b[:, m * CH:(m + 1) * CH],
                        p[:, :], AF.Gelu, bias=st["cnb1"][:, m:m + 1])
                guards[f"obrd{c % 2}"] = mm1
                mk.wait("tensor", mk.ch["scalar"], guards[f"h1_{(NB - 1) % 2}"])
                for m in range(4):
                    p = ph2[m % 2]
                    mk.wait("tensor", mk.ch["scalar"], guards.get(f"h2_{m % 2}", 0))
                    for b in range(NB):
                        mm2 = mk.mm(p[:, :],
                                    st["Wc2"][:, b * 512 + m * 128: b * 512 + (m + 1) * 128],
                                    h1b[:, b * CH:(b + 1) * CH],
                                    start=(b == 0), stop=(b == NB - 1),
                                    inc=(b == NB - 1))
                    mk.wait("scalar", mk.ch["tensor"], mm2)
                    guards[f"h2_{m % 2}"] = mk.vop(
                        "scalar", "activation", h2b[:, m * CH:(m + 1) * CH],
                        p[:, :], AF.Gelu, bias=st["cnb2"][:, m:m + 1])
                if nxt_cast is not None:
                    emit_cast(*nxt_cast)
                mk.wait("tensor", mk.ch["scalar"], guards[f"h2_{3 % 2}"])
                for m in range(2):
                    p = ph1[m % 2]
                    mk.wait("tensor", mk.ch["scalar"], guards.get(f"h1_{m % 2}", 0))
                    for b in range(4):
                        mm3 = mk.mm(p[:, :],
                                    st["Wc3"][:, b * 256 + m * 128: b * 256 + (m + 1) * 128],
                                    h2b[:, b * CH:(b + 1) * CH],
                                    start=(b == 0), stop=(b == 3), inc=(b == 3))
                    mk.wait("scalar", mk.ch["tensor"], mm3)
                    guards[f"h1_{m % 2}"] = mk.vop(
                        "scalar", "activation", h3b[:, m * CH:(m + 1) * CH],
                        p[:, :], AF.Gelu, bias=st["cnb3"][:, m:m + 1])
                mk.wait("tensor", mk.ch["scalar"], guards[f"h1_{1 % 2}"])
                mk.wait("tensor", mk.ch["scalar"], guards.get("s4", 0))
                for b in range(2):
                    mm4 = mk.mm(ps4[0:1, :], st["Wc4"][:, b:b + 1],
                                h3b[:, b * CH:(b + 1) * CH],
                                start=(b == 0), stop=(b == 1), inc=(b == 1))
                mk.wait("scalar", mk.ch["tensor"], mm4)
                guards["s4"] = mk.vop("scalar", "copy",
                                      st["s4full"][0:1, cols], ps4[0:1, :])

            emit_cast(0, False)
            for cyc in range(3):
                for c in range(NC2):
                    if c + 1 < NC2:
                        nxt_cast = (c + 1, cyc > 0)
                    elif cyc + 1 < 3:
                        nxt_cast = (0, True)
                    else:
                        nxt_cast = None
                    emit_passA(c, cyc, twisted=(cyc > 0), nxt_cast=nxt_cast)
                    cols = slice(c * CH, (c + 1) * CH)
                    s_t4 = mk.vop("scalar", "activation",
                                  st["s4full"][0:1, cols],
                                  st["s4full"][0:1, cols], AF.Tanh, scale=0.5,
                                  bias=st["arow"][0:1, 5:6])
                    mk.wait("vector", mk.ch["scalar"], s_t4)
                    guards["crow_v"] = mk.vop(
                        "vector", "tensor_scalar", st["crow"][0:1, cols],
                        st["s4full"][0:1, cols], st["arow"][0:1, 3:4],
                        st["arow"][0:1, 4:5], ALU.mult, ALU.add)
                    emit_bcast(c)
                    emit_twist(c)

    def residual_half(self, ores, half, dst_name, wait_sems=()):
        mk, st, T = self.mk, self.st, self.T
        NC2 = NCH // 2
        dst = self.y if dst_name == "y" else T[dst_name]
        mk.fence(f"rh{half}{dst_name}")
        with ExitStack() as s:
            xc = [self.sb(f"rh{half}{dst_name}{i}", (128, NB * CH), F32, s)
                  for i in range(2)]
            xld = [mk.sem("rhxa"), mk.sem("rhxb")]
            sst = [mk.sem("rhsa"), mk.sem("rhsb")]
            ss2 = [mk.sem("rh2a"), mk.sem("rh2b")]

            ld_tot = [0, 0]

            def loadx(c):
                src3 = T["xres"][:, (half * NC2 + c) * CH:
                                 (half * NC2 + c + 1) * CH].rearrange(
                    "(b p) s -> p b s", p=128)
                dst3 = xc[c % 2][:, :].rearrange("p (b s) -> p b s", b=NB)
                mk.dma("sync", dst3[:, 0:NB // 2, :], src3[:, 0:NB // 2, :],
                       xld[c % 2])
                mk.dma("scalar", dst3[:, NB // 2:NB, :], src3[:, NB // 2:NB, :],
                       xld[c % 2])
                ld_tot[c % 2] = xld[c % 2].n

            for sem in wait_sems:
                mk.wait("sync", sem, sem.n)
                mk.wait("scalar", sem, sem.n)
            loadx(0)
            freed = [0, 0]
            for c in range(NC2):
                gcol = (half * NC2 + c) * CH
                if c + 1 < NC2:
                    mk.wait("sync", mk.ch["vector"], freed[(c + 1) % 2])
                    mk.wait("scalar", mk.ch["vector"], freed[(c + 1) % 2])
                    mk.wait("sync", sst[(c + 1) % 2], sst[(c + 1) % 2].n)
                    mk.wait("scalar", sst[(c + 1) % 2], sst[(c + 1) % 2].n)
                    mk.wait("sync", ss2[(c + 1) % 2], ss2[(c + 1) % 2].n)
                    mk.wait("scalar", ss2[(c + 1) % 2], ss2[(c + 1) % 2].n)
                    loadx(c + 1)
                mk.wait("vector", xld[c % 2], ld_tot[c % 2])
                for b in range(NB):
                    v_y = mk.vop("vector", "tensor_add",
                                 xc[c % 2][:, b * CH:(b + 1) * CH],
                                 xc[c % 2][:, b * CH:(b + 1) * CH],
                                 ores[:, b * S2 + c * CH: b * S2 + (c + 1) * CH])
                freed[c % 2] = v_y
                mk.wait("gpsimd", mk.ch["vector"], v_y)
                d3 = dst[:, gcol:gcol + CH].rearrange("(b p) s -> p b s", p=128)
                s3 = xc[c % 2][:, :].rearrange("p (b s) -> p b s", b=NB)
                if dst_name == "y":
                    mk.dma("gpsimd", d3, s3, sst[c % 2])
                else:
                    mk.dma("gpsimd", d3[0:64, :, :], s3[0:64, :, :], sst[c % 2])
                    mk.wait("scalar", mk.ch["vector"], v_y)
                    mk.dma("scalar", d3[64:128, :, :], s3[64:128, :, :],
                           ss2[c % 2])
            for e in ENGS:
                mk.wait(e, sst[0], sst[0].n)
                mk.wait(e, sst[1], sst[1].n)
                mk.wait(e, ss2[0], ss2[0].n)
                mk.wait(e, ss2[1], ss2[1].n)

    def load_lnt(self, ores, lnt_sems):
        mk, st, T = self.mk, self.st, self.T
        for lnt_sem in lnt_sems:
            mk.wait("sync", lnt_sem, lnt_sem.n)
        mk.wait("sync", mk.ch["vector"], mk.ch["vector"].n)
        mk.wait("sync", mk.ch["tensor"], mk.ch["tensor"].n)
        mk.wait("sync", mk.ch["scalar"], mk.ch["scalar"].n)
        mk.wait("sync", mk.ch["gpsimd"], mk.ch["gpsimd"].n)
        Builder._lc_n += 1
        sem = mk.sem(f"lnt{Builder._lc_n}")
        for c in range(NCH // 2):
            mk.dma("sync", ores[:, :].rearrange("p (b s) -> p b s", b=NB)[
                :, :, c * CH:(c + 1) * CH],
                T["lnt"][:, c * CH:(c + 1) * CH].rearrange(
                    "(b p) s -> p b s", p=128), sem)
        for e in ("tensor", "vector", "scalar", "gpsimd"):
            mk.wait(e, sem, sem.n)
        return sem

    def p5(self):
        mk, st, T = self.mk, self.st, self.T
        mk.fence("p5pre")
        self.sD.close()

        self.sE = ExitStack()
        q2b = self.sb("q2b", (128, S), BF16, self.sE)
        k2b = self.sb("k2b", (128, S), BF16, self.sE)
        gcols = self.sb("gcols", (128, 8 * 2 + 4), F32, self.sE)
        ld5 = mk.sem("p5ld")
        mk.dma("sync", st["gcols"][:, 0:8].rearrange("p (m o) -> p m o", o=1),
               T["b_ln_g"][:, :].rearrange("(m p) o -> p m o", p=128), ld5,
               slow_ok=True)
        mk.dma("sync", st["gcols"][:, 8:16].rearrange("p (m o) -> p m o", o=1),
               T["b_ln_b"][:, :].rearrange("(m p) o -> p m o", p=128), ld5,
               slow_ok=True)
        mk.dma("sync", st["gcols"][:, 16:17], T["gamma_q"][:, :], ld5)
        mk.dma("sync", st["gcols"][:, 17:18], T["beta_q"][:, :], ld5)
        mk.dma("sync", st["gcols"][:, 18:19], T["gamma_k"][:, :], ld5)
        mk.dma("sync", st["gcols"][:, 19:20], T["beta_k"][:, :], ld5)
        gcold = ld5.n

        with ExitStack() as s:
            zt = self.sb("p5z", (128, S), F32, s)
            stg = self.sb("p5stage", (128, 4224), F32, s)
            wsl = self.sb("p5w", (128, NB * 2048), BF16, s)
            wz = self.sb("p5wz", (128, NB * 128), BF16, s)
            ubc = self.sb("p5ubc", (128, 17), F32, s)
            xc2 = self.sb("p5x2", (128, NB * CH), F32, s)
            xnp = self.sb("p5xnp", (128, NB * CH), F32, s)
            xsq = self.sb("p5xsq", (128, NB * CH // 2), BF16, s)
            xn2 = self.sb("p5xn2", (128, NB * CH), BF16, s)
            ubuf = self.sb("p5ubuf", (128, 16 * CH), BF16, s)
            pa = self.psum("p5pa", (128, 512), F32, s)
            pb = self.psum("p5pb", (128, 512), F32, s)
            pc = self.psum("p5pc", (128, 512), F32, s)
            pd = self.psum("p5pd", (128, 512), F32, s)
            pu = [self.psum(f"p5pu{i}", (128, 512), F32, s) for i in range(2)]

            def wload(c0, c1, dstbuf, dst_off, dst_w):
                n = c1 - c0
                mk.wait("sync", mk.ch["vector"], mk.ch["vector"].n)
                sem = mk.sem(f"p5wl{c0}")
                t = mk.dma("sync", stg[:, 0:NB * n].rearrange("p (b s) -> p b s", b=NB),
                           T["W_in"][:, c0:c1].rearrange("(b p) s -> p b s", p=128),
                           sem)
                mk.wait("vector", sem, t)
                dv = dstbuf[:, :].rearrange("p (b s) -> p b s", b=NB, s=dst_w)
                return mk.vop("vector", "tensor_copy",
                              dv[:, :, dst_off:dst_off + n],
                              stg[:, 0:NB * n].rearrange("p (b s) -> p b s", b=NB))

            for j in range(4):
                wload(j * 512, (j + 1) * 512, wsl, j * 512, 2048)
            wload(4096, 4224, wz, 0, 128)
            ld5b = mk.sem("p5ldb")
            mk.dma("sync", ubc[:, 0:16].rearrange("p (m o) -> p m o", o=1),
                   T["b_in"][:, 0:2048].rearrange("o (m p) -> p m o", p=128), ld5b,
                   slow_ok=True)
            mk.dma("sync", ubc[:, 16:17],
                   T["b_in"][:, 4096:4224].rearrange("o (m p) -> p m o", p=128)[
                       :, 0, :], ld5b, slow_ok=True)
            biasld = ld5b.n

            xld = [mk.sem("p5x2a"), mk.sem("p5x2b")]
            rst = mk.sem("p5rowst")
            u_st = mk.sem("p5ust")
            xnst = mk.sem("p5xnst")
            guards = {}

            def loadx2(c):
                mk.dma("scalar", xc2[:, :].rearrange("p (b s) -> p b s", b=NB),
                       T["xres"][:, c * CH:(c + 1) * CH].rearrange(
                           "(b p) s -> p b s", p=128), xld[c % 2])

            loadx2(0)
            mk.wait("vector", ld5, gcold)
            for c in range(NCH):
                cols = slice(c * CH, (c + 1) * CH)
                mk.wait("vector", xld[c % 2], 16 * (c // 2 + 1))
                mk.wait("vector", rst, rst.n)
                mm_mu, mm_rs, guards = self.ln_chunk(
                    xc2, xsq, pa, pb, pc, pd, guards,
                    src_sem=[(xld[c % 2], 16 * (c // 2 + 1))],
                    src_vch=guards.get("apply_done", 0))
                mk.wait("gpsimd", mk.ch["tensor"], mm_rs)
                mk.dma("gpsimd", T["rowst"][0:1, cols], st["rows"][0:1, 0:CH], rst)
                mk.dma("gpsimd", T["rowst"][1:2, cols],
                       st["rows"][0:1, 2 * CH:3 * CH], rst)
                mk.wait("vector", mk.ch["tensor"], mm_rs)
                for b in range(NB):
                    mk.vop("vector", "tensor_sub", xnp[:, b * CH:(b + 1) * CH],
                           xc2[:, b * CH:(b + 1) * CH], pc[:, :])
                for b in range(NB):
                    v_xnp = mk.vop("vector", "tensor_mul", xnp[:, b * CH:(b + 1) * CH],
                                   xnp[:, b * CH:(b + 1) * CH], pd[:, :])
                guards["apply_done"] = v_xnp
                if c + 1 < NCH:
                    mk.wait("scalar", mk.ch["vector"], v_xnp)
                    loadx2(c + 1)
                mm_mu2, mm_rs2, g2 = self.ln_chunk(xnp, xsq, pa, pb, pc, pd,
                                                   {"rows_done": guards.get("rows_done", 0),
                                                    "apply_done": guards.get("apply2", 0),
                                                    "xb_rd": guards.get("xb_rd", 0)},
                                                   ro=4, src_vch=v_xnp)
                guards["rows_done"] = g2["rows_done"]
                mk.wait("vector", mk.ch["tensor"], mm_rs2)
                for b in range(NB):
                    mk.vop("vector", "tensor_sub", xnp[:, b * CH:(b + 1) * CH],
                           xnp[:, b * CH:(b + 1) * CH], pc[:, :])
                for b in range(NB):
                    mk.vop("vector", "tensor_mul", xnp[:, b * CH:(b + 1) * CH],
                           xnp[:, b * CH:(b + 1) * CH], pd[:, :])
                mk.wait("vector", xnst, xnst.n)
                for b in range(NB):
                    v_xn2 = mk.vop("vector", "tensor_scalar", xn2[:, b * CH:(b + 1) * CH],
                                   xnp[:, b * CH:(b + 1) * CH],
                                   st["gcols"][:, b:b + 1], st["gcols"][:, 8 + b:9 + b],
                                   ALU.mult, ALU.add)
                guards["apply2"] = v_xn2
                mk.wait("gpsimd", mk.ch["vector"], v_xn2)
                mk.dma("gpsimd", T["xn2t"][:, cols].rearrange(
                    "(b p) s -> p b s", p=128),
                    xn2[:, :].rearrange("p (b s) -> p b s", b=NB), xnst)
                mk.wait("tensor", mk.ch["vector"], v_xn2)
                mk.wait("tensor", ld5b, biasld)
                mk.wait("scalar", u_st, u_st.n)
                for m in range(16):
                    p = pu[m % 2]
                    mk.wait("tensor", mk.ch["scalar"], guards.get(f"u{m % 2}", 0))
                    for b in range(NB):
                        mmu = mk.mm(p[:, :],
                                    wsl[:, b * 2048 + m * 128: b * 2048 + (m + 1) * 128],
                                    xn2[:, b * CH:(b + 1) * CH],
                                    start=(b == 0), stop=(b == NB - 1),
                                    inc=(b == NB - 1))
                    mk.wait("scalar", mk.ch["tensor"], mmu)
                    guards[f"u{m % 2}"] = mk.vop(
                        "scalar", "activation", ubuf[:, m * CH:(m + 1) * CH],
                        p[:, :], AF.Silu, bias=ubc[:, m:m + 1])
                mk.wait("gpsimd", mk.ch["scalar"], guards[f"u{15 % 2}"])
                mk.dma("gpsimd", T["ut2"][:, cols].rearrange("(m p) s -> p m s", p=128),
                       ubuf[:, :].rearrange("p (m s) -> p m s", m=16), u_st)
                mk.wait("tensor", mk.ch["vector"], g2["rows_done"])
                for b in range(NB):
                    mmz = mk.mm(pa[:, :],
                                wz[:, b * 128:(b + 1) * 128],
                                xn2[:, b * CH:(b + 1) * CH],
                                start=(b == 0), stop=(b == NB - 1), inc=(b == NB - 1))
                mk.wait("scalar", mk.ch["tensor"], mmz)
                s_z = mk.vop("scalar", "activation", zt[:, cols], pa[:, :],
                             AF.Silu, bias=ubc[:, 16:17])
                mk.wait("tensor", mk.ch["scalar"], s_z)
                guards["rows_done"] = max(guards.get("rows_done", 0),
                                          g2["rows_done"])
            mk.wait("vector", ld5, gcold)
            mk.wait("vector", mk.ch["scalar"], s_z)
            mk.vop("vector", "tensor_scalar", q2b[:, :], zt[:, :],
                   st["gcols"][:, 16:17], st["gcols"][:, 17:18], ALU.mult, ALU.add)
            v_qk = mk.vop("vector", "tensor_scalar", k2b[:, :], zt[:, :],
                          st["gcols"][:, 18:19], st["gcols"][:, 19:20],
                          ALU.mult, ALU.add)
            mk.fence("p5a", dma_sems=(u_st, xnst, rst))
        if self.upto == "p5a":
            return

        with ExitStack() as s:
            stg = self.sb("p5bstage", (128, 4224), F32, s)
            wsl = self.sb("p5bw", (128, NB * 2048), BF16, s)
            xnb = [self.sb(f"p5bx{i}", (128, NB * CH), BF16, s) for i in range(2)]
            bbc = self.sb("p5bbc", (128, E2), F32, s)
            v2s = self.sb("p5bv2", (128, 4 * E2), BF16, s)
            vtmp = self.sb("p5bt", (128, 512), F32, s)
            pv2 = [self.psum(f"p5bp{i}", (128, 512), F32, s) for i in range(2)]
            pbb = self.psum("p5bpb", (128, 512), F32, s)
            binr = self.sb("p5binr", (1, E2), F32, s)

            def wloadb(c0, c1, dst_off):
                n = c1 - c0
                mk.wait("sync", mk.ch["vector"], mk.ch["vector"].n)
                sem = mk.sem(f"p5bw{c0}")
                t = mk.dma("sync", stg[:, 0:NB * n].rearrange("p (b s) -> p b s", b=NB),
                           T["W_in"][:, c0:c1].rearrange("(b p) s -> p b s", p=128),
                           sem)
                mk.wait("vector", sem, t)
                dv = wsl[:, :].rearrange("p (b s) -> p b s", b=NB)
                return mk.vop("vector", "tensor_copy",
                              dv[:, :, dst_off:dst_off + n],
                              stg[:, 0:NB * n].rearrange("p (b s) -> p b s", b=NB))

            for j in range(4):
                wloadb(2048 + j * 512, 2048 + (j + 1) * 512, j * 512)
            sb5 = mk.sem("p5bbi")
            t = mk.dma("sync", binr[:, :], T["b_in"][:, 2048:4096], sb5)
            mk.wait("tensor", sb5, t)
            for j in range(4):
                mk.wait("tensor", mk.ch["scalar"], mk.ch["scalar"].n)
                mm_b = mk.mm(pbb[:, :], st["ones_row"][:, :],
                             binr[0:1, j * 512:(j + 1) * 512], inc=True)
                mk.wait("scalar", mk.ch["tensor"], mm_b)
                mk.vop("scalar", "copy", bbc[:, j * 512:(j + 1) * 512], pbb[:, :])
            xldb = [mk.sem("p5bxa"), mk.sem("p5bxb")]
            v2st = mk.sem("p5bv2st")
            xnrd = mk.sem("p5xnst")

            def loadxn(c):
                mk.dma("scalar", xnb[c % 2][:, :].rearrange("p (b s) -> p b s", b=NB),
                       T["xn2t"][:, c * CH:(c + 1) * CH].rearrange(
                           "(b p) s -> p b s", p=128), xldb[c % 2])

            mk.wait("scalar", xnrd, xnrd.n)
            loadxn(0)
            guards = {}
            freed = [0, 0]
            for c in range(NCH):
                if c + 1 < NCH:
                    mk.wait("scalar", mk.ch["tensor"], freed[(c + 1) % 2])
                    loadxn(c + 1)
                mk.wait("tensor", xldb[c % 2], 16 * (c // 2 + 1))
                mk.wait("scalar", v2st, v2st.n)
                for sT in range(4):
                    for ec in range(4):
                        p = pv2[ec % 2]
                        mk.wait("tensor", mk.ch["scalar"], guards.get(f"v{ec % 2}", 0))
                        for b in range(NB):
                            mmv = mk.mm(p[:, :],
                                        xnb[c % 2][:, b * CH + sT * 128: b * CH + (sT + 1) * 128],
                                        wsl[:, b * 2048 + ec * 512: b * 2048 + (ec + 1) * 512],
                                        start=(b == 0), stop=(b == NB - 1),
                                        inc=(b == NB - 1))
                        mk.wait("vector", mk.ch["tensor"], mmv)
                        mk.wait("vector", mk.ch["scalar"],
                                guards.get(f"v{(ec + 1) % 2}", 0))
                        v_a = mk.vop("vector", "tensor_add", vtmp[:, :], p[:, :],
                                     bbc[:, ec * 512:(ec + 1) * 512])
                        mk.wait("scalar", mk.ch["vector"], v_a)
                        guards[f"v{ec % 2}"] = mk.vop(
                            "scalar", "activation",
                            v2s[:, sT * E2 + ec * 512: sT * E2 + (ec + 1) * 512],
                            vtmp[:, :], AF.Silu)
                freed[c % 2] = mmv
                mk.wait("gpsimd", mk.ch["scalar"], mk.ch["scalar"].n)
                mk.dma("gpsimd", T["v2t"][c * CH:(c + 1) * CH, :].rearrange(
                    "(t p) e -> p t e", p=128),
                    v2s[:, :].rearrange("p (t e) -> p t e", t=4), v2st)
            mk.fence("p5b", dma_sems=(v2st,))

    def p6(self):
        mk, st, T = self.mk, self.st, self.T
        with ExitStack() as s:
            Wout = self.sb("p6Wout", (128, 16 * D), BF16, s)
            with ExitStack() as s6l:
                self.sb("p6stage", (128, 8448), F32, s6l)
                self.load_cast(Wout[:, 0:8 * D], T["W_out"][0:1024, :], 1024, D,
                               queue="sync", stage="p6stage")
                vwo = self.load_cast(Wout[:, 8 * D:16 * D], T["W_out"][1024:2048, :],
                                     1024, D, queue="sync", stage="p6stage")
                mk.wait("sync", mk.ch["vector"], vwo)
                mk.wait("scalar", mk.ch["vector"], vwo)
            attn2 = self.sb("p6attn", (128, NKT * 512), BF16, s)
            a2f = self.sb("p6a2f", (128, 512), F32, s)
            v2b = [self.sb(f"p6v2{i}", (128, NKT * 128), BF16, s) for i in range(2)]
            ubuf = [self.sb("p6u0", (128, 16 * CH), BF16, s)] * 2
            uo = self.sb("p6uo", (128, 16 * CH), BF16, s)
            xc2 = [self.sb(f"p6x{i}", (128, NB * CH), F32, s) for i in range(2)]
            xnp = self.sb("p6xnp", (128, NB * CH), F32, s)
            rws = self.sb("p6rws", (1, 2 * CH), F32, s)
            sc = [self.psum("p6s0", (128, GK * 512), F32, s),
                  self.psum("p6s1", (128, GK * 512), F32, s)]
            ppv = self.psum("p6pv", (128, 512), F32, s)
            pwo = self.psum("p6wo", (128, 512), F32, s)

            ld6 = mk.sem("p6ld")
            boc8 = self.sb("p6bo8", (128, 8), F32, s)
            mk.dma("sync", boc8[:, :].rearrange("p (m o) -> p m o", o=1),
                   T["b_out"][:, :].rearrange("(m p) o -> p m o", p=128), ld6,
                   slow_ok=True)
            bold = ld6.n

            q2b, k2b = st["q2b"], st["k2b"]
            uld = [mk.sem("p6ua"), mk.sem("p6ub")]
            vld = [mk.sem("p6va"), mk.sem("p6vb")]
            xld = [mk.sem("p6xa"), mk.sem("p6xb")]
            rld = mk.sem("p6r")
            ost = [mk.sem("p6sta"), mk.sem("p6stb")]
            ust5 = mk.sem("p5ust")
            v2st5 = mk.sem("p5bv2st")
            rst5 = mk.sem("p5rowst")
            x2st5 = mk.sem("p5st")

            def loadu(c):
                mk.dma("sync", ubuf[c % 2][:, :].rearrange("p (m s) -> p m s", m=16),
                       T["ut2"][:, c * CH:(c + 1) * CH].rearrange(
                           "(m p) s -> p m s", p=128), uld[0])

            def loadx(c):
                mk.dma("scalar", xc2[c % 2][:, :].rearrange("p (b s) -> p b s", b=NB),
                       T["xres"][:, c * CH:(c + 1) * CH].rearrange(
                           "(b p) s -> p b s", p=128), xld[0])

            def loadv(eb, par):
                mk.dma("sync", v2b[par][:, :].rearrange("p (t e) -> p t e", t=NKT),
                       T["v2t"][:, eb * 128:(eb + 1) * 128].rearrange(
                           "(t p) e -> p t e", p=128), vld[par])

            mk.wait("sync", ust5, ust5.n)
            mk.wait("sync", v2st5, v2st5.n)
            mk.wait("scalar", x2st5, x2st5.n)
            mk.wait("scalar", rst5, rst5.n)
            loadu(0)
            loadx(0)
            loadv(0, 0)
            vtot = [16, 0]
            guards = {}
            gidx = 0
            exp_cnt = {}
            rel_cnt = {}
            ufree = [0, 0]
            xfree = [0, 0]
            ost_par = [0, 0]
            def emit_sgroup(cc, g):
                nonlocal gidx
                ccols = slice(cc * CH, (cc + 1) * CH)
                buf = sc[gidx % 2]
                n = min(GK, NKT - g * GK)
                if g == 0:
                    mk.wait("tensor", mk.ch["vector"], guards.get("bcre", 0))
                if gidx >= 2:
                    mk.wait("tensor", mk.ch["scalar"], rel_cnt[gidx - 2])
                for j in range(n):
                    t = g * GK + j
                    mm = mk.mm(buf[:, j * 512:(j + 1) * 512],
                               k2b[:, t * 128:(t + 1) * 128],
                               q2b[:, ccols], start=True, stop=True,
                               inc=(j == n - 1))
                if g == 0:
                    mk.wait("scalar", mk.ch["tensor"], guards.get("pv_done", 0))
                mk.wait("scalar", mk.ch["tensor"], mm)
                aslab = attn2[:, g * GK * 512: g * GK * 512 + n * 512]
                s_r = mk.vop("scalar", "activation", aslab, buf[:, 0:n * 512],
                             AF.Relu, scale=1.0 / S)
                rel_cnt[gidx] = s_r
                mk.wait("vector", mk.ch["scalar"], s_r)
                exp_cnt[gidx] = mk.vop("vector", "tensor_mul", aslab, aslab,
                                       aslab)
                gidx += 1

            NCH6 = getattr(self, "nch6", NCH)
            for c in range(NCH6):
                cols = slice(c * CH, (c + 1) * CH)
                if c > 0:
                    mk.wait("sync", mk.ch["vector"], max(ufree))
                    mk.wait("sync", mk.ch["tensor"], guards.get("pv_done", 0))
                    loadu(c)
                    mk.wait("scalar", mk.ch["vector"], xfree[c % 2])
                    mk.wait("scalar", ost[c % 2], ost_par[c % 2])
                    loadx(c)
                if c == 0:
                    for g in range(NG):
                        emit_sgroup(0, g)
                for eb in range(16):
                    par = eb % 2
                    mk.wait("tensor", vld[par], vtot[par])
                    nxt = (eb + 1 if eb + 1 < 16
                           else (0 if c + 1 < getattr(self, "nch6", NCH) else None))
                    if nxt is not None:
                        opar = (par + 1) % 2
                        mk.wait("sync", mk.ch["tensor"],
                                guards.get(f"pv_mm_{opar}", 0))
                        loadv(nxt, opar)
                        vtot[opar] = vld[opar].n
                    p_pv = (ppv, pwo)[par]
                    if eb < 2:
                        mk.wait("tensor", mk.ch["scalar"],
                                max(guards.get("woev0", 0),
                                    guards.get("woev1", 0)))
                    mk.wait("tensor", mk.ch["vector"], guards.get(f"uo_ev{par}", 0))
                    for t in range(NKT):
                        g = t // GK
                        mk.wait("tensor", mk.ch["vector"], exp_cnt[c * NG + g])
                        mmpv = mk.mm(p_pv[:, :],
                                     v2b[par][:, t * 128:(t + 1) * 128],
                                     attn2[:, t * 512:(t + 1) * 512],
                                     start=(t == 0), stop=(t == NKT - 1),
                                     inc=(t == NKT - 1))
                    guards[f"pv_mm_{par}"] = mmpv
                    guards["pv_done"] = mmpv
                    mk.wait("vector", mk.ch["tensor"], mmpv)
                    mk.wait("vector", uld[0], 16 * (c + 1))
                    guards[f"uo_ev{par}"] = mk.vop(
                        "vector", "tensor_mul", uo[:, eb * CH:(eb + 1) * CH],
                        p_pv[:, :], ubuf[c % 2][:, eb * CH:(eb + 1) * CH])
                guards["uo_ev"] = max(guards["uo_ev0"], guards["uo_ev1"])
                ufree[c % 2] = guards["uo_ev"]
                mk.wait("scalar", mk.ch["tensor"], guards.get("rows_mm", 0))
                mk.dma("scalar", rws[0:1, 0:CH], T["rowst"][0:1, cols], rld)
                mk.dma("scalar", rws[0:1, CH:2 * CH], T["rowst"][1:2, cols], rld)
                r_tot = rld.n
                mk.wait("tensor", rld, r_tot)
                mk.wait("tensor", mk.ch["vector"], guards.get("bcre", 0))
                mm_mu = mk.mm(sc[0][:, 0:512], st["ones_row"][:, :], rws[0:1, 0:CH],
                              inc=True)
                mm_rs = mk.mm(sc[0][:, 512:1024], st["ones_row"][:, :],
                              rws[0:1, CH:2 * CH], inc=True)
                guards["rows_mm"] = mm_rs
                mk.wait("vector", xld[0], 16 * (c + 1))
                mk.wait("vector", mk.ch["tensor"], mm_rs)
                for b in range(NB):
                    mk.vop("vector", "tensor_sub", xnp[:, b * CH:(b + 1) * CH],
                           xc2[c % 2][:, b * CH:(b + 1) * CH], sc[0][:, 0:512])
                for b in range(NB):
                    v_xnp = mk.vop("vector", "tensor_mul",
                                   xnp[:, b * CH:(b + 1) * CH],
                                   xnp[:, b * CH:(b + 1) * CH], sc[0][:, 512:1024])
                guards["bcre"] = v_xnp
                mk.wait("tensor", mk.ch["vector"], guards["uo_ev"])
                mk.wait("tensor", ld6, bold)
                gpos = 0
                nxt = c + 1 if c + 1 < NCH6 else None
                for db in range(NB):
                    p_wo = (pwo, ppv)[db % 2]
                    mk.wait("tensor", mk.ch["scalar"], guards.get(f"woev{db % 2}", 0))
                    for eb in range(16):
                        mmw = mk.mm(p_wo[:, :],
                                    Wout[:, eb * D + db * 128: eb * D + (db + 1) * 128],
                                    uo[:, eb * CH:(eb + 1) * CH],
                                    start=(eb == 0), stop=(eb == 15), inc=(eb == 15))
                    if nxt is not None:
                        upto = ((db + 1) * NG + NB - 1) // NB
                        while gpos < upto:
                            emit_sgroup(nxt, gpos)
                            gpos += 1
                    mk.wait("scalar", mk.ch["tensor"], mmw)
                    mk.wait("scalar", mk.ch["vector"], guards.get("a2f_rd", 0))
                    s_bo = mk.vop("scalar", "activation", a2f[:, 0:512], p_wo[:, :],
                                  AF.Identity, bias=boc8[:, db:db + 1])
                    guards[f"woev{db % 2}"] = s_bo
                    mk.wait("vector", mk.ch["scalar"], s_bo)
                    guards["a2f_rd"] = mk.vop("vector", "tensor_add",
                                              xnp[:, db * CH:(db + 1) * CH],
                                              xnp[:, db * CH:(db + 1) * CH],
                                              a2f[:, 0:512])
                    v_x3 = mk.vop("vector", "tensor_add",
                                  xc2[c % 2][:, db * CH:(db + 1) * CH],
                                  xc2[c % 2][:, db * CH:(db + 1) * CH],
                                  xnp[:, db * CH:(db + 1) * CH])
                xfree[c % 2] = v_x3
                mk.wait("gpsimd", mk.ch["vector"], v_x3)
                mk.dma("gpsimd", T["xres"][:, cols].rearrange("(b p) s -> p b s", p=128),
                       xc2[c % 2][:, :].rearrange("p (b s) -> p b s", b=NB),
                       ost[c % 2])
                ost_par[c % 2] = ost[c % 2].n
            mk.fence("p6end", dma_sems=(ost[0], ost[1]))
        self.sE.close()

    def p7pre(self):
        mk, st, T = self.mk, self.st, self.T
        self.sG = ExitStack()
        ores = self.sb("ores2", (128, NB * S2), F32, self.sG)
        self.sF2 = ExitStack()
        self.load_ffn_weights(self.sF2)
        mk.vop("vector", "memset", st["gcs"][:, :], 0.0)
        with ExitStack() as s:
            xc = [self.sb(f"p7x{i}", (128, NB * CH), F32, s) for i in range(2)]
            xsq = self.sb("p7xsq", (128, NB * CH // 2), BF16, s)
            pa = self.psum("p7pa", (128, 512), F32, s)
            pb = self.psum("p7pb", (128, 512), F32, s)
            pc = self.psum("p7pc", (128, 512), F32, s)
            pd = self.psum("p7pd", (128, 512), F32, s)
            xld = [mk.sem("p7xa"), mk.sem("p7xb")]
            lst = [mk.sem("p7lnst"), mk.sem("p7lnst2")]
            lst_tot = [0, 0]

            def loadx(c):
                mk.dma("scalar", xc[c % 2][:, :].rearrange("p (b s) -> p b s", b=NB),
                       T["xres"][:, c * CH:(c + 1) * CH].rearrange(
                           "(b p) s -> p b s", p=128), xld[c % 2])

            mk.wait("scalar", mk.ch["vector"], mk.ch["vector"].n)
            loadx(0)
            guards = {}
            freed = [0, 0]
            g_ap = 0
            for c in range(NCH):
                if c + 1 < NCH:
                    mk.wait("scalar", mk.ch["vector"], freed[(c + 1) % 2])
                    mk.wait("scalar", mk.ch["gpsimd"], guards.get("g_ap", 0))
                    mk.wait("scalar", lst[(c + 1) % 2], lst_tot[(c + 1) % 2])
                    loadx(c + 1)
                mk.wait("vector", xld[c % 2], 16 * (c // 2 + 1))
                mk.wait("scalar", mk.ch["gpsimd"], guards.get("g_ap", 0))
                mm_mu, mm_rs, guards = self.ln_chunk(
                    xc[c % 2], xsq, pa, pb, pc, pd, guards,
                    src_sem=[(xld[c % 2], 16 * (c // 2 + 1))], src_vch=0)
                dst_is_sbuf = c < NCH // 2
                xqf = xsq[:, :].bitcast(F32)
                mk.wait("vector", mk.ch["tensor"], mm_rs)
                v_pc = mk.vop("vector", "tensor_copy", xqf[:, 0:512], pc[:, :])
                v_pd = mk.vop("vector", "tensor_copy", xqf[:, 512:1024], pd[:, :])

                def dstb(b):
                    return (ores[:, b * S2 + c * CH: b * S2 + (c + 1) * CH]
                            if dst_is_sbuf else xc[c % 2][:, b * CH:(b + 1) * CH])

                mk.wait("gpsimd", mk.ch["vector"], v_pd)
                for b in range(4, NB):
                    mk.vop("gpsimd", "tensor_sub", dstb(b),
                           xc[c % 2][:, b * CH:(b + 1) * CH], xqf[:, 0:512])
                for b in range(4, NB):
                    g_ap = mk.vop("gpsimd", "tensor_mul", dstb(b), dstb(b),
                                  xqf[:, 512:1024])
                guards["g_ap"] = g_ap
                for b in range(4):
                    mk.vop("vector", "tensor_sub", dstb(b),
                           xc[c % 2][:, b * CH:(b + 1) * CH], pc[:, :])
                for b in range(4):
                    v_ap = mk.vop("vector", "tensor_mul", dstb(b), dstb(b),
                                  pd[:, :])
                guards["apply_done"] = v_ap
                for b in range(4):
                    mk.vop("vector", "reduce_sum", st["colsc"][:, 3:4], dstb(b), AX)
                    v_gc = mk.vop("vector", "tensor_add", st["gcs"][:, b:b + 1],
                                  st["gcs"][:, b:b + 1], st["colsc"][:, 3:4])
                mk.wait("vector", mk.ch["gpsimd"], g_ap)
                for b in range(4, NB):
                    mk.vop("vector", "reduce_sum", st["colsc"][:, 3:4], dstb(b), AX)
                    v_gc = mk.vop("vector", "tensor_add", st["gcs"][:, b:b + 1],
                                  st["gcs"][:, b:b + 1], st["colsc"][:, 3:4])
                freed[c % 2] = max(v_ap, v_gc)
                if not dst_is_sbuf:
                    mk.wait("gpsimd", mk.ch["vector"], v_gc)
                    mk.dma("gpsimd", T["lnt"][:, (c - NCH // 2) * CH:
                                              (c - NCH // 2 + 1) * CH].rearrange(
                        "(b p) s -> p b s", p=128),
                        xc[c % 2][:, :].rearrange("p (b s) -> p b s", b=NB),
                        lst[c % 2])
                    lst_tot[c % 2] = lst[c % 2].n
                    freed[c % 2] = max(v_ap, v_gc)
            mk.fence("p7pre", dma_sems=(lst[0], lst[1]))

    def p8(self):
        pass


def _host_map(inputs, core):
    np32 = np.float32
    x = np.ascontiguousarray(np.asarray(inputs["L0_attn_a_x"][core]).T)
    m = {"x_t": x.astype(np32, copy=False)}
    def a(k):
        return np.ascontiguousarray(np.asarray(inputs[k], dtype=np32))
    m["W_q"] = a("L0_attn_a_W_q")
    m["W_k"] = a("L0_attn_a_W_k")
    m["W_v"] = a("L0_attn_a_W_v")
    m["W_o"] = a("L0_attn_a_W_o")
    m["W_down"] = a("L0_attn_a_W_down_kv")
    m["W_up_k"] = a("L0_attn_a_W_up_k")
    m["W_up_v"] = a("L0_attn_a_W_up_v")
    m["b_ln_g"] = a("L0_attn_b_ln_g").reshape(-1, 1)
    m["b_ln_b"] = a("L0_attn_b_ln_b").reshape(-1, 1)
    m["W_in"] = a("L0_attn_b_W_in")
    m["b_in"] = a("L0_attn_b_b_in").reshape(1, -1)
    m["W_out"] = a("L0_attn_b_W_out")
    m["b_out"] = a("L0_attn_b_b_out").reshape(-1, 1)
    m["gamma_q"] = a("L0_attn_b_gamma_q").reshape(-1, 1)
    m["beta_q"] = a("L0_attn_b_beta_q").reshape(-1, 1)
    m["gamma_k"] = a("L0_attn_b_gamma_k").reshape(-1, 1)
    m["beta_k"] = a("L0_attn_b_beta_k").reshape(-1, 1)
    for i in range(1, 5):
        m[f"cn_w{i}"] = a(f"L0_ffn_cn_w{i}")
        m[f"cn_b{i}"] = a(f"L0_ffn_cn_b{i}").reshape(-1, 1)
    for i in range(1, 4):
        m[f"gc_w{i}"] = a(f"L0_ffn_gc_w{i}")
        m[f"gc_b{i}"] = a(f"L0_ffn_gc_b{i}").reshape(-1, 1)
    m["a_range"] = a("L0_ffn_adaptive_range").reshape(1, 1)
    return m


def kernel(**inputs):
    n_cores = 8
    in_maps = [_host_map(inputs, c) for c in range(n_cores)]
    b = Builder()
    nc = b.build()
    res = run_bass_kernel_spmd(nc, in_maps, core_ids=list(range(n_cores)))
    out = np.stack([np.ascontiguousarray(r["y_t"].T) for r in res.results], axis=0)
    return out.astype(np.float32, copy=False)
